# version 18
# baseline (speedup 1.0000x reference)
import contextlib
import math
import numpy as np
import ml_dtypes
import concourse.bass as bass
import concourse.mybir as mybir
from concourse.bass_utils import run_bass_kernel_spmd

F32 = mybir.dt.float32
BF16 = mybir.dt.bfloat16
I32 = mybir.dt.int32
AF = mybir.ActivationFunctionType
AX = mybir.AxisListType
OP = mybir.AluOpType

T = 4096
D = 1024
NH = 16
NT = 8
HID = 3072
VST = 65
INV127 = 1.0 / 127.0


class Ctr:
    def __init__(self, ctx, nc, name):
        self.sem = ctx.enter_context(nc.semaphore(name))
        self.n = 0

    def inc(self, instr, k=1):
        self.n += k
        instr.then_inc(self.sem, k)
        return instr


def build_program(dbg=False):
    nc = bass.Bass("TRN2", target_bir_lowering=False, debug=False)

    x_h = nc.dram_tensor("x", [T, D], F32, kind="ExternalInput").ap()
    wat_h = nc.dram_tensor("w_attn", [D, 3 * D], F32, kind="ExternalInput").ap()
    ba_h = nc.dram_tensor("b_attn", [1, 3 * D], F32, kind="ExternalInput").ap()
    wp_h = nc.dram_tensor("w_proj", [D, D], F32, kind="ExternalInput").ap()
    bp_h = nc.dram_tensor("b_proj", [1, D], F32, kind="ExternalInput").ap()
    wgT_h = nc.dram_tensor("wg_t", [D, HID], I32, kind="ExternalInput").ap()
    wuT_h = nc.dram_tensor("wu_t", [D, HID], I32, kind="ExternalInput").ap()
    wdT_h = nc.dram_tensor("wd_t", [HID, D], I32, kind="ExternalInput").ap()
    cosr_h = nc.dram_tensor("cosrow", [1, D], BF16, kind="ExternalInput").ap()
    sinr_h = nc.dram_tensor("sinrow", [1, D], BF16, kind="ExternalInput").ap()
    ident_h = nc.dram_tensor("ident", [40, 8], F32, kind="ExternalInput").ap()
    identb_h = nc.dram_tensor("identb", [128, 128], BF16, kind="ExternalInput").ap()
    onesb_h = nc.dram_tensor("onesb", [1, 128], BF16, kind="ExternalInput").ap()
    onescol_h = nc.dram_tensor("onescol", [128, 1], BF16, kind="ExternalInput").ap()
    mask_h = nc.dram_tensor("masktri", [128, 128], BF16, kind="ExternalInput").ap()
    sel16_h = nc.dram_tensor("sel16", [16, NH * 64], BF16, kind="ExternalInput").ap()

    out_h = nc.dram_tensor("out", [T, D], F32, kind="ExternalOutput").ap()

    KI = "ExternalOutput" if dbg else "Internal"
    wrs_h = nc.dram_tensor("w_rope_s", [16, 8, 128, 128], BF16, kind=KI).ap()
    wp_s = nc.dram_tensor("wp_s", [8, 128, D], BF16, kind=KI).ap()
    wbs_h = nc.dram_tensor("wbs", [72, 128, 1024], BF16, kind=KI).ap()
    xnT_s = nc.dram_tensor("xnT_s", [NT, 128, T], BF16, kind=KI).ap()
    s_s = nc.dram_tensor("s_s", [4 * NT, 128, D], BF16, kind=KI).ap()

    ctx = contextlib.ExitStack()
    with ctx:
        def sb(name, shape, dt):
            return ctx.enter_context(nc.sbuf_tensor(name, shape, dt))

        ident = sb("s_ident", [40, 8], F32)
        onesb = sb("s_onesb", [1, 128], BF16)
        onescol = sb("s_onescol", [128, 1], BF16)
        mask = sb("s_mask", [128, 128], BF16)
        bT = sb("bT", [128, 24], F32)
        bTb = sb("bTb", [128, 24], BF16)
        bpp = sb("bpp", [1, D], BF16)
        qmaxT = sb("qmaxT", [128, 16], BF16)
        gtmp = sb("gtmp", [128, 1], BF16)
        cbt = sb("cbt", [128, 8], F32)
        lnsc = sb("lnsc", [128, 65], F32)
        identb = sb("s_identb", [128, 128], BF16)
        c8 = sb("c8", [8, 1], F32)
        crowb = sb("crowb", [1, 8], BF16)
        prodall = sb("prodall", [128, 8], BF16)
        sel16 = sb("s_sel16", [16, NH * 64], BF16)

        d_ms = Ctr(ctx, nc, "d_ms")
        d_w4 = Ctr(ctx, nc, "d_w4")
        d_x = [Ctr(ctx, nc, "d_x0"), Ctr(ctx, nc, "d_x1")]
        d_we = [Ctr(ctx, nc, "d_we0"), Ctr(ctx, nc, "d_we1")]
        d_wp = [Ctr(ctx, nc, f"d_wp{i}") for i in range(4)]
        d_wcl = Ctr(ctx, nc, "d_wcl")
        d_wcs = Ctr(ctx, nc, "d_wcs")
        d_swp = [Ctr(ctx, nc, "d_swp0"), Ctr(ctx, nc, "d_swp1")]
        d_wrs = Ctr(ctx, nc, "d_wrs")
        d_sx = Ctr(ctx, nc, "d_sx")
        d_ss = Ctr(ctx, nc, "d_ss")
        d_sl = Ctr(ctx, nc, "d_sl")
        d_so = [Ctr(ctx, nc, "d_so0"), Ctr(ctx, nc, "d_so1")]
        d_cw = Ctr(ctx, nc, "d_cw")
        d_cx = Ctr(ctx, nc, "d_cx")
        d_cs = [Ctr(ctx, nc, "d_cs0"), Ctr(ctx, nc, "d_cs1")]
        t_p = Ctr(ctx, nc, "t_p")
        v_p = Ctr(ctx, nc, "v_p")
        c_p = Ctr(ctx, nc, "c_p")
        g_p = Ctr(ctx, nc, "g_p")

        blk = ctx.enter_context(nc.Block())
        W = {}

        ab = contextlib.ExitStack()
        with ab:
            def sba(name, shape, dt):
                return ab.enter_context(nc.sbuf_tensor(name, shape, dt))

            kT = sba("kT", [128, 8 * T], BF16)
            vP = sba("vP", [128, 32 * NH * VST], BF16)
            wv = sba("wv", [128, 8 * 1024], BF16)
            we = [sba(f"we{i}", [128, 1024], BF16) for i in range(2)]
            wpb = [sba(f"wpb{i}", [128, 1024], BF16) for i in range(2)]
            pP = [sba(f"pP{i}", [128, 512], BF16) for i in range(4)]
            xb = [sba(f"xb{i}", [128, D], F32) for i in range(2)]
            Lr = sba("Lr", [16, 512], F32)

            pA = ab.enter_context(nc.psum_tensor("pA", [128, 1024], F32))
            pB = ab.enter_context(nc.psum_tensor("pB", [128, 1024], F32))
            pC = ab.enter_context(nc.psum_tensor("pC", [128, 1024], F32))
            pD = ab.enter_context(nc.psum_tensor("pD", [128, 1024], F32))

            pro = contextlib.ExitStack()
            with pro:
                csb = pro.enter_context(nc.sbuf_tensor("csb", [128, 1024], BF16))

                @blk.sync
                def _(sy):
                    for dst, src in ((ident, ident_h), (identb, identb_h),
                                     (onesb, onesb_h),
                                     (onescol, onescol_h), (mask, mask_h),
                                     (sel16, sel16_h)):
                        d_ms.inc(sy.dma_start(out=dst[:], in_=src), 16)
                    d_ms.inc(sy.dma_start(out=csb[0:1, 0:512], in_=cosr_h[0:1, 0:512]), 16)
                    d_ms.inc(sy.dma_start(out=csb[0:1, 512:1024], in_=sinr_h[0:1, 0:512]), 16)
                    d_ms.inc(sy.dma_start(out=xb[0][0:1, :], in_=bp_h), 16)
                    d_ms.inc(sy.dma_start(out=kT[0:1, 0:2048].bitcast(F32), in_=ba_h[0:1, 0:1024]), 16)
                    d_ms.inc(sy.dma_start(out=kT[0:1, 2048:4096].bitcast(F32), in_=ba_h[0:1, 1024:2048]), 16)
                    d_ms.inc(sy.dma_start(out=xb[1][0:1, :], in_=ba_h[0:1, 2048:3072]), 16)
                    W["const_all"] = d_ms.n

                v0_cs = v_p.n

                @blk.tensor
                def _(te):
                    te.wait_ge(d_ms.sem, W["const_all"])
                    for j in range(2):
                        t_p.inc(te.matmul(pA[:, j * 512:(j + 1) * 512],
                                          onesb[0:1, :], csb[0:1, j * 512:(j + 1) * 512],
                                          start=True, stop=True))
                        W[f"csmm{j}"] = t_p.n

                @blk.vector
                def _(ve):
                    for j in range(2):
                        ve.wait_ge(t_p.sem, W[f"csmm{j}"])
                        v_p.inc(ve.tensor_copy(csb[:, j * 512:(j + 1) * 512],
                                               pA[:, j * 512:(j + 1) * 512]))
                    W["csb"] = v_p.n
                    ve.wait_ge(d_ms.sem, W["const_all"])
                    for third in (0, 1):
                        row = kT[0:1, third * 2048:(third + 1) * 2048].bitcast(F32)
                        bev = row[0:1, 0:1024:2]
                        bod = row[0:1, 1:1024:2]
                        cev = csb[0:1, 0:512]
                        snv = csb[0:1, 512:1024]
                        pa_ = pD[0:1, 0:512]
                        pb_ = pD[0:1, 512:1024]
                        sb_ = Lr[0:1, :]
                        ve.tensor_mul(pb_, bev, snv)
                        ve.tensor_mul(sb_, bod, snv)
                        v_p.inc(ve.tensor_mul(pa_, bev, cev))
                        ve.wait_ge(v_p.sem, v_p.n)
                        v_p.inc(ve.tensor_tensor(bev, pa_, sb_, OP.subtract))
                        ve.wait_ge(v_p.sem, v_p.n)
                        v_p.inc(ve.tensor_mul(sb_, bod, cev))
                        ve.wait_ge(v_p.sem, v_p.n)
                        ve.tensor_tensor(bod, pb_, sb_, OP.add)
                    v_p.inc(ve.engine_nop())
                    W["bfold"] = v_p.n

                @blk.tensor
                def _(te):
                    te.wait_ge(v_p.sem, W["bfold"])
                    for c in range(24):
                        src = (kT[0:1, 0:2048].bitcast(F32),
                               kT[0:1, 2048:4096].bitcast(F32), xb[1][0:1, :])[c // 8]
                        ins = te.transpose(pC[:, c:c + 1],
                                           src[0:1, (c % 8) * 128:(c % 8) * 128 + 128],
                                           ident[0:1, 0:1])
                    t_p.inc(ins)
                    W["btr"] = t_p.n

                @blk.vector
                def _(ve):
                    ve.wait_ge(t_p.sem, W["btr"])
                    v_p.inc(ve.tensor_copy(bT[:], pC[:, 0:24]))
                    ve.wait_ge(v_p.sem, v_p.n)
                    ve.tensor_copy(bTb[:], bT[:])
                    ve.memset(lnsc[:, 64:65], 1e-5)
                    ve.tensor_copy(pP[0][0:1, :], xb[0][0:1, 0:512])
                    v_p.inc(ve.tensor_copy(pP[1][0:1, :], xb[0][0:1, 512:1024]))
                    W["bT_bp"] = v_p.n

                wst4 = xb[1]
                if True:
                    v_wp0 = v_p.n

                    @blk.sync
                    def _(sy):
                        sy.wait_ge(t_p.sem, W["btr"])
                        for c in range(8):
                            if c >= 1:
                                sy.wait_ge(v_p.sem, v_wp0 + c)
                            d_w4.inc(sy.dma_start(out=wst4[:],
                                                  in_=wp_h[c * 128:(c + 1) * 128, :]), 16)
                            W[f"wpld{c}"] = d_w4.n

                    st_wp0 = [d_swp[0].n, d_swp[1].n]

                    @blk.vector
                    def _(ve):
                        for c in range(8):
                            ve.wait_ge(d_w4.sem, W[f"wpld{c}"])
                            if c >= 2:
                                ve.wait_ge(d_swp[c % 2].sem, st_wp0[c % 2] + 16 * (c // 2))
                            v_p.inc(ve.tensor_copy(we[c % 2][:], wst4[:]))
                            W[f"wpcv{c}"] = v_p.n

                    @blk.gpsimd
                    def _(gp):
                        for c in range(8):
                            gp.wait_ge(v_p.sem, W[f"wpcv{c}"])
                            d_swp[c % 2].inc(gp.dma_start(out=wp_s[c], in_=we[c % 2][:]), 16)
                            W[f"wpst{c}"] = d_swp[c % 2].n
                        W["wp_done0"] = d_swp[0].n
                        W["wp_done1"] = d_swp[1].n

                    t_bpp0 = t_p.n

                    @blk.sync
                    def _(sy):
                        sy.wait_ge(d_swp[0].sem, W["wp_done0"])
                        sy.wait_ge(d_swp[1].sem, W["wp_done1"])
                        for c in range(8):
                            if c >= 2:
                                sy.wait_ge(t_p.sem, t_bpp0 + c - 1)
                            d_wp[c % 2].inc(sy.dma_start(out=wpb[c % 2][:], in_=wp_s[c]), 16)
                            W[f"bppld{c}"] = d_wp[c % 2].n

                    @blk.tensor
                    def _(te):
                        te.wait_ge(v_p.sem, W["bT_bp"])
                        for c in range(8):
                            te.wait_ge(d_wp[c % 2].sem, W[f"bppld{c}"])
                            te.matmul(pD[0:1, 0:512], bTb[:, 16 + c:17 + c], wpb[c % 2][:, 0:512],
                                      start=(c == 0), stop=False)
                            t_p.inc(te.matmul(pD[0:1, 512:1024], bTb[:, 16 + c:17 + c],
                                              wpb[c % 2][:, 512:1024], start=(c == 0), stop=False))
                            W[f"bppmm{c}"] = t_p.n
                        te.matmul(pD[0:1, 0:512], onesb[0:1, 0:1], pP[0][0:1, :], start=False, stop=True)
                        t_p.inc(te.matmul(pD[0:1, 512:1024], onesb[0:1, 0:1], pP[1][0:1, :],
                                          start=False, stop=True))
                        W["bppdone"] = t_p.n

                    @blk.vector
                    def _(ve):
                        ve.wait_ge(t_p.sem, W["bppdone"])
                        ve.tensor_copy(bpp[:], pD[0:1, :])
                        ve.memset(qmaxT[:], 0.0)
                        v_p.inc(ve.memset(vP[:, 64::VST], 1.0))
                        W["initdone"] = v_p.n

                    v_wf0 = v_p.n

                    @blk.sync
                    def _(sy):
                        for dk in range(8):
                            for th in range(3):
                                if dk == 0 and th == 0:
                                    sy.wait_ge(t_p.sem, W["bppdone"])
                                else:
                                    sy.wait_ge(v_p.sem, v_wf0 + 3 * dk + th)
                                d_w4.inc(sy.dma_start(
                                    out=wst4[:],
                                    in_=wat_h[dk * 128:(dk + 1) * 128, th * 1024:(th + 1) * 1024]), 16)
                                W[f"watld_{dk}_{th}"] = d_w4.n

                    st_wf0 = d_wrs.n

                    @blk.vector
                    def _(ve):
                        for dk in range(8):
                            for th in range(3):
                                ve.wait_ge(d_w4.sem, W[f"watld_{dk}_{th}"])
                                if th < 2:
                                    if dk >= 1:
                                        ve.wait_ge(d_wrs.sem, st_wf0 + 32 * dk)
                                    dst = wpb[th]
                                    cev = csb[:, 0:512]
                                    snv = csb[:, 512:1024]
                                    ve.tensor_mul(xb[0][:, 0:512], wst4[:, 0::2], cev)
                                    ve.tensor_mul(pD[:, 0:512], wst4[:, 1::2], snv)
                                    ve.tensor_tensor(dst[:, 0::2], xb[0][:, 0:512],
                                                     pD[:, 0:512], OP.subtract)
                                    ve.tensor_mul(xb[0][:, 0:512], wst4[:, 1::2], cev)
                                    ve.tensor_mul(pD[:, 0:512], wst4[:, 0::2], snv)
                                    v_p.inc(ve.tensor_tensor(dst[:, 1::2], xb[0][:, 0:512],
                                                             pD[:, 0:512], OP.add))
                                else:
                                    v_p.inc(ve.tensor_copy(wv[:, dk * 1024:(dk + 1) * 1024], wst4[:]))
                                W[f"wf_{dk}_{th}"] = v_p.n

                    @blk.gpsimd
                    def _(gp):
                        for dk in range(8):
                            gp.wait_ge(v_p.sem, W[f"wf_{dk}_1"])
                            d_wrs.inc(gp.dma_start(
                                out=wrs_h[0:8, dk].rearrange("e p c -> p e c"),
                                in_=wpb[0][:].rearrange("p (e c) -> p e c", e=8)), 16)
                            d_wrs.inc(gp.dma_start(
                                out=wrs_h[8:16, dk].rearrange("e p c -> p e c"),
                                in_=wpb[1][:].rearrange("p (e c) -> p e c", e=8)), 16)
                            W[f"wqkst{dk}"] = d_wrs.n
                        W["wrs_done"] = d_wrs.n
            qy = sba("qy", [128, 8 * 512], BF16)
            yT = sba("yT", [128, 8 * 512], BF16)
            sev0 = sba("sev0", [128, D], BF16)
            xnb = [sba("xnb0", [128, D], BF16)] * 2
            xnT = sba("xnT", [128, 8 * 512], BF16)
            Rb = sba("Rb", [16, 512], BF16)
            wcb = [sba(f"wcb{i}", [128, 512], I32) for i in range(2)]
            wcvb = sba("wcvb", [128, 512], BF16)

            lastS = {("A", 0): 0, ("A", 1): 0, ("B", 0): 0, ("B", 1): 0}
            W["pj_done_-1"] = 0
            W["pjev_done_-1"] = W["initdone"]
            W["yev_done_-1"] = W["bT_bp"]
            last_wpb = [W["bppmm6"], W["bppmm7"]]

            def emit_ln_load(t, i):
                ti = 4 * t + i

                @blk.sync
                def _(sy, t=t, i=i, ti=ti):
                    if t == 0 and i <= 1:
                        sy.wait_ge(d_wrs.sem, W["wrs_done"])
                        sy.wait_ge(v_p.sem, W["bT_bp"])
                    if ti >= 2:
                        sy.wait_ge(v_p.sem, W[f"xnapply_{ti - 2}"])
                    d_x[i % 2].inc(sy.dma_start(out=xb[i % 2][:], in_=x_h[ti * 128:(ti + 1) * 128, :]), 16)
                    W[f"xld_{ti}"] = d_x[i % 2].n

            def emit_ln_calc(t, i, c_rstd_done):
                ti = 4 * t + i
                L = 16 * i

                @blk.vector
                def _(ve, t=t, i=i, ti=ti, L=L, c_rstd_done=c_rstd_done):
                    ve.wait_ge(d_x[i % 2].sem, W[f"xld_{ti}"])
                    if ti >= 1:
                        ve.wait_ge(t_p.sem, W[f"trmm_{ti - 1}"])
                    ve.bn_stats(lnsc[:, L + 0:L + 6], xb[i % 2][:, 0:512])
                    v_p.inc(ve.bn_stats(lnsc[:, L + 6:L + 12], xb[i % 2][:, 512:1024]))
                    ve.wait_ge(v_p.sem, v_p.n)
                    v_p.inc(ve.bn_aggr(lnsc[:, L + 12:L + 14], lnsc[:, L + 0:L + 12]))
                    W[f"var_{ti}"] = v_p.n
                    ve.wait_ge(c_p.sem, c_rstd_done(i))
                    v_p.inc(ve.scalar_tensor_tensor(lnsc[:, L + 0:L + 1], lnsc[:, L + 12:L + 13], -1.0,
                                                    lnsc[:, L + 15:L + 16], OP.mult, OP.mult))
                    ve.wait_ge(v_p.sem, v_p.n)
                    v_p.inc(ve.tensor_scalar(xnb[i % 2][:], xb[i % 2][:],
                                             lnsc[:, L + 15:L + 16], lnsc[:, L + 0:L + 1],
                                             OP.mult, OP.add))
                    W[f"xnapply_{ti}"] = v_p.n

            def emit_ln_tile(t, i, c_rstd_done):
                emit_ln_load(t, i)
                emit_ln_calc(t, i, c_rstd_done)

            def emit_trans_tile(t, i):
                ti = 4 * t + i

                @blk.tensor
                def _(te, t=t, i=i, ti=ti):
                    te.wait_ge(v_p.sem, W[f"xnapply_{ti}"])
                    if i == 0:
                        te.wait_ge(v_p.sem, W[f"cbready_{t - 1}"])
                        te.wait_ge(v_p.sem, W.get(f"pjev_{t - 2}_2", 0))
                    else:
                        te.wait_ge(v_p.sem, W[f"trev_{ti - 1}"])
                    for dk in range(8):
                        ins = te.transpose(pD[:, dk * 64:dk * 64 + 64].bitcast(BF16),
                                           xnb[i % 2][:, dk * 128:(dk + 1) * 128], identb[:])
                    t_p.inc(ins)
                    W[f"trmm_{ti}"] = t_p.n

                @blk.vector
                def _(ve, t=t, i=i, ti=ti):
                    ve.wait_ge(t_p.sem, W[f"trmm_{ti}"])
                    if i == 0:
                        ve.wait_ge(d_sx.sem, W.get(f"xnst_{t - 1}", 0))
                        ve.wait_ge(t_p.sem, W.get(f"vmm_{4 * (t - 1) + 3}", 0))
                    for dk in range(8):
                        ins = ve.tensor_copy(xnT[:, dk * 512 + i * 128:dk * 512 + i * 128 + 128],
                                             pD[:, dk * 64:dk * 64 + 64].bitcast(BF16))
                    v_p.inc(ins)
                    W[f"trev_{ti}"] = v_p.n

            def emit_qk_chunk(t, e):
                if e >= 10:
                    prev_mm = f"qkmm_{t}_{e - 2}"
                    prev_ev = f"qkev_{t}_{e - 2}"
                elif e >= 8:
                    prev_mm = f"qkmm_{t - 1}_{6 + (e % 2)}"
                    prev_ev = None
                elif e >= 2:
                    prev_mm = f"qkmm_{t}_{e - 2}"
                    prev_ev = f"qkev_{t}_{e - 2}"
                else:
                    prev_mm = f"qkmm_{t}_{14 + (e % 2)}"
                    prev_ev = None

                @blk.sync
                def _(sy, t=t, e=e, prev_mm=prev_mm):
                    if t == 0 and e in (8, 9):
                        sy.wait_ge(d_swp[0].sem, W["wp_done0"])
                        sy.wait_ge(d_swp[1].sem, W["wp_done1"])
                    prev = W.get(prev_mm, 0)
                    if prev:
                        sy.wait_ge(t_p.sem, prev)
                    d_we[e % 2].inc(sy.dma_start(
                        out=we[e % 2][:].rearrange("p (d c) -> p d c", d=8),
                        in_=wrs_h[e].rearrange("d p c -> p d c")), 16)
                    W[f"weld_{t}_{e}"] = d_we[e % 2].n

                @blk.tensor
                def _(te, t=t, e=e, prev_ev=prev_ev):
                    ps = pA if e < 8 else pD
                    te.wait_ge(d_we[e % 2].sem, W[f"weld_{t}_{e}"])
                    te.wait_ge(v_p.sem, W[f"trev_{4 * t + 3}"])
                    if e < 2:
                        te.wait_ge(c_p.sem, W.get(f"exps_done_{t - 1}", 0))
                    if prev_ev is not None and prev_ev in W:
                        te.wait_ge(v_p.sem, W[prev_ev])
                    half = (e % 2) * 512
                    for dk in range(8):
                        ins = te.matmul(ps[:, half:half + 512],
                                        we[e % 2][:, dk * 128:(dk + 1) * 128],
                                        xnT[:, dk * 512:(dk + 1) * 512],
                                        start=(dk == 0), stop=(dk == 7))
                    t_p.inc(ins)
                    W[f"qkmm_{t}_{e}"] = t_p.n

                @blk.vector
                def _(ve, t=t, e=e):
                    ps = pA if e < 8 else pD
                    ve.wait_ge(t_p.sem, W[f"qkmm_{t}_{e}"])
                    if e < 8:
                        dst = qy[:, e * 512:(e + 1) * 512]
                    else:
                        dst = kT[:, (e - 8) * T + t * 512:(e - 8) * T + t * 512 + 512]
                    half = (e % 2) * 512
                    v_p.inc(ve.tensor_scalar(dst, ps[:, half:half + 512],
                                             bT[:, e:e + 1], None, OP.add))
                    W[f"qkev_{t}_{e}"] = v_p.n
                    ve.wait_ge(v_p.sem, v_p.n)
                    v_p.inc(ve.reduce_max(gtmp[:], dst, axis=AX.X, apply_absolute_value=True))
                    ve.wait_ge(v_p.sem, v_p.n)
                    g_p.inc(ve.tensor_max(qmaxT[:, e:e + 1], qmaxT[:, e:e + 1], gtmp[:]))
                    W[f"gmax_{t}_{e}"] = g_p.n

            def emit_qk_sync(t, e):
                prev_mm = f"qkmm_{t}_{e - 2}" if e >= 10 else f"qkmm_{t - 1}_{6 + (e % 2)}"

                @blk.sync
                def _(sy, t=t, e=e, prev_mm=prev_mm):
                    if t == 0 and e in (8, 9):
                        sy.wait_ge(d_swp[0].sem, W["wp_done0"])
                        sy.wait_ge(d_swp[1].sem, W["wp_done1"])
                    prev = W.get(prev_mm, 0)
                    if prev:
                        sy.wait_ge(t_p.sem, prev)
                    d_we[e % 2].inc(sy.dma_start(
                        out=we[e % 2][:].rearrange("p (d c) -> p d c", d=8),
                        in_=wrs_h[e].rearrange("d p c -> p d c")), 16)
                    W[f"weld_{t}_{e}"] = d_we[e % 2].n

            def qk_fillers(t, e):
                prev_ev = f"qkev_{t}_{e - 2}" if e >= 10 else None

                def mk(dk, t=t, e=e, prev_ev=prev_ev):
                    def emit(te):
                        if dk == 0:
                            te.wait_ge(d_we[e % 2].sem, W[f"weld_{t}_{e}"])
                            te.wait_ge(v_p.sem, W[f"trev_{4 * t + 3}"])
                            if prev_ev is not None and prev_ev in W:
                                te.wait_ge(v_p.sem, W[prev_ev])
                        half = (e % 2) * 512
                        ins = te.matmul(pD[:, half:half + 512],
                                        we[e % 2][:, dk * 128:(dk + 1) * 128],
                                        xnT[:, dk * 512:(dk + 1) * 512],
                                        start=(dk == 0), stop=(dk == 7))
                        if dk == 7:
                            t_p.inc(ins)
                            W[f"qkmm_{t}_{e}"] = t_p.n
                    return emit
                return [mk(dk) for dk in range(8)]

            def trans_fillers(t, i):
                ti = 4 * t + i

                def mk(dk, t=t, i=i, ti=ti):
                    def emit(te):
                        if dk == 0:
                            te.wait_ge(v_p.sem, W[f"xnapply_{ti}"])
                            if i == 0:
                                te.wait_ge(v_p.sem, W[f"cbready_{t - 1}"])
                                te.wait_ge(v_p.sem, W.get(f"pjev_{t - 2}_2", 0))
                            else:
                                te.wait_ge(v_p.sem, W[f"trev_{ti - 1}"])
                        ins = te.transpose(pD[:, dk * 64:dk * 64 + 64].bitcast(BF16),
                                           xnb[i % 2][:, dk * 128:(dk + 1) * 128], identb[:])
                        if dk == 7:
                            t_p.inc(ins)
                            W[f"trmm_{ti}"] = t_p.n
                    return emit
                return [mk(dk) for dk in range(8)]

            def emit_trans_ev(t, i):
                ti = 4 * t + i

                @blk.vector
                def _(ve, t=t, i=i, ti=ti):
                    ve.wait_ge(t_p.sem, W[f"trmm_{ti}"])
                    if i == 0:
                        ve.wait_ge(d_sx.sem, W.get(f"xnst_{t - 1}", 0))
                        ve.wait_ge(t_p.sem, W.get(f"vmm_{4 * (t - 1) + 3}", 0))
                    for dk in range(8):
                        ins = ve.tensor_copy(xnT[:, dk * 512 + i * 128:dk * 512 + i * 128 + 128],
                                             pD[:, dk * 64:dk * 64 + 64].bitcast(BF16))
                    v_p.inc(ins)
                    W[f"trev_{ti}"] = v_p.n

            def emit_qk_ev(t, e):
                @blk.vector
                def _(ve, t=t, e=e):
                    ve.wait_ge(t_p.sem, W[f"qkmm_{t}_{e}"])
                    dst = kT[:, (e - 8) * T + t * 512:(e - 8) * T + t * 512 + 512]
                    half = (e % 2) * 512
                    v_p.inc(ve.tensor_scalar(dst, pD[:, half:half + 512],
                                             bT[:, e:e + 1], None, OP.add))
                    W[f"qkev_{t}_{e}"] = v_p.n
                    ve.wait_ge(v_p.sem, v_p.n)
                    v_p.inc(ve.reduce_max(gtmp[:], dst, axis=AX.X, apply_absolute_value=True))
                    ve.wait_ge(v_p.sem, v_p.n)
                    g_p.inc(ve.tensor_max(qmaxT[:, e:e + 1], qmaxT[:, e:e + 1], gtmp[:]))
                    W[f"gmax_{t}_{e}"] = g_p.n

            def emit_v_part_i(t, i):
                ti = 4 * t + i

                @blk.tensor
                def _(te, t=t, i=i, ti=ti):
                    if i == 0:
                        te.wait_ge(c_p.sem, W.get(f"exps_done_{t - 1}", 0))
                    else:
                        te.wait_ge(v_p.sem, W.get(f"vev_{ti - 1}", 0))
                    for dk in range(8):
                        lhs = xnT[:, dk * 512 + i * 128:dk * 512 + i * 128 + 128]
                        te.matmul(pB[:, 0:512], lhs, wv[:, dk * 1024:dk * 1024 + 512],
                                  start=(dk == 0), stop=(dk == 7))
                        ins = te.matmul(pB[:, 512:1024], lhs, wv[:, dk * 1024 + 512:dk * 1024 + 1024],
                                        start=(dk == 0), stop=(dk == 7))
                    t_p.inc(ins)
                    W[f"vmm_{ti}"] = t_p.n

                @blk.vector
                def _(ve, ti=ti):
                    ve.wait_ge(t_p.sem, W[f"vmm_{ti}"])
                    dst = vP[:, ti * NH * VST:(ti + 1) * NH * VST]
                    dst3 = dst.rearrange("p (h s) -> p h s", s=VST)[:, :, 0:64]
                    src3 = pB[:].rearrange("p (h s) -> p h s", s=64)
                    v_p.inc(ve.tensor_copy(dst3, src3))
                    W[f"vev_{ti}"] = v_p.n

            def emit_cb(t):
                @blk.vector
                def _(ve, t=t):
                    ve.wait_ge(g_p.sem, W[f"gmax_{t}_7"])
                    ve.wait_ge(t_p.sem, W.get(f"cbm3_{t - 1}", 0))
                    v_p.inc(ve.tensor_mul(prodall[:], qmaxT[:, 0:8], qmaxT[:, 8:16]))
                    W[f"cbp_{t}"] = v_p.n

                @blk.tensor
                def _(te, t=t):
                    te.wait_ge(v_p.sem, W[f"cbp_{t}"])
                    t_p.inc(te.matmul(pD[0:8, 0:1], prodall[:], onescol[:], start=True, stop=True))
                    W[f"cbm1_{t}"] = t_p.n

                @blk.vector
                def _(ve, t=t):
                    ve.wait_ge(t_p.sem, W[f"cbm1_{t}"])
                    v_p.inc(ve.tensor_copy(c8[:], pD[0:8, 0:1]))
                    W[f"cbc8_{t}"] = v_p.n

                @blk.tensor
                def _(te, t=t):
                    te.wait_ge(v_p.sem, W[f"cbc8_{t}"])
                    t_p.inc(te.transpose(pD[0:1, 8:16], c8[:], ident[0:8, 0:8]))
                    W[f"cbm2_{t}"] = t_p.n

                @blk.vector
                def _(ve, t=t):
                    ve.wait_ge(t_p.sem, W[f"cbm2_{t}"])
                    v_p.inc(ve.tensor_scalar_mul(crowb[:], pD[0:1, 8:16], -0.0625))
                    W[f"cbrow_{t}"] = v_p.n

                @blk.tensor
                def _(te, t=t):
                    te.wait_ge(v_p.sem, W[f"cbrow_{t}"])
                    t_p.inc(te.matmul(pD[:, 16:24], onesb[0:1, :], crowb[:], start=True, stop=True))
                    W[f"cbm3_{t}"] = t_p.n

                @blk.vector
                def _(ve, t=t):
                    ve.wait_ge(t_p.sem, W[f"cbm3_{t}"])
                    v_p.inc(ve.tensor_copy(cbt[:], pD[:, 16:24]))
                    W[f"cbready_{t}"] = v_p.n

            def emit_recip(tb):
                @blk.vector
                def _(ve, tb=tb):
                    ve.wait_ge(t_p.sem, W.get(f"rbmm_{tb - 1}_7", 0))
                    ve.wait_ge(d_sl.sem, W[f"lst_{tb}_7"])
                    ve.reciprocal(Lr[:], Lr[:])
                    v_p.inc(ve.tensor_copy(Rb[:], Lr[:]))
                    W[f"rb_{tb}"] = v_p.n

            def emit_ynorm(tb, yc):
                @blk.tensor
                def _(te, tb=tb, yc=yc):
                    reg = (pD if yc % 4 < 2 else pC)
                    off = (yc % 2) * 512
                    te.wait_ge(v_p.sem, W[f"rb_{tb}"])
                    if yc >= 4:
                        te.wait_ge(v_p.sem, W[f"ynorm_{tb}_{yc - 4}"])
                    t_p.inc(te.matmul(reg[:, off:off + 512],
                                      sel16[:, yc * 128:(yc + 1) * 128], Rb[:],
                                      start=True, stop=True))
                    W[f"rbmm_{tb}_{yc}"] = t_p.n

                @blk.vector
                def _(ve, tb=tb, yc=yc):
                    reg = (pD if yc % 4 < 2 else pC)
                    off = (yc % 2) * 512
                    ve.wait_ge(t_p.sem, W[f"rbmm_{tb}_{yc}"])
                    v_p.inc(ve.tensor_mul(yT[:, yc * 512:(yc + 1) * 512],
                                          yT[:, yc * 512:(yc + 1) * 512],
                                          reg[:, off:off + 512]))
                    W[f"ynorm_{tb}_{yc}"] = v_p.n

            def emit_proj(tb):
                @blk.sync
                def _(sy, tb=tb):
                    t_base = t_p.n
                    if tb == 0:
                        sy.wait_ge(d_wrs.sem, W["wrs_done"])
                    for cc in range(8):
                        if cc < 2:
                            sy.wait_ge(t_p.sem, last_wpb[cc])
                        else:
                            sy.wait_ge(t_p.sem, t_base + 4 * (cc - 1))
                        d_wp[cc % 2].inc(sy.dma_start(out=wpb[cc % 2][:], in_=wp_s[cc]), 16)
                        W[f"wpld_{tb}_{cc}"] = d_wp[cc % 2].n

                @blk.tensor
                def _(te, tb=tb):
                    if f"cbready_{tb + 1}" in W:
                        te.wait_ge(v_p.sem, W[f"cbready_{tb + 1}"])
                    else:
                        te.wait_ge(c_p.sem, W[f"exps_done_{tb}"])
                        te.wait_ge(v_p.sem, W[f"ynorm_{tb}_7"])
                    pjs = (pA, pB, pD, pC)
                    for cc in range(8):
                        te.wait_ge(d_wp[cc % 2].sem, W[f"wpld_{tb}_{cc}"])
                        for i in range(4):
                            pj = pjs[i]
                            te.matmul(pj[:, 0:512],
                                      yT[:, cc * 512 + i * 128:cc * 512 + i * 128 + 128],
                                      wpb[cc % 2][:, 0:512], start=(cc == 0), stop=False)
                            t_p.inc(te.matmul(pj[:, 512:1024],
                                              yT[:, cc * 512 + i * 128:cc * 512 + i * 128 + 128],
                                              wpb[cc % 2][:, 512:1024], start=(cc == 0), stop=False))
                        last_wpb[cc % 2] = t_p.n
                    for i in range(4):
                        pj = pjs[i]
                        te.matmul(pj[:, 0:512], onesb[:], bpp[0:1, 0:512], start=False, stop=True)
                        t_p.inc(te.matmul(pj[:, 512:1024], onesb[:], bpp[0:1, 512:1024],
                                          start=False, stop=True))
                    W[f"pjmm_{tb}"] = t_p.n

                for i, pj in ((0, pA), (1, pB), (2, pD), (3, pC)):
                    @blk.vector
                    def _(ve, tb=tb, i=i, pj=pj):
                        ve.wait_ge(t_p.sem, W[f"pjmm_{tb}"])
                        ve.wait_ge(d_ss.sem, W.get(f"sst_{tb}_{i - 1}", 0))
                        v_p.inc(ve.tensor_copy(sev0[:], pj[:]))
                        W[f"pjev_{tb}_{i}"] = v_p.n

                    @blk.gpsimd
                    def _(gp, tb=tb, i=i):
                        gp.wait_ge(v_p.sem, W[f"pjev_{tb}_{i}"])
                        d_ss.inc(gp.dma_start(out=s_s[4 * tb + i], in_=sev0[:]), 16)
                        W[f"sst_{tb}_{i}"] = d_ss.n

                W[f"pjev_done_{tb}"] = v_p.n

            wconv_chunks = []
            for kind, n_tiles, nh in (("g", 8, 3), ("u", 8, 3), ("d", 24, 1)):
                for j in range(n_tiles):
                    for th in range(nh):
                        wconv_chunks.append((kind, j, th))
            NWH = 2 * len(wconv_chunks)

            def emit_wconv(sl):
                if sl < NWH:
                    kind, j, th = wconv_chunks[sl // 2]
                    src = {"g": wgT_h, "u": wuT_h, "d": wdT_h}[kind]
                    c0 = th * 1024 + (sl % 2) * 512

                    @blk.sync
                    def _(sy, sl=sl, src=src, j=j, c0=c0):
                        if sl >= 2:
                            sy.wait_ge(v_p.sem, W[f"wcv_{sl - 2}"])
                        d_wcl.inc(sy.dma_start(
                            out=wcb[sl % 2][:],
                            in_=src[j * 128:(j + 1) * 128, c0:c0 + 512]), 16)
                        W[f"wcl_{sl}"] = d_wcl.n

                if 0 <= sl - 1 < NWH:
                    @blk.vector
                    def _(ve, sl=sl - 1):
                        ve.wait_ge(d_wcl.sem, W[f"wcl_{sl}"])
                        if sl >= 1:
                            ve.wait_ge(d_wcs.sem, W[f"wcs_{sl - 1}"])
                        v_p.inc(ve.tensor_copy(wcvb[:], wcb[sl % 2][:]))
                        W[f"wcv_{sl}"] = v_p.n

                    @blk.gpsimd
                    def _(gp, sl=sl - 1):
                        gp.wait_ge(v_p.sem, W[f"wcv_{sl}"])
                        d_wcs.inc(gp.dma_start(
                            out=wbs_h[sl // 2, :, (sl % 2) * 512:(sl % 2) * 512 + 512],
                            in_=wcvb[:]), 16)
                        W[f"wcs_{sl}"] = d_wcs.n

            pre_c0 = c_p.n

            def c_rstd0(i, pre_c0=pre_c0):
                return pre_c0 + 2 * (i + 1)

            W["cbready_-1"] = W["initdone"]
            emit_ln_tile(0, 0, c_rstd0)
            emit_trans_tile(0, 0)
            emit_ln_tile(0, 1, c_rstd0)
            emit_trans_tile(0, 1)
            emit_ln_tile(0, 2, c_rstd0)
            emit_trans_tile(0, 2)
            emit_ln_tile(0, 3, c_rstd0)
            emit_trans_tile(0, 3)

            @blk.scalar
            def _(sc, pre_c0=pre_c0):
                for i in range(4):
                    L = 16 * i
                    sc.wait_ge(v_p.sem, W[f"var_{i}"])
                    c_p.inc(sc.activation(lnsc[:, L + 14:L + 15], lnsc[:, L + 13:L + 14],
                                          AF.Ln, bias=lnsc[:, 64:65]))
                    sc.wait_ge(c_p.sem, c_p.n)
                    c_p.inc(sc.activation(lnsc[:, L + 15:L + 16], lnsc[:, L + 14:L + 15],
                                          AF.Exp, scale=-0.5))
                    assert c_p.n == pre_c0 + 2 * (i + 1)

            @blk.gpsimd
            def _(gp):
                gp.wait_ge(v_p.sem, W["trev_3"])
                d_sx.inc(gp.dma_start(out=xnT_s[0], in_=xnT[:]), 16)
                W["xnst_0"] = d_sx.n

            emit_ln_load(1, 0)
            emit_ln_load(1, 1)
            for e in range(8, 16):
                emit_qk_chunk(0, e)

            for t in range(NT):
                nkc = 4 * t + 4
                has_next = t + 1 < NT
                R = 4 if has_next else 0
                c_base_t = c_p.n

                def c_exp_done(c, kc, c_base_t=c_base_t, nkc=nkc, R=R):
                    return c_base_t + 2 * min(c + 1, R) + c * 2 * nkc + 2 * (kc + 1)

                def c_rstd_next(i, c_base_t=c_base_t, nkc=nkc):
                    return c_base_t + 2 * (i + 1) + i * 2 * nkc

                emit_wconv(18 * t)
                tb = t - 1
                if t >= 1:
                    emit_recip(tb)
                emit_qk_chunk(t, 0)
                emit_qk_chunk(t, 1)
                if t >= 1:
                    emit_ynorm(tb, 0)
                    emit_ynorm(tb, 1)
                emit_qk_chunk(t, 2)
                emit_qk_chunk(t, 3)
                if t >= 1:
                    emit_ynorm(tb, 2)
                    emit_ynorm(tb, 3)
                emit_qk_chunk(t, 4)
                emit_qk_chunk(t, 5)
                if t >= 1:
                    emit_ynorm(tb, 4)
                    emit_ynorm(tb, 5)
                emit_qk_chunk(t, 6)
                emit_qk_chunk(t, 7)
                if t >= 1:
                    emit_ynorm(tb, 6)
                    emit_ynorm(tb, 7)
                emit_v_part_i(t, 0)
                emit_v_part_i(t, 1)
                emit_v_part_i(t, 2)
                emit_v_part_i(t, 3)
                emit_cb(t)
                if has_next:
                    emit_ln_calc(t + 1, 0, c_rstd_next)
                if t >= 1:
                    emit_proj(tb)
                emit_wconv(18 * t + 1)

                for c in range(8):
                    emit_wconv(18 * t + 2 + 2 * c)
                    if has_next and 1 <= c <= 2:
                        emit_ln_load(t + 1, c + 1)
                    if t + 2 < NT and 6 <= c:
                        emit_ln_load(t + 2, c - 6)
                    if has_next and 1 <= c < 4:
                        emit_ln_calc(t + 1, c, c_rstd_next)
                    fillers = []
                    if has_next:
                        if c < 4:
                            fillers = trans_fillers(t + 1, c)
                        else:
                            for e in (8 + 2 * (c - 4), 9 + 2 * (c - 4)):
                                emit_qk_sync(t + 1, e)
                                fillers += qk_fillers(t + 1, e)
                    v_base = v_p.n

                    def v_mask_done(kc, v_base=v_base, t=t):
                        return v_base + (kc - 4 * t) + 1

                    def mk_mms(te, t=t, c=c, nkc=nkc, c_exp_done=c_exp_done,
                               v_mask_done=v_mask_done):
                        def mm1(kc):
                            b = kc % 2
                            q0 = 128 * (kc - 4 * t) if kc >= 4 * t else 0
                            if kc >= 2:
                                te.wait_ge(c_p.sem, c_exp_done(c, kc - 2))
                            else:
                                te.wait_ge(c_p.sem, lastS[("A", b)])
                                te.wait_ge(c_p.sem, lastS[("B", b)])
                                if c == 0:
                                    te.wait_ge(v_p.sem, W[f"cbready_{t}"])
                                    te.wait_ge(v_p.sem, W.get(f"pjev_{t - 1}_1", 0))
                            te.matmul(pA[:, b * 512 + q0:b * 512 + 512],
                                      kT[0:64, c * T + kc * 128:c * T + kc * 128 + 128],
                                      qy[0:64, c * 512 + q0:(c + 1) * 512],
                                      start=True, stop=True)
                            t_p.inc(te.matmul(pB[:, b * 512 + q0:b * 512 + 512],
                                              kT[64:128, c * T + kc * 128:c * T + kc * 128 + 128],
                                              qy[64:128, c * 512 + q0:(c + 1) * 512],
                                              start=True, stop=True))
                            W[f"mm1_{t}_{c}_{kc}"] = t_p.n

                        def mm2(kc):
                            b = kc % 2
                            diag = kc >= 4 * t
                            q0 = 128 * (kc - 4 * t) if diag else 0
                            te.wait_ge(c_p.sem, c_exp_done(c, kc))
                            if diag:
                                te.wait_ge(v_p.sem, v_mask_done(kc))
                            if kc == 0 and c == 0:
                                te.wait_ge(v_p.sem, W[f"vev_{4 * t + 3}"])
                                te.wait_ge(v_p.sem, W[f"pjev_done_{t - 1}"])
                            if kc == 0:
                                te.wait_ge(v_p.sem, W.get(f"yev_{t}_{c - 1}", W[f"yev_done_{t - 1}"]))
                            vbase = kc * NH * VST
                            last = (kc == nkc - 1)
                            te.matmul(pC[0:65, q0:512],
                                      vP[:, vbase + (2 * c) * VST:vbase + (2 * c) * VST + 65],
                                      pP[0 + b][:, q0:512], start=(kc == 0), stop=last)
                            t_p.inc(te.matmul(pC[0:65, 512 + q0:1024],
                                              vP[:, vbase + (2 * c + 1) * VST:vbase + (2 * c + 1) * VST + 65],
                                              pP[2 + b][:, q0:512], start=(kc == 0), stop=last))
                            W[f"mm2_{t}_{c}_{kc}"] = t_p.n
                        return mm1, mm2

                    @blk.tensor
                    def _(te, mk_mms=mk_mms, nkc=nkc, fillers=fillers):
                        mm1, mm2 = mk_mms(te)
                        fi = 0
                        for kc in range(nkc):
                            mm1(kc)
                            if 1 <= kc <= nkc - 2:
                                mm2(kc - 1)
                            while fi < len(fillers) and fi <= 2 * (kc - 3):
                                fillers[fi](te)
                                fi += 1
                        while fi < len(fillers):
                            fillers[fi](te)
                            fi += 1

                    assert v_p.n == v_base, (t, c, v_p.n, v_base)

                    @blk.tensor
                    def _(te, mk_mms=mk_mms, t=t, c=c, nkc=nkc):
                        mm1, mm2 = mk_mms(te)
                        mm2(nkc - 2)
                        mm2(nkc - 1)
                        W[f"mm2last_{t}_{c}"] = t_p.n

                    @blk.vector
                    def _(ve, t=t, c=c, nkc=nkc, c_exp_done=c_exp_done):
                        for kc in range(4 * t, nkc):
                            b = kc % 2
                            q0 = 128 * (kc - 4 * t)
                            ve.wait_ge(c_p.sem, c_exp_done(c, kc))
                            ve.tensor_mul(pP[0 + b][:, q0:q0 + 128], pP[0 + b][:, q0:q0 + 128], mask[:])
                            v_p.inc(ve.tensor_mul(pP[2 + b][:, q0:q0 + 128],
                                                  pP[2 + b][:, q0:q0 + 128], mask[:]))
                        lrowA = sev0[0:1, :].bitcast(F32)
                        lrowB = sev0[64:65, :].bitcast(F32)
                        ve.wait_ge(t_p.sem, W[f"mm2last_{t}_{c}"])
                        if c == 0:
                            ve.wait_ge(t_p.sem, W.get(f"pjmm_{t - 1}", 0))
                        ve.wait_ge(d_sl.sem, W.get(f"lst_{t}_{c - 1}", W.get(f"lst_{t - 1}_7", 0)))
                        if c == 0:
                            ve.wait_ge(d_ss.sem, W.get(f"sst_{t - 1}_3", 0))
                        ve.tensor_copy(yT[0:64, c * 512:(c + 1) * 512], pC[0:64, 0:512])
                        ve.tensor_copy(yT[64:128, c * 512:(c + 1) * 512], pC[0:64, 512:1024])
                        ve.tensor_copy(lrowA[0:1, 0:512], pC[64:65, 0:512])
                        v_p.inc(ve.tensor_copy(lrowB[0:1, 0:512], pC[64:65, 512:1024]))
                        W[f"yev_{t}_{c}"] = v_p.n

                    @blk.gpsimd
                    def _(gp, t=t, c=c):
                        lrowA = sev0[0:1, :].bitcast(F32)
                        lrowB = sev0[64:65, :].bitcast(F32)
                        gp.wait_ge(v_p.sem, W[f"yev_{t}_{c}"])
                        d_sl.inc(gp.dma_start(out=Lr[2 * c:2 * c + 1, :], in_=lrowA[0:1, 0:512]), 16)
                        d_sl.inc(gp.dma_start(out=Lr[2 * c + 1:2 * c + 2, :], in_=lrowB[0:1, 0:512]), 16)
                        W[f"lst_{t}_{c}"] = d_sl.n

                    if has_next:
                        if c < 4:
                            emit_trans_ev(t + 1, c)
                            if c == 3:
                                @blk.gpsimd
                                def _(gp, t=t):
                                    gp.wait_ge(v_p.sem, W[f"trev_{4 * (t + 1) + 3}"])
                                    d_sx.inc(gp.dma_start(out=xnT_s[t + 1], in_=xnT[:]), 16)
                                    W[f"xnst_{t + 1}"] = d_sx.n
                        else:
                            emit_qk_ev(t + 1, 8 + 2 * (c - 4))
                            emit_qk_ev(t + 1, 9 + 2 * (c - 4))

                    emit_wconv(18 * t + 3 + 2 * c)
                    lastS[("A", 0)] = lastS[("B", 0)] = c_exp_done(c, nkc - 2)
                    lastS[("A", 1)] = lastS[("B", 1)] = c_exp_done(c, nkc - 1)
                W[f"yev_done_{t}"] = v_p.n

                @blk.scalar
                def _(sc, t=t, nkc=nkc, R=R, c_base_t=c_base_t):
                    for c in range(8):
                        if c < R:
                            i = c
                            L = 16 * i
                            sc.wait_ge(v_p.sem, W[f"var_{4 * (t + 1) + i}"])
                            c_p.inc(sc.activation(lnsc[:, L + 14:L + 15], lnsc[:, L + 13:L + 14],
                                                  AF.Ln, bias=lnsc[:, 64:65]))
                            sc.wait_ge(c_p.sem, c_p.n)
                            c_p.inc(sc.activation(lnsc[:, L + 15:L + 16], lnsc[:, L + 14:L + 15],
                                                  AF.Exp, scale=-0.5))
                        for kc in range(nkc):
                            b = kc % 2
                            q0 = 128 * (kc - 4 * t) if kc >= 4 * t else 0
                            sc.wait_ge(t_p.sem, W[f"mm1_{t}_{c}_{kc}"])
                            if kc >= 2:
                                sc.wait_ge(t_p.sem, W[f"mm2_{t}_{c}_{kc - 2}"])
                            c_p.inc(sc.activation(pP[0 + b][:, q0:512], pA[:, b * 512 + q0:b * 512 + 512],
                                                  AF.Exp, bias=cbt[:, c:c + 1], scale=0.125))
                            c_p.inc(sc.activation(pP[2 + b][:, q0:512], pB[:, b * 512 + q0:b * 512 + 512],
                                                  AF.Exp, bias=cbt[:, c:c + 1], scale=0.125))
                            assert c_p.n == c_base_t + 2 * min(c + 1, R) + c * 2 * nkc + 2 * (kc + 1)

                W[f"exps_done_{t}"] = c_p.n

            emit_wconv(NWH)
            emit_recip(NT - 1)
            for yc in range(8):
                emit_ynorm(NT - 1, yc)
            emit_proj(NT - 1)

            AB_T, AB_V, AB_C = t_p.n, v_p.n, c_p.n
            AB_SX, AB_SS, AB_SL = d_sx.n, d_ss.n, d_sl.n
            AB_WCS = d_wcs.n

        cs = contextlib.ExitStack()
        with cs:
            def sbc(name, shape, dt):
                return cs.enter_context(nc.sbuf_tensor(name, shape, dt))

            wg = [sbc(f"wg{d}", [128, HID], BF16) for d in range(8)]
            wu = [sbc(f"wu{d}", [128, HID], BF16) for d in range(8)]
            wd = [sbc(f"wd{h}", [128, D], BF16) for h in range(24)]
            xnC = sbc("xnC", [128, T], BF16)
            hT = [sbc(f"hT{h}", [128, 512], BF16) for h in range(24)]
            xc = [sbc(f"xc{i}", [128, D], F32) for i in range(2)]
            scb_ = [sbc(f"scb{i}", [128, D], BF16) for i in range(2)]
            gs = [sbc("gs0", [128, D], F32)] * 2
            sgb = [sbc("sgb0", [128, 512], F32)] * 2

            pG = [cs.enter_context(nc.psum_tensor(f"pG{i}", [128, 512], F32)) for i in range(2)]
            pU = [cs.enter_context(nc.psum_tensor(f"pU{i}", [128, 512], F32)) for i in range(2)]
            pDn = [cs.enter_context(nc.psum_tensor(f"pDn{i}", [128, 1024], F32)) for i in range(2)]

            def cbar(eng):
                eng.wait_ge(t_p.sem, AB_T)
                eng.wait_ge(v_p.sem, AB_V)
                eng.wait_ge(c_p.sem, AB_C)
                eng.wait_ge(d_sx.sem, AB_SX)
                eng.wait_ge(d_ss.sem, AB_SS)
                eng.wait_ge(d_sl.sem, AB_SL)

            @blk.sync
            def _(sy):
                cbar(sy)
                sy.wait_ge(d_wcs.sem, AB_WCS)
                for dk in range(8):
                    d_cw.inc(sy.dma_start(
                        out=wg[dk][:].rearrange("p (c x) -> p c x", c=3),
                        in_=wbs_h[3 * dk:3 * dk + 3].rearrange("c p x -> p c x")), 16)
                W["wg_ld"] = d_cw.n
                for dk in range(8):
                    d_cw.inc(sy.dma_start(
                        out=wu[dk][:].rearrange("p (c x) -> p c x", c=3),
                        in_=wbs_h[24 + 3 * dk:24 + 3 * dk + 3].rearrange("c p x -> p c x")), 16)
                W["wu_ld"] = d_cw.n
                for h in range(24):
                    d_cw.inc(sy.dma_start(out=wd[h][:], in_=wbs_h[48 + h]), 16)
                W["wd_ld"] = d_cw.n

            dn_last = {}
            for tc in range(NT):
                c_tc0 = c_p.n
                v_tc0 = v_p.n

                def c_sil(hc, c_tc0=c_tc0):
                    return c_tc0 + hc + 1

                def c_gsil(i, c_tc0=c_tc0):
                    return c_tc0 + 24 + i + 1

                def v_hm(hc, v_tc0=v_tc0):
                    return v_tc0 + hc + 1

                def v_fin(i, v_tc0=v_tc0):
                    return v_tc0 + 24 + i + 1

                dn_prev = dict(dn_last)
                so_tc0 = [d_so[0].n, d_so[1].n]

                @blk.sync
                def _(sy, tc=tc):
                    if tc == 0:
                        cbar(sy)
                    else:
                        sy.wait_ge(t_p.sem, W[f"gumm_{tc - 1}_23"])
                    d_cx.inc(sy.dma_start(out=xnC[:], in_=xnT_s[tc]), 16)
                    W[f"xnld_{tc}"] = d_cx.n

                for i in range(4):
                    ti = 4 * tc + i

                    @blk.sync
                    def _(sy, tc=tc, i=i, ti=ti, v_fin=v_fin):
                        if ti >= 2:
                            if i < 2:
                                sy.wait_ge(v_p.sem, W[f"fin_{ti - 2}"])
                            else:
                                sy.wait_ge(v_p.sem, v_fin(i - 2))
                        d_cs[i % 2].inc(sy.dma_start(out=xc[i % 2][:], in_=x_h[ti * 128:(ti + 1) * 128, :]), 16)
                        d_cs[i % 2].inc(sy.dma_start(out=scb_[i % 2][:], in_=s_s[ti]), 16)
                        W[f"xsld_{ti}"] = d_cs[i % 2].n

                for hc in range(24):
                    @blk.tensor
                    def _(te, tc=tc, hc=hc, c_sil=c_sil, v_hm=v_hm):
                        if tc == 0 and hc == 0:
                            cbar(te)
                            te.wait_ge(d_cw.sem, W["wg_ld"])
                        te.wait_ge(d_cx.sem, W[f"xnld_{tc}"])
                        if hc >= 2:
                            te.wait_ge(c_p.sem, c_sil(hc - 2))
                            te.wait_ge(v_p.sem, v_hm(hc - 2))
                        elif tc >= 1:
                            te.wait_ge(c_p.sem, W[f"sil_{tc - 1}_{22 + (hc % 2)}"])
                            te.wait_ge(v_p.sem, W[f"hm_{tc - 1}_{22 + (hc % 2)}"])
                        for dk in range(8):
                            te.matmul(pG[hc % 2][:], wg[dk][:, hc * 128:(hc + 1) * 128],
                                      xnC[:, dk * 512:(dk + 1) * 512],
                                      start=(dk == 0), stop=(dk == 7))
                        if tc == 0 and hc == 0:
                            te.wait_ge(d_cw.sem, W["wu_ld"])
                        for dk in range(8):
                            ins = te.matmul(pU[hc % 2][:], wu[dk][:, hc * 128:(hc + 1) * 128],
                                            xnC[:, dk * 512:(dk + 1) * 512],
                                            start=(dk == 0), stop=(dk == 7))
                        t_p.inc(ins)
                        W[f"gumm_{tc}_{hc}"] = t_p.n

                for i in range(4):
                    ti = 4 * tc + i

                    @blk.tensor
                    def _(te, tc=tc, i=i, ti=ti, v_hm=v_hm, c_gsil=c_gsil):
                        if tc == 0 and i == 0:
                            te.wait_ge(d_cw.sem, W["wd_ld"])
                        if i >= 2:
                            te.wait_ge(c_p.sem, c_gsil(i - 2))
                        elif tc >= 1:
                            te.wait_ge(c_p.sem, W[f"gsil_{4 * tc + i - 2}"])
                        for hc in range(24):
                            te.wait_ge(v_p.sem, v_hm(hc))
                            te.matmul(pDn[i % 2][:, 0:512], hT[hc][:, i * 128:(i + 1) * 128],
                                      wd[hc][:, 0:512], start=(hc == 0), stop=(hc == 23))
                            t_p.inc(te.matmul(pDn[i % 2][:, 512:1024], hT[hc][:, i * 128:(i + 1) * 128],
                                              wd[hc][:, 512:1024], start=(hc == 0), stop=(hc == 23)))
                            if i == 3:
                                dn_last[hc] = t_p.n
                        W[f"dnmm_{ti}"] = t_p.n

                @blk.scalar
                def _(sc, tc=tc, c_tc0=c_tc0, v_hm=v_hm, v_tc0=v_tc0, so_tc0=so_tc0):
                    if tc == 0:
                        cbar(sc)
                    for hc in range(24):
                        sc.wait_ge(t_p.sem, W[f"gumm_{tc}_{hc}"])
                        if hc >= 1:
                            sc.wait_ge(v_p.sem, v_hm(hc - 1))
                        elif tc >= 1:
                            sc.wait_ge(v_p.sem, W[f"hm_{tc - 1}_23"])
                        c_p.inc(sc.activation(sgb[hc % 2][:], pG[hc % 2][:], AF.Silu, scale=INV127))
                        W[f"sil_{tc}_{hc}"] = c_p.n
                    for i in range(4):
                        ti = 4 * tc + i
                        sc.wait_ge(t_p.sem, W[f"dnmm_{ti}"])
                        if i >= 1:
                            sc.wait_ge(v_p.sem, v_tc0 + 24 + (i - 1) + 1)
                            sc.wait_ge(d_so[(i - 1) % 2].sem, so_tc0[(i - 1) % 2] + 16 * ((i - 1) // 2 + 1))
                        elif ti >= 1:
                            sc.wait_ge(v_p.sem, W[f"fin_{ti - 1}"])
                            sc.wait_ge(d_so[(i - 1) % 2].sem, W.get(f"outst_{ti - 1}", 0))
                        c_p.inc(sc.activation(gs[i % 2][:], pDn[i % 2][:], AF.Silu,
                                              scale=INV127 * INV127))
                        W[f"gsil_{ti}"] = c_p.n

                @blk.vector
                def _(ve, tc=tc, c_sil=c_sil, c_gsil=c_gsil, dn_prev=dn_prev, so_tc0=so_tc0):
                    if tc == 0:
                        cbar(ve)
                    for hc in range(24):
                        ve.wait_ge(c_p.sem, c_sil(hc))
                        ve.wait_ge(t_p.sem, dn_prev.get(hc, 0))
                        v_p.inc(ve.tensor_mul(hT[hc][:], sgb[hc % 2][:], pU[hc % 2][:]))
                        W[f"hm_{tc}_{hc}"] = v_p.n
                    for i in range(4):
                        ti = 4 * tc + i
                        ve.wait_ge(c_p.sem, c_gsil(i))
                        ve.wait_ge(d_cs[i % 2].sem, W[f"xsld_{ti}"])
                        if i >= 2:
                            ve.wait_ge(d_so[i % 2].sem, so_tc0[i % 2] + 16)
                        else:
                            ve.wait_ge(d_so[i % 2].sem, W.get(f"outst_{ti - 2}", 0))
                        ve.tensor_mul(gs[i % 2][:], gs[i % 2][:], scb_[i % 2][:])
                        v_p.inc(ve.tensor_add(gs[i % 2][:], gs[i % 2][:], xc[i % 2][:]))
                        W[f"fin_{ti}"] = v_p.n

                for i in range(4):
                    ti = 4 * tc + i

                    @blk.gpsimd
                    def _(gp, ti=ti, i=i):
                        if ti == 0:
                            cbar(gp)
                        gp.wait_ge(v_p.sem, W[f"fin_{ti}"])
                        d_so[i % 2].inc(gp.dma_start(out=out_h[ti * 128:(ti + 1) * 128, :], in_=gs[i % 2][:]), 16)
                        W[f"outst_{ti}"] = d_so[i % 2].n

            @blk.gpsimd
            def _(gp):
                gp.wait_ge(d_so[0].sem, d_so[0].n)
                gp.wait_ge(d_so[1].sem, d_so[1].n)

    return nc


def _host_consts():
    hd = 64
    theta = 1.0 / (10000.0 ** (np.arange(0, hd, 2, dtype=np.float64) / hd))
    h_idx = np.arange(NH, dtype=np.float64)
    ang = h_idx[:, None] * theta[None, :]
    cos_t = np.cos(ang)
    sin_t = np.sin(ang)
    row = cos_t.reshape(-1)
    cosrow = np.concatenate([row, row]).reshape(1, D).astype(ml_dtypes.bfloat16)
    row = sin_t.reshape(-1)
    sinrow = np.concatenate([row, row]).reshape(1, D).astype(ml_dtypes.bfloat16)
    ident = np.zeros((40, 8), dtype=np.float32)
    ident[0:8, 0:8] = np.eye(8)
    ident[32:40, 0:8] = np.eye(8)
    identb = np.eye(128, dtype=ml_dtypes.bfloat16)
    onesb = np.ones((1, 128), dtype=ml_dtypes.bfloat16)
    onescol = np.ones((128, 1), dtype=ml_dtypes.bfloat16)
    kk = np.arange(128)[:, None]
    qq = np.arange(128)[None, :]
    masktri = (qq >= kk).astype(ml_dtypes.bfloat16)
    sel16 = np.zeros((16, NH * 64), dtype=ml_dtypes.bfloat16)
    for c in range(8):
        for m in range(128):
            sel16[2 * c + (m >= 64), c * 128 + m] = 1
    return cosrow, sinrow, ident, identb, onesb, onescol, masktri, sel16


def kernel(L0_attn_x, L0_attn_W_attn, L0_attn_b_attn, L0_attn_W_proj,
           L0_attn_b_proj, L0_ffn_w_gate, L0_ffn_w_up, L0_ffn_w_down):
    x = np.ascontiguousarray(np.asarray(L0_attn_x, dtype=np.float32))
    wat = np.ascontiguousarray(np.asarray(L0_attn_W_attn, dtype=np.float32))
    ba = np.ascontiguousarray(np.asarray(L0_attn_b_attn, dtype=np.float32)).reshape(1, 3 * D)
    wp = np.ascontiguousarray(np.asarray(L0_attn_W_proj, dtype=np.float32))
    bp = np.ascontiguousarray(np.asarray(L0_attn_b_proj, dtype=np.float32)).reshape(1, D)
    wgT = np.ascontiguousarray(np.asarray(L0_ffn_w_gate, dtype=np.int32).T)
    wuT = np.ascontiguousarray(np.asarray(L0_ffn_w_up, dtype=np.int32).T)
    wdT = np.ascontiguousarray(np.asarray(L0_ffn_w_down, dtype=np.int32).T)
    cosrow, sinrow, ident, identb, onesb, onescol, masktri, sel16 = _host_consts()

    nc = build_program()
    shared = dict(w_attn=wat, b_attn=ba, w_proj=wp, b_proj=bp,
                  wg_t=wgT, wu_t=wuT, wd_t=wdT,
                  cosrow=cosrow, sinrow=sinrow, ident=ident, identb=identb,
                  onesb=onesb, onescol=onescol, masktri=masktri, sel16=sel16)
    in_maps = [dict(x=np.ascontiguousarray(x[b]), **shared) for b in range(8)]
    res = run_bass_kernel_spmd(nc, in_maps, core_ids=list(range(8)))
    out = np.stack([res.results[b]["out"] for b in range(8)], axis=0)
    return out.astype(np.float32)


# version 19
# speedup vs baseline: 1.0141x; 1.0141x over previous
import contextlib
import math
import numpy as np
import ml_dtypes
import concourse.bass as bass
import concourse.mybir as mybir
from concourse.bass_utils import run_bass_kernel_spmd

F32 = mybir.dt.float32
BF16 = mybir.dt.bfloat16
I32 = mybir.dt.int32
AF = mybir.ActivationFunctionType
AX = mybir.AxisListType
OP = mybir.AluOpType

T = 4096
D = 1024
NH = 16
NT = 8
HID = 3072
VST = 65
INV127 = 1.0 / 127.0


class Ctr:
    def __init__(self, ctx, nc, name):
        self.sem = ctx.enter_context(nc.semaphore(name))
        self.n = 0

    def inc(self, instr, k=1):
        self.n += k
        instr.then_inc(self.sem, k)
        return instr


def build_program(dbg=False):
    nc = bass.Bass("TRN2", target_bir_lowering=False, debug=False)

    x_h = nc.dram_tensor("x", [T, D], F32, kind="ExternalInput").ap()
    wat_h = nc.dram_tensor("w_attn", [D, 3 * D], F32, kind="ExternalInput").ap()
    ba_h = nc.dram_tensor("b_attn", [1, 3 * D], F32, kind="ExternalInput").ap()
    wp_h = nc.dram_tensor("w_proj", [D, D], F32, kind="ExternalInput").ap()
    bp_h = nc.dram_tensor("b_proj", [1, D], F32, kind="ExternalInput").ap()
    wgT_h = nc.dram_tensor("wg_t", [D, HID], I32, kind="ExternalInput").ap()
    wuT_h = nc.dram_tensor("wu_t", [D, HID], I32, kind="ExternalInput").ap()
    wdT_h = nc.dram_tensor("wd_t", [HID, D], I32, kind="ExternalInput").ap()
    cosr_h = nc.dram_tensor("cosrow", [1, D], BF16, kind="ExternalInput").ap()
    sinr_h = nc.dram_tensor("sinrow", [1, D], BF16, kind="ExternalInput").ap()
    ident_h = nc.dram_tensor("ident", [40, 8], F32, kind="ExternalInput").ap()
    identb_h = nc.dram_tensor("identb", [128, 128], BF16, kind="ExternalInput").ap()
    onesb_h = nc.dram_tensor("onesb", [1, 128], BF16, kind="ExternalInput").ap()
    onescol_h = nc.dram_tensor("onescol", [128, 1], BF16, kind="ExternalInput").ap()
    mask_h = nc.dram_tensor("masktri", [128, 128], BF16, kind="ExternalInput").ap()
    sel16_h = nc.dram_tensor("sel16", [16, NH * 64], BF16, kind="ExternalInput").ap()

    out_h = nc.dram_tensor("out", [T, D], F32, kind="ExternalOutput").ap()

    KI = "ExternalOutput" if dbg else "Internal"
    wrs_h = nc.dram_tensor("w_rope_s", [16, 8, 128, 128], BF16, kind=KI).ap()
    wp_s = nc.dram_tensor("wp_s", [8, 128, D], BF16, kind=KI).ap()
    wbs_h = nc.dram_tensor("wbs", [72, 128, 1024], BF16, kind=KI).ap()
    xnT_s = nc.dram_tensor("xnT_s", [NT, 128, T], BF16, kind=KI).ap()
    s_s = nc.dram_tensor("s_s", [4 * NT, 128, D], BF16, kind=KI).ap()

    ctx = contextlib.ExitStack()
    with ctx:
        def sb(name, shape, dt):
            return ctx.enter_context(nc.sbuf_tensor(name, shape, dt))

        ident = sb("s_ident", [40, 8], F32)
        onesb = sb("s_onesb", [1, 128], BF16)
        onescol = sb("s_onescol", [128, 1], BF16)
        mask = sb("s_mask", [128, 128], BF16)
        bT = sb("bT", [128, 24], F32)
        bTb = sb("bTb", [128, 24], BF16)
        bpp = sb("bpp", [1, D], BF16)
        qmaxT = sb("qmaxT", [128, 16], BF16)
        gtmp = sb("gtmp", [128, 1], BF16)
        cbt = sb("cbt", [128, 8], F32)
        lnsc = sb("lnsc", [128, 65], F32)
        identb = sb("s_identb", [128, 128], BF16)
        c8 = sb("c8", [8, 1], F32)
        crowb = sb("crowb", [1, 8], BF16)
        prodall = sb("prodall", [128, 8], BF16)
        sel16 = sb("s_sel16", [16, NH * 64], BF16)

        d_ms = Ctr(ctx, nc, "d_ms")
        d_w4 = Ctr(ctx, nc, "d_w4")
        d_x = [Ctr(ctx, nc, "d_x0"), Ctr(ctx, nc, "d_x1")]
        d_we = [Ctr(ctx, nc, "d_we0"), Ctr(ctx, nc, "d_we1")]
        d_wp = [Ctr(ctx, nc, f"d_wp{i}") for i in range(4)]
        d_wcl = Ctr(ctx, nc, "d_wcl")
        d_wcs = Ctr(ctx, nc, "d_wcs")
        d_swp = [Ctr(ctx, nc, "d_swp0"), Ctr(ctx, nc, "d_swp1")]
        d_wrs = Ctr(ctx, nc, "d_wrs")
        d_sx = Ctr(ctx, nc, "d_sx")
        d_ss = Ctr(ctx, nc, "d_ss")
        d_sl = Ctr(ctx, nc, "d_sl")
        d_so = [Ctr(ctx, nc, "d_so0"), Ctr(ctx, nc, "d_so1")]
        d_cw = Ctr(ctx, nc, "d_cw")
        d_cx = Ctr(ctx, nc, "d_cx")
        d_cs = [Ctr(ctx, nc, "d_cs0"), Ctr(ctx, nc, "d_cs1")]
        t_p = Ctr(ctx, nc, "t_p")
        v_p = Ctr(ctx, nc, "v_p")
        c_p = Ctr(ctx, nc, "c_p")
        g_p = Ctr(ctx, nc, "g_p")

        blk = ctx.enter_context(nc.Block())
        W = {}

        ab = contextlib.ExitStack()
        with ab:
            def sba(name, shape, dt):
                return ab.enter_context(nc.sbuf_tensor(name, shape, dt))

            kT = sba("kT", [128, 8 * T], BF16)
            vP = sba("vP", [128, 32 * NH * VST], BF16)
            wv = sba("wv", [128, 8 * 1024], BF16)
            we = [sba(f"we{i}", [128, 1024], BF16) for i in range(2)]
            wpb = [sba(f"wpb{i}", [128, 1024], BF16) for i in range(2)]
            pP = [sba(f"pP{i}", [128, 512], BF16) for i in range(4)]
            xb = [sba(f"xb{i}", [128, D], F32) for i in range(2)]
            Lr = sba("Lr", [16, 512], F32)

            pA = ab.enter_context(nc.psum_tensor("pA", [128, 1024], F32))
            pB = ab.enter_context(nc.psum_tensor("pB", [128, 1024], F32))
            pC = ab.enter_context(nc.psum_tensor("pC", [128, 1024], F32))
            pD = ab.enter_context(nc.psum_tensor("pD", [128, 1024], F32))

            pro = contextlib.ExitStack()
            with pro:
                csb = pro.enter_context(nc.sbuf_tensor("csb", [128, 1024], BF16))

                @blk.sync
                def _(sy):
                    for dst, src in ((ident, ident_h), (identb, identb_h),
                                     (onesb, onesb_h),
                                     (onescol, onescol_h), (mask, mask_h),
                                     (sel16, sel16_h)):
                        d_ms.inc(sy.dma_start(out=dst[:], in_=src), 16)
                    d_ms.inc(sy.dma_start(out=csb[0:1, 0:512], in_=cosr_h[0:1, 0:512]), 16)
                    d_ms.inc(sy.dma_start(out=csb[0:1, 512:1024], in_=sinr_h[0:1, 0:512]), 16)
                    d_ms.inc(sy.dma_start(out=xb[0][0:1, :], in_=bp_h), 16)
                    d_ms.inc(sy.dma_start(out=kT[0:1, 0:2048].bitcast(F32), in_=ba_h[0:1, 0:1024]), 16)
                    d_ms.inc(sy.dma_start(out=kT[0:1, 2048:4096].bitcast(F32), in_=ba_h[0:1, 1024:2048]), 16)
                    d_ms.inc(sy.dma_start(out=xb[1][0:1, :], in_=ba_h[0:1, 2048:3072]), 16)
                    W["const_all"] = d_ms.n

                v0_cs = v_p.n

                @blk.tensor
                def _(te):
                    te.wait_ge(d_ms.sem, W["const_all"])
                    for j in range(2):
                        t_p.inc(te.matmul(pA[:, j * 512:(j + 1) * 512],
                                          onesb[0:1, :], csb[0:1, j * 512:(j + 1) * 512],
                                          start=True, stop=True))
                        W[f"csmm{j}"] = t_p.n

                @blk.vector
                def _(ve):
                    for j in range(2):
                        ve.wait_ge(t_p.sem, W[f"csmm{j}"])
                        v_p.inc(ve.tensor_copy(csb[:, j * 512:(j + 1) * 512],
                                               pA[:, j * 512:(j + 1) * 512]))
                    W["csb"] = v_p.n
                    ve.wait_ge(d_ms.sem, W["const_all"])
                    for third in (0, 1):
                        row = kT[0:1, third * 2048:(third + 1) * 2048].bitcast(F32)
                        bev = row[0:1, 0:1024:2]
                        bod = row[0:1, 1:1024:2]
                        cev = csb[0:1, 0:512]
                        snv = csb[0:1, 512:1024]
                        pa_ = pD[0:1, 0:512]
                        pb_ = pD[0:1, 512:1024]
                        sb_ = Lr[0:1, :]
                        ve.tensor_mul(pb_, bev, snv)
                        ve.tensor_mul(sb_, bod, snv)
                        v_p.inc(ve.tensor_mul(pa_, bev, cev))
                        ve.wait_ge(v_p.sem, v_p.n)
                        v_p.inc(ve.tensor_tensor(bev, pa_, sb_, OP.subtract))
                        ve.wait_ge(v_p.sem, v_p.n)
                        v_p.inc(ve.tensor_mul(sb_, bod, cev))
                        ve.wait_ge(v_p.sem, v_p.n)
                        ve.tensor_tensor(bod, pb_, sb_, OP.add)
                    v_p.inc(ve.engine_nop())
                    W["bfold"] = v_p.n

                @blk.tensor
                def _(te):
                    te.wait_ge(v_p.sem, W["bfold"])
                    for c in range(24):
                        src = (kT[0:1, 0:2048].bitcast(F32),
                               kT[0:1, 2048:4096].bitcast(F32), xb[1][0:1, :])[c // 8]
                        ins = te.transpose(pC[:, c:c + 1],
                                           src[0:1, (c % 8) * 128:(c % 8) * 128 + 128],
                                           ident[0:1, 0:1])
                    t_p.inc(ins)
                    W["btr"] = t_p.n

                @blk.vector
                def _(ve):
                    ve.wait_ge(t_p.sem, W["btr"])
                    v_p.inc(ve.tensor_copy(bT[:], pC[:, 0:24]))
                    ve.wait_ge(v_p.sem, v_p.n)
                    ve.tensor_copy(bTb[:], bT[:])
                    ve.memset(lnsc[:, 64:65], 1e-5)
                    ve.tensor_copy(pP[0][0:1, :], xb[0][0:1, 0:512])
                    v_p.inc(ve.tensor_copy(pP[1][0:1, :], xb[0][0:1, 512:1024]))
                    W["bT_bp"] = v_p.n

                wst4 = xb[1]
                if True:
                    v_wp0 = v_p.n

                    @blk.sync
                    def _(sy):
                        sy.wait_ge(t_p.sem, W["btr"])
                        for c in range(8):
                            if c >= 1:
                                sy.wait_ge(v_p.sem, v_wp0 + c)
                            d_w4.inc(sy.dma_start(out=wst4[:],
                                                  in_=wp_h[c * 128:(c + 1) * 128, :]), 16)
                            W[f"wpld{c}"] = d_w4.n

                    st_wp0 = [d_swp[0].n, d_swp[1].n]

                    @blk.vector
                    def _(ve):
                        for c in range(8):
                            ve.wait_ge(d_w4.sem, W[f"wpld{c}"])
                            if c >= 2:
                                ve.wait_ge(d_swp[c % 2].sem, st_wp0[c % 2] + 16 * (c // 2))
                            v_p.inc(ve.tensor_copy(we[c % 2][:], wst4[:]))
                            W[f"wpcv{c}"] = v_p.n

                    @blk.gpsimd
                    def _(gp):
                        for c in range(8):
                            gp.wait_ge(v_p.sem, W[f"wpcv{c}"])
                            d_swp[c % 2].inc(gp.dma_start(out=wp_s[c], in_=we[c % 2][:]), 16)
                            W[f"wpst{c}"] = d_swp[c % 2].n
                        W["wp_done0"] = d_swp[0].n
                        W["wp_done1"] = d_swp[1].n

                    t_bpp0 = t_p.n

                    @blk.sync
                    def _(sy):
                        sy.wait_ge(d_swp[0].sem, W["wp_done0"])
                        sy.wait_ge(d_swp[1].sem, W["wp_done1"])
                        for c in range(8):
                            if c >= 2:
                                sy.wait_ge(t_p.sem, t_bpp0 + c - 1)
                            d_wp[c % 2].inc(sy.dma_start(out=wpb[c % 2][:], in_=wp_s[c]), 16)
                            W[f"bppld{c}"] = d_wp[c % 2].n

                    @blk.tensor
                    def _(te):
                        te.wait_ge(v_p.sem, W["bT_bp"])
                        for c in range(8):
                            te.wait_ge(d_wp[c % 2].sem, W[f"bppld{c}"])
                            te.matmul(pD[0:1, 0:512], bTb[:, 16 + c:17 + c], wpb[c % 2][:, 0:512],
                                      start=(c == 0), stop=False)
                            t_p.inc(te.matmul(pD[0:1, 512:1024], bTb[:, 16 + c:17 + c],
                                              wpb[c % 2][:, 512:1024], start=(c == 0), stop=False))
                            W[f"bppmm{c}"] = t_p.n
                        te.matmul(pD[0:1, 0:512], onesb[0:1, 0:1], pP[0][0:1, :], start=False, stop=True)
                        t_p.inc(te.matmul(pD[0:1, 512:1024], onesb[0:1, 0:1], pP[1][0:1, :],
                                          start=False, stop=True))
                        W["bppdone"] = t_p.n

                    @blk.vector
                    def _(ve):
                        ve.wait_ge(t_p.sem, W["bppdone"])
                        ve.tensor_copy(bpp[:], pD[0:1, :])
                        ve.memset(qmaxT[:], 0.0)
                        v_p.inc(ve.memset(vP[:, 64::VST], 1.0))
                        W["initdone"] = v_p.n

                    v_wf0 = v_p.n

                    @blk.sync
                    def _(sy):
                        for dk in range(8):
                            for th in range(3):
                                if dk == 0 and th == 0:
                                    sy.wait_ge(t_p.sem, W["bppdone"])
                                else:
                                    sy.wait_ge(v_p.sem, v_wf0 + 3 * dk + th)
                                d_w4.inc(sy.dma_start(
                                    out=wst4[:],
                                    in_=wat_h[dk * 128:(dk + 1) * 128, th * 1024:(th + 1) * 1024]), 16)
                                W[f"watld_{dk}_{th}"] = d_w4.n

                    st_wf0 = d_wrs.n

                    @blk.vector
                    def _(ve):
                        for dk in range(8):
                            for th in range(3):
                                ve.wait_ge(d_w4.sem, W[f"watld_{dk}_{th}"])
                                if th < 2:
                                    if dk >= 1:
                                        ve.wait_ge(d_wrs.sem, st_wf0 + 32 * dk)
                                    dst = wpb[th]
                                    cev = csb[:, 0:512]
                                    snv = csb[:, 512:1024]
                                    ve.tensor_mul(xb[0][:, 0:512], wst4[:, 0::2], cev)
                                    ve.tensor_mul(pD[:, 0:512], wst4[:, 1::2], snv)
                                    ve.tensor_tensor(dst[:, 0::2], xb[0][:, 0:512],
                                                     pD[:, 0:512], OP.subtract)
                                    ve.tensor_mul(xb[0][:, 0:512], wst4[:, 1::2], cev)
                                    ve.tensor_mul(pD[:, 0:512], wst4[:, 0::2], snv)
                                    v_p.inc(ve.tensor_tensor(dst[:, 1::2], xb[0][:, 0:512],
                                                             pD[:, 0:512], OP.add))
                                else:
                                    v_p.inc(ve.tensor_copy(wv[:, dk * 1024:(dk + 1) * 1024], wst4[:]))
                                W[f"wf_{dk}_{th}"] = v_p.n

                    @blk.gpsimd
                    def _(gp):
                        for dk in range(8):
                            gp.wait_ge(v_p.sem, W[f"wf_{dk}_1"])
                            d_wrs.inc(gp.dma_start(
                                out=wrs_h[0:8, dk].rearrange("e p c -> p e c"),
                                in_=wpb[0][:].rearrange("p (e c) -> p e c", e=8)), 16)
                            d_wrs.inc(gp.dma_start(
                                out=wrs_h[8:16, dk].rearrange("e p c -> p e c"),
                                in_=wpb[1][:].rearrange("p (e c) -> p e c", e=8)), 16)
                            W[f"wqkst{dk}"] = d_wrs.n
                        W["wrs_done"] = d_wrs.n
            qy = sba("qy", [128, 8 * 512], BF16)
            yT = sba("yT", [128, 8 * 512], BF16)
            sev0 = sba("sev0", [128, D], BF16)
            xnb = [sba("xnb0", [128, D], BF16)] * 2
            xnT = sba("xnT", [128, 8 * 512], BF16)
            Rb = sba("Rb", [16, 512], BF16)
            wcb = [sba(f"wcb{i}", [128, 512], I32) for i in range(2)]
            wcvb = sba("wcvb", [128, 512], BF16)

            lastS = {("A", 0): 0, ("A", 1): 0, ("B", 0): 0, ("B", 1): 0}
            W["pj_done_-1"] = 0
            W["pjev_done_-1"] = W["initdone"]
            W["yev_done_-1"] = W["bT_bp"]
            last_wpb = [W["bppmm6"], W["bppmm7"]]

            def emit_ln_load(t, i):
                ti = 4 * t + i

                @blk.sync
                def _(sy, t=t, i=i, ti=ti):
                    if t == 0 and i <= 1:
                        sy.wait_ge(d_wrs.sem, W["wrs_done"])
                        sy.wait_ge(v_p.sem, W["bT_bp"])
                    if ti >= 2:
                        sy.wait_ge(v_p.sem, W[f"xnapply_{ti - 2}"])
                    d_x[i % 2].inc(sy.dma_start(out=xb[i % 2][:], in_=x_h[ti * 128:(ti + 1) * 128, :]), 16)
                    W[f"xld_{ti}"] = d_x[i % 2].n

            def emit_ln_stats(t, i):
                ti = 4 * t + i
                L = 16 * i

                @blk.vector
                def _(ve, t=t, i=i, ti=ti, L=L):
                    ve.wait_ge(d_x[i % 2].sem, W[f"xld_{ti}"])
                    ve.bn_stats(lnsc[:, L + 0:L + 6], xb[i % 2][:, 0:512])
                    v_p.inc(ve.bn_stats(lnsc[:, L + 6:L + 12], xb[i % 2][:, 512:1024]))
                    ve.wait_ge(v_p.sem, v_p.n)
                    v_p.inc(ve.bn_aggr(lnsc[:, L + 12:L + 14], lnsc[:, L + 0:L + 12]))
                    W[f"var_{ti}"] = v_p.n

            def emit_ln_apply(t, i, c_rstd_done):
                ti = 4 * t + i
                L = 16 * i

                @blk.vector
                def _(ve, t=t, i=i, ti=ti, L=L, c_rstd_done=c_rstd_done):
                    if ti >= 1:
                        ve.wait_ge(t_p.sem, W[f"trmm_{ti - 1}"])
                    ve.wait_ge(c_p.sem, c_rstd_done(i))
                    v_p.inc(ve.scalar_tensor_tensor(lnsc[:, L + 0:L + 1], lnsc[:, L + 12:L + 13], -1.0,
                                                    lnsc[:, L + 15:L + 16], OP.mult, OP.mult))
                    ve.wait_ge(v_p.sem, v_p.n)
                    v_p.inc(ve.tensor_scalar(xnb[i % 2][:], xb[i % 2][:],
                                             lnsc[:, L + 15:L + 16], lnsc[:, L + 0:L + 1],
                                             OP.mult, OP.add))
                    W[f"xnapply_{ti}"] = v_p.n

            def emit_ln_calc(t, i, c_rstd_done):
                emit_ln_stats(t, i)
                emit_ln_apply(t, i, c_rstd_done)

            def emit_ln_tile(t, i, c_rstd_done):
                emit_ln_load(t, i)
                emit_ln_calc(t, i, c_rstd_done)

            def emit_trans_tile(t, i):
                ti = 4 * t + i

                @blk.tensor
                def _(te, t=t, i=i, ti=ti):
                    te.wait_ge(v_p.sem, W[f"xnapply_{ti}"])
                    if i == 0:
                        te.wait_ge(v_p.sem, W[f"cbready_{t - 1}"])
                        te.wait_ge(v_p.sem, W.get(f"pjev_{t - 2}_2", 0))
                    else:
                        te.wait_ge(v_p.sem, W[f"trev_{ti - 1}"])
                    for dk in range(8):
                        ins = te.transpose(pD[:, dk * 64:dk * 64 + 64].bitcast(BF16),
                                           xnb[i % 2][:, dk * 128:(dk + 1) * 128], identb[:])
                    t_p.inc(ins)
                    W[f"trmm_{ti}"] = t_p.n

                @blk.vector
                def _(ve, t=t, i=i, ti=ti):
                    ve.wait_ge(t_p.sem, W[f"trmm_{ti}"])
                    if i == 0:
                        ve.wait_ge(d_sx.sem, W.get(f"xnst_{t - 1}", 0))
                        ve.wait_ge(t_p.sem, W.get(f"vmm_{4 * (t - 1) + 3}", 0))
                    for dk in range(8):
                        ins = ve.tensor_copy(xnT[:, dk * 512 + i * 128:dk * 512 + i * 128 + 128],
                                             pD[:, dk * 64:dk * 64 + 64].bitcast(BF16))
                    v_p.inc(ins)
                    W[f"trev_{ti}"] = v_p.n

            def emit_qk_chunk(t, e):
                if e >= 10:
                    prev_mm = f"qkmm_{t}_{e - 2}"
                    prev_ev = f"qkev_{t}_{e - 2}"
                elif e >= 8:
                    prev_mm = f"qkmm_{t - 1}_{6 + (e % 2)}"
                    prev_ev = None
                elif e >= 2:
                    prev_mm = f"qkmm_{t}_{e - 2}"
                    prev_ev = f"qkev_{t}_{e - 2}"
                else:
                    prev_mm = f"qkmm_{t}_{14 + (e % 2)}"
                    prev_ev = None

                @blk.sync
                def _(sy, t=t, e=e, prev_mm=prev_mm):
                    if t == 0 and e in (8, 9):
                        sy.wait_ge(d_swp[0].sem, W["wp_done0"])
                        sy.wait_ge(d_swp[1].sem, W["wp_done1"])
                    prev = W.get(prev_mm, 0)
                    if prev:
                        sy.wait_ge(t_p.sem, prev)
                    d_we[e % 2].inc(sy.dma_start(
                        out=we[e % 2][:].rearrange("p (d c) -> p d c", d=8),
                        in_=wrs_h[e].rearrange("d p c -> p d c")), 16)
                    W[f"weld_{t}_{e}"] = d_we[e % 2].n

                @blk.tensor
                def _(te, t=t, e=e, prev_ev=prev_ev):
                    ps = pA if e < 8 else pD
                    te.wait_ge(d_we[e % 2].sem, W[f"weld_{t}_{e}"])
                    te.wait_ge(v_p.sem, W[f"trev_{4 * t + 3}"])
                    if e < 2:
                        te.wait_ge(c_p.sem, W.get(f"exps_done_{t - 1}", 0))
                    if prev_ev is not None and prev_ev in W:
                        te.wait_ge(v_p.sem, W[prev_ev])
                    half = (e % 2) * 512
                    for dk in range(8):
                        ins = te.matmul(ps[:, half:half + 512],
                                        we[e % 2][:, dk * 128:(dk + 1) * 128],
                                        xnT[:, dk * 512:(dk + 1) * 512],
                                        start=(dk == 0), stop=(dk == 7))
                    t_p.inc(ins)
                    W[f"qkmm_{t}_{e}"] = t_p.n

                @blk.vector
                def _(ve, t=t, e=e):
                    ps = pA if e < 8 else pD
                    ve.wait_ge(t_p.sem, W[f"qkmm_{t}_{e}"])
                    if e < 8:
                        dst = qy[:, e * 512:(e + 1) * 512]
                    else:
                        dst = kT[:, (e - 8) * T + t * 512:(e - 8) * T + t * 512 + 512]
                    half = (e % 2) * 512
                    v_p.inc(ve.tensor_scalar(dst, ps[:, half:half + 512],
                                             bT[:, e:e + 1], None, OP.add))
                    W[f"qkev_{t}_{e}"] = v_p.n
                    ve.wait_ge(v_p.sem, v_p.n)
                    v_p.inc(ve.reduce_max(gtmp[:], dst, axis=AX.X, apply_absolute_value=True))
                    ve.wait_ge(v_p.sem, v_p.n)
                    g_p.inc(ve.tensor_max(qmaxT[:, e:e + 1], qmaxT[:, e:e + 1], gtmp[:]))
                    W[f"gmax_{t}_{e}"] = g_p.n

            def emit_qk_sync(t, e):
                prev_mm = f"qkmm_{t}_{e - 2}" if e >= 10 else f"qkmm_{t - 1}_{6 + (e % 2)}"

                @blk.sync
                def _(sy, t=t, e=e, prev_mm=prev_mm):
                    if t == 0 and e in (8, 9):
                        sy.wait_ge(d_swp[0].sem, W["wp_done0"])
                        sy.wait_ge(d_swp[1].sem, W["wp_done1"])
                    prev = W.get(prev_mm, 0)
                    if prev:
                        sy.wait_ge(t_p.sem, prev)
                    d_we[e % 2].inc(sy.dma_start(
                        out=we[e % 2][:].rearrange("p (d c) -> p d c", d=8),
                        in_=wrs_h[e].rearrange("d p c -> p d c")), 16)
                    W[f"weld_{t}_{e}"] = d_we[e % 2].n

            def qk_fillers(t, e):
                prev_ev = f"qkev_{t}_{e - 2}" if e >= 10 else None

                def mk(dk, t=t, e=e, prev_ev=prev_ev):
                    def emit(te):
                        if dk == 0:
                            te.wait_ge(d_we[e % 2].sem, W[f"weld_{t}_{e}"])
                            te.wait_ge(v_p.sem, W[f"trev_{4 * t + 3}"])
                            if prev_ev is not None and prev_ev in W:
                                te.wait_ge(v_p.sem, W[prev_ev])
                        half = (e % 2) * 512
                        ins = te.matmul(pD[:, half:half + 512],
                                        we[e % 2][:, dk * 128:(dk + 1) * 128],
                                        xnT[:, dk * 512:(dk + 1) * 512],
                                        start=(dk == 0), stop=(dk == 7))
                        if dk == 7:
                            t_p.inc(ins)
                            W[f"qkmm_{t}_{e}"] = t_p.n
                    return emit
                return [mk(dk) for dk in range(8)]

            def trans_fillers(t, i):
                ti = 4 * t + i

                def mk(dk, t=t, i=i, ti=ti):
                    def emit(te):
                        if dk == 0:
                            te.wait_ge(v_p.sem, W[f"xnapply_{ti}"])
                            if i == 0:
                                te.wait_ge(v_p.sem, W[f"cbready_{t - 1}"])
                                te.wait_ge(v_p.sem, W.get(f"pjev_{t - 2}_2", 0))
                            else:
                                te.wait_ge(v_p.sem, W[f"trev_{ti - 1}"])
                        ins = te.transpose(pD[:, dk * 64:dk * 64 + 64].bitcast(BF16),
                                           xnb[i % 2][:, dk * 128:(dk + 1) * 128], identb[:])
                        if dk == 7:
                            t_p.inc(ins)
                            W[f"trmm_{ti}"] = t_p.n
                    return emit
                return [mk(dk) for dk in range(8)]

            def emit_trans_ev(t, i):
                ti = 4 * t + i

                @blk.vector
                def _(ve, t=t, i=i, ti=ti):
                    ve.wait_ge(t_p.sem, W[f"trmm_{ti}"])
                    if i == 0:
                        ve.wait_ge(d_sx.sem, W.get(f"xnst_{t - 1}", 0))
                        ve.wait_ge(t_p.sem, W.get(f"vmm_{4 * (t - 1) + 3}", 0))
                    for dk in range(8):
                        ins = ve.tensor_copy(xnT[:, dk * 512 + i * 128:dk * 512 + i * 128 + 128],
                                             pD[:, dk * 64:dk * 64 + 64].bitcast(BF16))
                    v_p.inc(ins)
                    W[f"trev_{ti}"] = v_p.n

            def emit_qk_ev(t, e):
                @blk.vector
                def _(ve, t=t, e=e):
                    ve.wait_ge(t_p.sem, W[f"qkmm_{t}_{e}"])
                    dst = kT[:, (e - 8) * T + t * 512:(e - 8) * T + t * 512 + 512]
                    half = (e % 2) * 512
                    v_p.inc(ve.tensor_scalar(dst, pD[:, half:half + 512],
                                             bT[:, e:e + 1], None, OP.add))
                    W[f"qkev_{t}_{e}"] = v_p.n
                    ve.wait_ge(v_p.sem, v_p.n)
                    v_p.inc(ve.reduce_max(gtmp[:], dst, axis=AX.X, apply_absolute_value=True))
                    ve.wait_ge(v_p.sem, v_p.n)
                    g_p.inc(ve.tensor_max(qmaxT[:, e:e + 1], qmaxT[:, e:e + 1], gtmp[:]))
                    W[f"gmax_{t}_{e}"] = g_p.n

            def emit_v_part_i(t, i):
                ti = 4 * t + i

                @blk.tensor
                def _(te, t=t, i=i, ti=ti):
                    if i == 0:
                        te.wait_ge(c_p.sem, W.get(f"exps_done_{t - 1}", 0))
                    else:
                        te.wait_ge(v_p.sem, W.get(f"vev_{ti - 1}", 0))
                    for dk in range(8):
                        lhs = xnT[:, dk * 512 + i * 128:dk * 512 + i * 128 + 128]
                        te.matmul(pB[:, 0:512], lhs, wv[:, dk * 1024:dk * 1024 + 512],
                                  start=(dk == 0), stop=(dk == 7))
                        ins = te.matmul(pB[:, 512:1024], lhs, wv[:, dk * 1024 + 512:dk * 1024 + 1024],
                                        start=(dk == 0), stop=(dk == 7))
                    t_p.inc(ins)
                    W[f"vmm_{ti}"] = t_p.n

                @blk.vector
                def _(ve, ti=ti):
                    ve.wait_ge(t_p.sem, W[f"vmm_{ti}"])
                    dst = vP[:, ti * NH * VST:(ti + 1) * NH * VST]
                    dst3 = dst.rearrange("p (h s) -> p h s", s=VST)[:, :, 0:64]
                    src3 = pB[:].rearrange("p (h s) -> p h s", s=64)
                    v_p.inc(ve.tensor_copy(dst3, src3))
                    W[f"vev_{ti}"] = v_p.n

            def emit_cb(t):
                @blk.vector
                def _(ve, t=t):
                    ve.wait_ge(g_p.sem, W[f"gmax_{t}_7"])
                    ve.wait_ge(t_p.sem, W.get(f"cbm3_{t - 1}", 0))
                    v_p.inc(ve.tensor_mul(prodall[:], qmaxT[:, 0:8], qmaxT[:, 8:16]))
                    W[f"cbp_{t}"] = v_p.n

                @blk.tensor
                def _(te, t=t):
                    te.wait_ge(v_p.sem, W[f"cbp_{t}"])
                    t_p.inc(te.matmul(pD[0:8, 0:1], prodall[:], onescol[:], start=True, stop=True))
                    W[f"cbm1_{t}"] = t_p.n

                @blk.vector
                def _(ve, t=t):
                    ve.wait_ge(t_p.sem, W[f"cbm1_{t}"])
                    v_p.inc(ve.tensor_copy(c8[:], pD[0:8, 0:1]))
                    W[f"cbc8_{t}"] = v_p.n

                @blk.tensor
                def _(te, t=t):
                    te.wait_ge(v_p.sem, W[f"cbc8_{t}"])
                    t_p.inc(te.transpose(pD[0:1, 8:16], c8[:], ident[0:8, 0:8]))
                    W[f"cbm2_{t}"] = t_p.n

                @blk.vector
                def _(ve, t=t):
                    ve.wait_ge(t_p.sem, W[f"cbm2_{t}"])
                    v_p.inc(ve.tensor_scalar_mul(crowb[:], pD[0:1, 8:16], -0.0625))
                    W[f"cbrow_{t}"] = v_p.n

                @blk.tensor
                def _(te, t=t):
                    te.wait_ge(v_p.sem, W[f"cbrow_{t}"])
                    t_p.inc(te.matmul(pD[:, 16:24], onesb[0:1, :], crowb[:], start=True, stop=True))
                    W[f"cbm3_{t}"] = t_p.n

                @blk.vector
                def _(ve, t=t):
                    ve.wait_ge(t_p.sem, W[f"cbm3_{t}"])
                    v_p.inc(ve.tensor_copy(cbt[:], pD[:, 16:24]))
                    W[f"cbready_{t}"] = v_p.n

            def emit_recip(tb):
                @blk.vector
                def _(ve, tb=tb):
                    ve.wait_ge(t_p.sem, W.get(f"rbmm_{tb - 1}_7", 0))
                    ve.wait_ge(d_sl.sem, W[f"lst_{tb}_7"])
                    ve.reciprocal(Lr[:], Lr[:])
                    v_p.inc(ve.tensor_copy(Rb[:], Lr[:]))
                    W[f"rb_{tb}"] = v_p.n

            def emit_ynorm(tb, yc):
                @blk.tensor
                def _(te, tb=tb, yc=yc):
                    reg = (pD if yc % 4 < 2 else pC)
                    off = (yc % 2) * 512
                    te.wait_ge(v_p.sem, W[f"rb_{tb}"])
                    if yc >= 4:
                        te.wait_ge(v_p.sem, W[f"ynorm_{tb}_{yc - 4}"])
                    t_p.inc(te.matmul(reg[:, off:off + 512],
                                      sel16[:, yc * 128:(yc + 1) * 128], Rb[:],
                                      start=True, stop=True))
                    W[f"rbmm_{tb}_{yc}"] = t_p.n

                @blk.vector
                def _(ve, tb=tb, yc=yc):
                    reg = (pD if yc % 4 < 2 else pC)
                    off = (yc % 2) * 512
                    ve.wait_ge(t_p.sem, W[f"rbmm_{tb}_{yc}"])
                    v_p.inc(ve.tensor_mul(yT[:, yc * 512:(yc + 1) * 512],
                                          yT[:, yc * 512:(yc + 1) * 512],
                                          reg[:, off:off + 512]))
                    W[f"ynorm_{tb}_{yc}"] = v_p.n

            def emit_proj(tb):
                @blk.sync
                def _(sy, tb=tb):
                    t_base = t_p.n
                    if tb == 0:
                        sy.wait_ge(d_wrs.sem, W["wrs_done"])
                    for cc in range(8):
                        if cc < 2:
                            sy.wait_ge(t_p.sem, last_wpb[cc])
                        else:
                            sy.wait_ge(t_p.sem, t_base + 4 * (cc - 1))
                        d_wp[cc % 2].inc(sy.dma_start(out=wpb[cc % 2][:], in_=wp_s[cc]), 16)
                        W[f"wpld_{tb}_{cc}"] = d_wp[cc % 2].n

                @blk.tensor
                def _(te, tb=tb):
                    if f"cbready_{tb + 1}" in W:
                        te.wait_ge(v_p.sem, W[f"cbready_{tb + 1}"])
                    else:
                        te.wait_ge(c_p.sem, W[f"exps_done_{tb}"])
                        te.wait_ge(v_p.sem, W[f"ynorm_{tb}_7"])
                    pjs = (pA, pB, pD, pC)
                    for cc in range(8):
                        te.wait_ge(d_wp[cc % 2].sem, W[f"wpld_{tb}_{cc}"])
                        for i in range(4):
                            pj = pjs[i]
                            te.matmul(pj[:, 0:512],
                                      yT[:, cc * 512 + i * 128:cc * 512 + i * 128 + 128],
                                      wpb[cc % 2][:, 0:512], start=(cc == 0), stop=False)
                            t_p.inc(te.matmul(pj[:, 512:1024],
                                              yT[:, cc * 512 + i * 128:cc * 512 + i * 128 + 128],
                                              wpb[cc % 2][:, 512:1024], start=(cc == 0), stop=False))
                        last_wpb[cc % 2] = t_p.n
                    for i in range(4):
                        pj = pjs[i]
                        te.matmul(pj[:, 0:512], onesb[:], bpp[0:1, 0:512], start=False, stop=True)
                        t_p.inc(te.matmul(pj[:, 512:1024], onesb[:], bpp[0:1, 512:1024],
                                          start=False, stop=True))
                    W[f"pjmm_{tb}"] = t_p.n

                for i, pj in ((0, pA), (1, pB), (2, pD), (3, pC)):
                    @blk.vector
                    def _(ve, tb=tb, i=i, pj=pj):
                        ve.wait_ge(t_p.sem, W[f"pjmm_{tb}"])
                        ve.wait_ge(d_ss.sem, W.get(f"sst_{tb}_{i - 1}", 0))
                        v_p.inc(ve.tensor_copy(sev0[:], pj[:]))
                        W[f"pjev_{tb}_{i}"] = v_p.n

                    @blk.gpsimd
                    def _(gp, tb=tb, i=i):
                        gp.wait_ge(v_p.sem, W[f"pjev_{tb}_{i}"])
                        d_ss.inc(gp.dma_start(out=s_s[4 * tb + i], in_=sev0[:]), 16)
                        W[f"sst_{tb}_{i}"] = d_ss.n

                W[f"pjev_done_{tb}"] = v_p.n

            wconv_chunks = []
            for kind, n_tiles, nh in (("g", 8, 3), ("u", 8, 3), ("d", 24, 1)):
                for j in range(n_tiles):
                    for th in range(nh):
                        wconv_chunks.append((kind, j, th))
            NWH = 2 * len(wconv_chunks)

            def emit_wconv(sl):
                if sl < NWH:
                    kind, j, th = wconv_chunks[sl // 2]
                    src = {"g": wgT_h, "u": wuT_h, "d": wdT_h}[kind]
                    c0 = th * 1024 + (sl % 2) * 512

                    @blk.sync
                    def _(sy, sl=sl, src=src, j=j, c0=c0):
                        if sl >= 2:
                            sy.wait_ge(v_p.sem, W[f"wcv_{sl - 2}"])
                        d_wcl.inc(sy.dma_start(
                            out=wcb[sl % 2][:],
                            in_=src[j * 128:(j + 1) * 128, c0:c0 + 512]), 16)
                        W[f"wcl_{sl}"] = d_wcl.n

                if 0 <= sl - 1 < NWH:
                    @blk.vector
                    def _(ve, sl=sl - 1):
                        ve.wait_ge(d_wcl.sem, W[f"wcl_{sl}"])
                        if sl >= 1:
                            ve.wait_ge(d_wcs.sem, W[f"wcs_{sl - 1}"])
                        v_p.inc(ve.tensor_copy(wcvb[:], wcb[sl % 2][:]))
                        W[f"wcv_{sl}"] = v_p.n

                    @blk.gpsimd
                    def _(gp, sl=sl - 1):
                        gp.wait_ge(v_p.sem, W[f"wcv_{sl}"])
                        d_wcs.inc(gp.dma_start(
                            out=wbs_h[sl // 2, :, (sl % 2) * 512:(sl % 2) * 512 + 512],
                            in_=wcvb[:]), 16)
                        W[f"wcs_{sl}"] = d_wcs.n

            pre_c0 = c_p.n

            def c_rstd0(i, pre_c0=pre_c0):
                return pre_c0 + 2 * (i + 1)

            W["cbready_-1"] = W["initdone"]
            emit_ln_tile(0, 0, c_rstd0)
            emit_trans_tile(0, 0)
            emit_ln_tile(0, 1, c_rstd0)
            emit_trans_tile(0, 1)
            emit_ln_tile(0, 2, c_rstd0)
            emit_trans_tile(0, 2)
            emit_ln_tile(0, 3, c_rstd0)
            emit_trans_tile(0, 3)

            @blk.scalar
            def _(sc, pre_c0=pre_c0):
                for i in range(4):
                    L = 16 * i
                    sc.wait_ge(v_p.sem, W[f"var_{i}"])
                    c_p.inc(sc.activation(lnsc[:, L + 14:L + 15], lnsc[:, L + 13:L + 14],
                                          AF.Ln, bias=lnsc[:, 64:65]))
                    sc.wait_ge(c_p.sem, c_p.n)
                    c_p.inc(sc.activation(lnsc[:, L + 15:L + 16], lnsc[:, L + 14:L + 15],
                                          AF.Exp, scale=-0.5))
                    assert c_p.n == pre_c0 + 2 * (i + 1)

            @blk.gpsimd
            def _(gp):
                gp.wait_ge(v_p.sem, W["trev_3"])
                d_sx.inc(gp.dma_start(out=xnT_s[0], in_=xnT[:]), 16)
                W["xnst_0"] = d_sx.n

            emit_ln_load(1, 0)
            emit_ln_load(1, 1)
            for e in range(8, 16):
                emit_qk_chunk(0, e)

            for t in range(NT):
                nkc = 4 * t + 4
                has_next = t + 1 < NT
                R = 4 if has_next else 0
                c_base_t = c_p.n

                K0 = min(6, 4 * t + 2, nkc)

                def c_exp_done(c, kc, c_base_t=c_base_t, nkc=nkc, R=R, K0=K0):
                    return (c_base_t + 2 * min(c, R) + c * 2 * nkc + 2 * (kc + 1)
                            + (2 if (c < R and kc >= K0) else 0))

                def c_rstd_next(i, c_base_t=c_base_t, nkc=nkc, R=R, K0=K0):
                    return c_base_t + 2 * min(i, R) + i * 2 * nkc + 2 * K0 + 2

                emit_wconv(18 * t)
                tb = t - 1
                if t >= 1:
                    emit_recip(tb)
                emit_qk_chunk(t, 0)
                emit_qk_chunk(t, 1)
                if t >= 1:
                    emit_ynorm(tb, 0)
                    emit_ynorm(tb, 1)
                emit_qk_chunk(t, 2)
                emit_qk_chunk(t, 3)
                if t >= 1:
                    emit_ynorm(tb, 2)
                    emit_ynorm(tb, 3)
                emit_qk_chunk(t, 4)
                emit_qk_chunk(t, 5)
                if t >= 1:
                    emit_ynorm(tb, 4)
                    emit_ynorm(tb, 5)
                emit_qk_chunk(t, 6)
                emit_qk_chunk(t, 7)
                if t >= 1:
                    emit_ynorm(tb, 6)
                    emit_ynorm(tb, 7)
                emit_v_part_i(t, 0)
                emit_v_part_i(t, 1)
                emit_v_part_i(t, 2)
                emit_v_part_i(t, 3)
                emit_cb(t)
                if has_next:
                    emit_ln_stats(t + 1, 0)
                if t >= 1:
                    emit_proj(tb)
                if has_next:
                    emit_ln_apply(t + 1, 0, c_rstd_next)
                emit_wconv(18 * t + 1)

                for c in range(8):
                    emit_wconv(18 * t + 2 + 2 * c)
                    if has_next and 1 <= c <= 2:
                        emit_ln_load(t + 1, c + 1)
                    if t + 2 < NT and 6 <= c:
                        emit_ln_load(t + 2, c - 6)
                    if has_next and 1 <= c < 4:
                        emit_ln_calc(t + 1, c, c_rstd_next)
                    fillers = []
                    if has_next:
                        if c < 4:
                            fillers = trans_fillers(t + 1, c)
                        else:
                            for e in (8 + 2 * (c - 4), 9 + 2 * (c - 4)):
                                emit_qk_sync(t + 1, e)
                                fillers += qk_fillers(t + 1, e)
                    v_base = v_p.n

                    def v_mask_done(kc, v_base=v_base, t=t):
                        return v_base + (kc - 4 * t) + 1

                    def mk_mms(te, t=t, c=c, nkc=nkc, c_exp_done=c_exp_done,
                               v_mask_done=v_mask_done):
                        def mm1(kc):
                            b = kc % 2
                            q0 = 128 * (kc - 4 * t) if kc >= 4 * t else 0
                            if kc >= 2:
                                te.wait_ge(c_p.sem, c_exp_done(c, kc - 2))
                            else:
                                te.wait_ge(c_p.sem, lastS[("A", b)])
                                te.wait_ge(c_p.sem, lastS[("B", b)])
                                if c == 0:
                                    te.wait_ge(v_p.sem, W[f"cbready_{t}"])
                                    te.wait_ge(v_p.sem, W.get(f"pjev_{t - 1}_1", 0))
                            te.matmul(pA[:, b * 512 + q0:b * 512 + 512],
                                      kT[0:64, c * T + kc * 128:c * T + kc * 128 + 128],
                                      qy[0:64, c * 512 + q0:(c + 1) * 512],
                                      start=True, stop=True)
                            t_p.inc(te.matmul(pB[:, b * 512 + q0:b * 512 + 512],
                                              kT[64:128, c * T + kc * 128:c * T + kc * 128 + 128],
                                              qy[64:128, c * 512 + q0:(c + 1) * 512],
                                              start=True, stop=True))
                            W[f"mm1_{t}_{c}_{kc}"] = t_p.n

                        def mm2(kc):
                            b = kc % 2
                            diag = kc >= 4 * t
                            q0 = 128 * (kc - 4 * t) if diag else 0
                            te.wait_ge(c_p.sem, c_exp_done(c, kc))
                            if diag:
                                te.wait_ge(v_p.sem, v_mask_done(kc))
                            if kc == 0 and c == 0:
                                te.wait_ge(v_p.sem, W[f"vev_{4 * t + 3}"])
                                te.wait_ge(v_p.sem, W[f"pjev_done_{t - 1}"])
                            if kc == 0:
                                te.wait_ge(v_p.sem, W.get(f"yev_{t}_{c - 1}", W[f"yev_done_{t - 1}"]))
                            vbase = kc * NH * VST
                            last = (kc == nkc - 1)
                            te.matmul(pC[0:65, q0:512],
                                      vP[:, vbase + (2 * c) * VST:vbase + (2 * c) * VST + 65],
                                      pP[0 + b][:, q0:512], start=(kc == 0), stop=last)
                            t_p.inc(te.matmul(pC[0:65, 512 + q0:1024],
                                              vP[:, vbase + (2 * c + 1) * VST:vbase + (2 * c + 1) * VST + 65],
                                              pP[2 + b][:, q0:512], start=(kc == 0), stop=last))
                            W[f"mm2_{t}_{c}_{kc}"] = t_p.n
                        return mm1, mm2

                    fst = 5 if c < 4 else 3

                    @blk.tensor
                    def _(te, mk_mms=mk_mms, nkc=nkc, fillers=fillers, fst=fst):
                        mm1, mm2 = mk_mms(te)
                        fi = 0
                        for kc in range(nkc):
                            mm1(kc)
                            if 1 <= kc <= nkc - 2:
                                mm2(kc - 1)
                            while fi < len(fillers) and fi <= 2 * (kc - fst):
                                fillers[fi](te)
                                fi += 1
                        while fi < len(fillers):
                            fillers[fi](te)
                            fi += 1

                    assert v_p.n == v_base, (t, c, v_p.n, v_base)

                    @blk.tensor
                    def _(te, mk_mms=mk_mms, t=t, c=c, nkc=nkc):
                        mm1, mm2 = mk_mms(te)
                        mm2(nkc - 2)
                        mm2(nkc - 1)
                        W[f"mm2last_{t}_{c}"] = t_p.n

                    @blk.vector
                    def _(ve, t=t, c=c, nkc=nkc, c_exp_done=c_exp_done):
                        for kc in range(4 * t, nkc):
                            b = kc % 2
                            q0 = 128 * (kc - 4 * t)
                            ve.wait_ge(c_p.sem, c_exp_done(c, kc))
                            ve.tensor_mul(pP[0 + b][:, q0:q0 + 128], pP[0 + b][:, q0:q0 + 128], mask[:])
                            v_p.inc(ve.tensor_mul(pP[2 + b][:, q0:q0 + 128],
                                                  pP[2 + b][:, q0:q0 + 128], mask[:]))
                        lrowA = sev0[0:1, :].bitcast(F32)
                        lrowB = sev0[64:65, :].bitcast(F32)
                        ve.wait_ge(t_p.sem, W[f"mm2last_{t}_{c}"])
                        if c == 0:
                            ve.wait_ge(t_p.sem, W.get(f"pjmm_{t - 1}", 0))
                        ve.wait_ge(d_sl.sem, W.get(f"lst_{t}_{c - 1}", W.get(f"lst_{t - 1}_7", 0)))
                        if c == 0:
                            ve.wait_ge(d_ss.sem, W.get(f"sst_{t - 1}_3", 0))
                        ve.tensor_copy(yT[0:64, c * 512:(c + 1) * 512], pC[0:64, 0:512])
                        ve.tensor_copy(yT[64:128, c * 512:(c + 1) * 512], pC[0:64, 512:1024])
                        ve.tensor_copy(lrowA[0:1, 0:512], pC[64:65, 0:512])
                        v_p.inc(ve.tensor_copy(lrowB[0:1, 0:512], pC[64:65, 512:1024]))
                        W[f"yev_{t}_{c}"] = v_p.n

                    @blk.gpsimd
                    def _(gp, t=t, c=c):
                        lrowA = sev0[0:1, :].bitcast(F32)
                        lrowB = sev0[64:65, :].bitcast(F32)
                        gp.wait_ge(v_p.sem, W[f"yev_{t}_{c}"])
                        d_sl.inc(gp.dma_start(out=Lr[2 * c:2 * c + 1, :], in_=lrowA[0:1, 0:512]), 16)
                        d_sl.inc(gp.dma_start(out=Lr[2 * c + 1:2 * c + 2, :], in_=lrowB[0:1, 0:512]), 16)
                        W[f"lst_{t}_{c}"] = d_sl.n

                    if has_next:
                        if c < 4:
                            emit_trans_ev(t + 1, c)
                            if c == 3:
                                @blk.gpsimd
                                def _(gp, t=t):
                                    gp.wait_ge(v_p.sem, W[f"trev_{4 * (t + 1) + 3}"])
                                    d_sx.inc(gp.dma_start(out=xnT_s[t + 1], in_=xnT[:]), 16)
                                    W[f"xnst_{t + 1}"] = d_sx.n
                        else:
                            emit_qk_ev(t + 1, 8 + 2 * (c - 4))
                            emit_qk_ev(t + 1, 9 + 2 * (c - 4))

                    emit_wconv(18 * t + 3 + 2 * c)
                    lastS[("A", 0)] = lastS[("B", 0)] = c_exp_done(c, nkc - 2)
                    lastS[("A", 1)] = lastS[("B", 1)] = c_exp_done(c, nkc - 1)
                W[f"yev_done_{t}"] = v_p.n

                @blk.scalar
                def _(sc, t=t, nkc=nkc, R=R, c_base_t=c_base_t, K0=K0,
                      c_exp_done=c_exp_done, c_rstd_next=c_rstd_next):
                    for c in range(8):
                        for kc in range(nkc):
                            b = kc % 2
                            q0 = 128 * (kc - 4 * t) if kc >= 4 * t else 0
                            sc.wait_ge(t_p.sem, W[f"mm1_{t}_{c}_{kc}"])
                            if kc >= 2:
                                sc.wait_ge(t_p.sem, W[f"mm2_{t}_{c}_{kc - 2}"])
                            c_p.inc(sc.activation(pP[0 + b][:, q0:512], pA[:, b * 512 + q0:b * 512 + 512],
                                                  AF.Exp, bias=cbt[:, c:c + 1], scale=0.125))
                            c_p.inc(sc.activation(pP[2 + b][:, q0:512], pB[:, b * 512 + q0:b * 512 + 512],
                                                  AF.Exp, bias=cbt[:, c:c + 1], scale=0.125))
                            if c < R and kc == K0 - 1:
                                i = c
                                L = 16 * i
                                sc.wait_ge(v_p.sem, W[f"var_{4 * (t + 1) + i}"])
                                c_p.inc(sc.activation(lnsc[:, L + 14:L + 15], lnsc[:, L + 13:L + 14],
                                                      AF.Ln, bias=lnsc[:, 64:65]))
                                sc.wait_ge(c_p.sem, c_p.n)
                                c_p.inc(sc.activation(lnsc[:, L + 15:L + 16], lnsc[:, L + 14:L + 15],
                                                      AF.Exp, scale=-0.5))
                                assert c_p.n == c_rstd_next(i)
                            assert c_p.n == c_exp_done(c, kc) + (
                                2 if (c < R and kc == K0 - 1) else 0)

                W[f"exps_done_{t}"] = c_p.n

            emit_wconv(NWH)
            emit_recip(NT - 1)
            for yc in range(8):
                emit_ynorm(NT - 1, yc)
            emit_proj(NT - 1)

            AB_T, AB_V, AB_C = t_p.n, v_p.n, c_p.n
            AB_SX, AB_SS, AB_SL = d_sx.n, d_ss.n, d_sl.n
            AB_WCS = d_wcs.n

        cs = contextlib.ExitStack()
        with cs:
            def sbc(name, shape, dt):
                return cs.enter_context(nc.sbuf_tensor(name, shape, dt))

            wg = [sbc(f"wg{d}", [128, HID], BF16) for d in range(8)]
            wu = [sbc(f"wu{d}", [128, HID], BF16) for d in range(8)]
            wd = [sbc(f"wd{h}", [128, D], BF16) for h in range(24)]
            xnC = sbc("xnC", [128, T], BF16)
            hT = [sbc(f"hT{h}", [128, 512], BF16) for h in range(24)]
            xc = [sbc(f"xc{i}", [128, D], F32) for i in range(2)]
            scb_ = [sbc(f"scb{i}", [128, D], BF16) for i in range(2)]
            gs = [sbc("gs0", [128, D], F32)] * 2
            sgb = [sbc("sgb0", [128, 512], F32)] * 2

            pG = [cs.enter_context(nc.psum_tensor(f"pG{i}", [128, 512], F32)) for i in range(2)]
            pU = [cs.enter_context(nc.psum_tensor(f"pU{i}", [128, 512], F32)) for i in range(2)]
            pDn = [cs.enter_context(nc.psum_tensor(f"pDn{i}", [128, 1024], F32)) for i in range(2)]

            def cbar(eng):
                eng.wait_ge(t_p.sem, AB_T)
                eng.wait_ge(v_p.sem, AB_V)
                eng.wait_ge(c_p.sem, AB_C)
                eng.wait_ge(d_sx.sem, AB_SX)
                eng.wait_ge(d_ss.sem, AB_SS)
                eng.wait_ge(d_sl.sem, AB_SL)

            @blk.sync
            def _(sy):
                cbar(sy)
                sy.wait_ge(d_wcs.sem, AB_WCS)
                for dk in range(8):
                    d_cw.inc(sy.dma_start(
                        out=wg[dk][:].rearrange("p (c x) -> p c x", c=3),
                        in_=wbs_h[3 * dk:3 * dk + 3].rearrange("c p x -> p c x")), 16)
                W["wg_ld"] = d_cw.n
                for dk in range(8):
                    d_cw.inc(sy.dma_start(
                        out=wu[dk][:].rearrange("p (c x) -> p c x", c=3),
                        in_=wbs_h[24 + 3 * dk:24 + 3 * dk + 3].rearrange("c p x -> p c x")), 16)
                W["wu_ld"] = d_cw.n
                for h in range(24):
                    d_cw.inc(sy.dma_start(out=wd[h][:], in_=wbs_h[48 + h]), 16)
                W["wd_ld"] = d_cw.n

            dn_last = {}
            for tc in range(NT):
                c_tc0 = c_p.n
                v_tc0 = v_p.n

                def c_sil(hc, c_tc0=c_tc0):
                    return c_tc0 + hc + 1

                def c_gsil(i, c_tc0=c_tc0):
                    return c_tc0 + 24 + i + 1

                def v_hm(hc, v_tc0=v_tc0):
                    return v_tc0 + hc + 1

                def v_fin(i, v_tc0=v_tc0):
                    return v_tc0 + 24 + i + 1

                dn_prev = dict(dn_last)
                so_tc0 = [d_so[0].n, d_so[1].n]

                @blk.sync
                def _(sy, tc=tc):
                    if tc == 0:
                        cbar(sy)
                    else:
                        sy.wait_ge(t_p.sem, W[f"gumm_{tc - 1}_23"])
                    d_cx.inc(sy.dma_start(out=xnC[:], in_=xnT_s[tc]), 16)
                    W[f"xnld_{tc}"] = d_cx.n

                for i in range(4):
                    ti = 4 * tc + i

                    @blk.sync
                    def _(sy, tc=tc, i=i, ti=ti, v_fin=v_fin):
                        if ti >= 2:
                            if i < 2:
                                sy.wait_ge(v_p.sem, W[f"fin_{ti - 2}"])
                            else:
                                sy.wait_ge(v_p.sem, v_fin(i - 2))
                        d_cs[i % 2].inc(sy.dma_start(out=xc[i % 2][:], in_=x_h[ti * 128:(ti + 1) * 128, :]), 16)
                        d_cs[i % 2].inc(sy.dma_start(out=scb_[i % 2][:], in_=s_s[ti]), 16)
                        W[f"xsld_{ti}"] = d_cs[i % 2].n

                for hc in range(24):
                    @blk.tensor
                    def _(te, tc=tc, hc=hc, c_sil=c_sil, v_hm=v_hm):
                        if tc == 0 and hc == 0:
                            cbar(te)
                            te.wait_ge(d_cw.sem, W["wg_ld"])
                        te.wait_ge(d_cx.sem, W[f"xnld_{tc}"])
                        if hc >= 2:
                            te.wait_ge(c_p.sem, c_sil(hc - 2))
                            te.wait_ge(v_p.sem, v_hm(hc - 2))
                        elif tc >= 1:
                            te.wait_ge(c_p.sem, W[f"sil_{tc - 1}_{22 + (hc % 2)}"])
                            te.wait_ge(v_p.sem, W[f"hm_{tc - 1}_{22 + (hc % 2)}"])
                        for dk in range(8):
                            te.matmul(pG[hc % 2][:], wg[dk][:, hc * 128:(hc + 1) * 128],
                                      xnC[:, dk * 512:(dk + 1) * 512],
                                      start=(dk == 0), stop=(dk == 7))
                        if tc == 0 and hc == 0:
                            te.wait_ge(d_cw.sem, W["wu_ld"])
                        for dk in range(8):
                            ins = te.matmul(pU[hc % 2][:], wu[dk][:, hc * 128:(hc + 1) * 128],
                                            xnC[:, dk * 512:(dk + 1) * 512],
                                            start=(dk == 0), stop=(dk == 7))
                        t_p.inc(ins)
                        W[f"gumm_{tc}_{hc}"] = t_p.n

                for i in range(4):
                    ti = 4 * tc + i

                    @blk.tensor
                    def _(te, tc=tc, i=i, ti=ti, v_hm=v_hm, c_gsil=c_gsil):
                        if tc == 0 and i == 0:
                            te.wait_ge(d_cw.sem, W["wd_ld"])
                        if i >= 2:
                            te.wait_ge(c_p.sem, c_gsil(i - 2))
                        elif tc >= 1:
                            te.wait_ge(c_p.sem, W[f"gsil_{4 * tc + i - 2}"])
                        for hc in range(24):
                            te.wait_ge(v_p.sem, v_hm(hc))
                            te.matmul(pDn[i % 2][:, 0:512], hT[hc][:, i * 128:(i + 1) * 128],
                                      wd[hc][:, 0:512], start=(hc == 0), stop=(hc == 23))
                            t_p.inc(te.matmul(pDn[i % 2][:, 512:1024], hT[hc][:, i * 128:(i + 1) * 128],
                                              wd[hc][:, 512:1024], start=(hc == 0), stop=(hc == 23)))
                            if i == 3:
                                dn_last[hc] = t_p.n
                        W[f"dnmm_{ti}"] = t_p.n

                @blk.scalar
                def _(sc, tc=tc, c_tc0=c_tc0, v_hm=v_hm, v_tc0=v_tc0, so_tc0=so_tc0):
                    if tc == 0:
                        cbar(sc)
                    for hc in range(24):
                        sc.wait_ge(t_p.sem, W[f"gumm_{tc}_{hc}"])
                        if hc >= 1:
                            sc.wait_ge(v_p.sem, v_hm(hc - 1))
                        elif tc >= 1:
                            sc.wait_ge(v_p.sem, W[f"hm_{tc - 1}_23"])
                        c_p.inc(sc.activation(sgb[hc % 2][:], pG[hc % 2][:], AF.Silu, scale=INV127))
                        W[f"sil_{tc}_{hc}"] = c_p.n
                    for i in range(4):
                        ti = 4 * tc + i
                        sc.wait_ge(t_p.sem, W[f"dnmm_{ti}"])
                        if i >= 1:
                            sc.wait_ge(v_p.sem, v_tc0 + 24 + (i - 1) + 1)
                            sc.wait_ge(d_so[(i - 1) % 2].sem, so_tc0[(i - 1) % 2] + 16 * ((i - 1) // 2 + 1))
                        elif ti >= 1:
                            sc.wait_ge(v_p.sem, W[f"fin_{ti - 1}"])
                            sc.wait_ge(d_so[(i - 1) % 2].sem, W.get(f"outst_{ti - 1}", 0))
                        c_p.inc(sc.activation(gs[i % 2][:], pDn[i % 2][:], AF.Silu,
                                              scale=INV127 * INV127))
                        W[f"gsil_{ti}"] = c_p.n

                @blk.vector
                def _(ve, tc=tc, c_sil=c_sil, c_gsil=c_gsil, dn_prev=dn_prev, so_tc0=so_tc0):
                    if tc == 0:
                        cbar(ve)
                    for hc in range(24):
                        ve.wait_ge(c_p.sem, c_sil(hc))
                        ve.wait_ge(t_p.sem, dn_prev.get(hc, 0))
                        v_p.inc(ve.tensor_mul(hT[hc][:], sgb[hc % 2][:], pU[hc % 2][:]))
                        W[f"hm_{tc}_{hc}"] = v_p.n
                    for i in range(4):
                        ti = 4 * tc + i
                        ve.wait_ge(c_p.sem, c_gsil(i))
                        ve.wait_ge(d_cs[i % 2].sem, W[f"xsld_{ti}"])
                        if i >= 2:
                            ve.wait_ge(d_so[i % 2].sem, so_tc0[i % 2] + 16)
                        else:
                            ve.wait_ge(d_so[i % 2].sem, W.get(f"outst_{ti - 2}", 0))
                        ve.tensor_mul(gs[i % 2][:], gs[i % 2][:], scb_[i % 2][:])
                        v_p.inc(ve.tensor_add(gs[i % 2][:], gs[i % 2][:], xc[i % 2][:]))
                        W[f"fin_{ti}"] = v_p.n

                for i in range(4):
                    ti = 4 * tc + i

                    @blk.gpsimd
                    def _(gp, ti=ti, i=i):
                        if ti == 0:
                            cbar(gp)
                        gp.wait_ge(v_p.sem, W[f"fin_{ti}"])
                        d_so[i % 2].inc(gp.dma_start(out=out_h[ti * 128:(ti + 1) * 128, :], in_=gs[i % 2][:]), 16)
                        W[f"outst_{ti}"] = d_so[i % 2].n

            @blk.gpsimd
            def _(gp):
                gp.wait_ge(d_so[0].sem, d_so[0].n)
                gp.wait_ge(d_so[1].sem, d_so[1].n)

    return nc


def _host_consts():
    hd = 64
    theta = 1.0 / (10000.0 ** (np.arange(0, hd, 2, dtype=np.float64) / hd))
    h_idx = np.arange(NH, dtype=np.float64)
    ang = h_idx[:, None] * theta[None, :]
    cos_t = np.cos(ang)
    sin_t = np.sin(ang)
    row = cos_t.reshape(-1)
    cosrow = np.concatenate([row, row]).reshape(1, D).astype(ml_dtypes.bfloat16)
    row = sin_t.reshape(-1)
    sinrow = np.concatenate([row, row]).reshape(1, D).astype(ml_dtypes.bfloat16)
    ident = np.zeros((40, 8), dtype=np.float32)
    ident[0:8, 0:8] = np.eye(8)
    ident[32:40, 0:8] = np.eye(8)
    identb = np.eye(128, dtype=ml_dtypes.bfloat16)
    onesb = np.ones((1, 128), dtype=ml_dtypes.bfloat16)
    onescol = np.ones((128, 1), dtype=ml_dtypes.bfloat16)
    kk = np.arange(128)[:, None]
    qq = np.arange(128)[None, :]
    masktri = (qq >= kk).astype(ml_dtypes.bfloat16)
    sel16 = np.zeros((16, NH * 64), dtype=ml_dtypes.bfloat16)
    for c in range(8):
        for m in range(128):
            sel16[2 * c + (m >= 64), c * 128 + m] = 1
    return cosrow, sinrow, ident, identb, onesb, onescol, masktri, sel16


def kernel(L0_attn_x, L0_attn_W_attn, L0_attn_b_attn, L0_attn_W_proj,
           L0_attn_b_proj, L0_ffn_w_gate, L0_ffn_w_up, L0_ffn_w_down):
    x = np.ascontiguousarray(np.asarray(L0_attn_x, dtype=np.float32))
    wat = np.ascontiguousarray(np.asarray(L0_attn_W_attn, dtype=np.float32))
    ba = np.ascontiguousarray(np.asarray(L0_attn_b_attn, dtype=np.float32)).reshape(1, 3 * D)
    wp = np.ascontiguousarray(np.asarray(L0_attn_W_proj, dtype=np.float32))
    bp = np.ascontiguousarray(np.asarray(L0_attn_b_proj, dtype=np.float32)).reshape(1, D)
    wgT = np.ascontiguousarray(np.asarray(L0_ffn_w_gate, dtype=np.int32).T)
    wuT = np.ascontiguousarray(np.asarray(L0_ffn_w_up, dtype=np.int32).T)
    wdT = np.ascontiguousarray(np.asarray(L0_ffn_w_down, dtype=np.int32).T)
    cosrow, sinrow, ident, identb, onesb, onescol, masktri, sel16 = _host_consts()

    nc = build_program()
    shared = dict(w_attn=wat, b_attn=ba, w_proj=wp, b_proj=bp,
                  wg_t=wgT, wu_t=wuT, wd_t=wdT,
                  cosrow=cosrow, sinrow=sinrow, ident=ident, identb=identb,
                  onesb=onesb, onescol=onescol, masktri=masktri, sel16=sel16)
    in_maps = [dict(x=np.ascontiguousarray(x[b]), **shared) for b in range(8)]
    res = run_bass_kernel_spmd(nc, in_maps, core_ids=list(range(8)))
    out = np.stack([res.results[b]["out"] for b in range(8)], axis=0)
    return out.astype(np.float32)


# version 22
# speedup vs baseline: 1.0681x; 1.0532x over previous
import contextlib
import math
import numpy as np
import ml_dtypes
import concourse.bass as bass
import concourse.mybir as mybir
from concourse.bass_utils import run_bass_kernel_spmd

F32 = mybir.dt.float32
BF16 = mybir.dt.bfloat16
I32 = mybir.dt.int32
AF = mybir.ActivationFunctionType
AX = mybir.AxisListType
OP = mybir.AluOpType

T = 4096
D = 1024
NH = 16
NT = 8
HID = 3072
VST = 65
INV127 = 1.0 / 127.0


class Ctr:
    def __init__(self, ctx, nc, name):
        self.sem = ctx.enter_context(nc.semaphore(name))
        self.n = 0

    def inc(self, instr, k=1):
        self.n += k
        instr.then_inc(self.sem, k)
        return instr


def build_program(dbg=False):
    nc = bass.Bass("TRN2", target_bir_lowering=False, debug=False)

    x_h = nc.dram_tensor("x", [T, D], F32, kind="ExternalInput").ap()
    wat_h = nc.dram_tensor("w_attn", [D, 3 * D], F32, kind="ExternalInput").ap()
    ba_h = nc.dram_tensor("b_attn", [1, 3 * D], F32, kind="ExternalInput").ap()
    wp_h = nc.dram_tensor("w_proj", [D, D], F32, kind="ExternalInput").ap()
    bp_h = nc.dram_tensor("b_proj", [1, D], F32, kind="ExternalInput").ap()
    wgT_h = nc.dram_tensor("wg_t", [D, HID], I32, kind="ExternalInput").ap()
    wuT_h = nc.dram_tensor("wu_t", [D, HID], I32, kind="ExternalInput").ap()
    wdT_h = nc.dram_tensor("wd_t", [HID, D], I32, kind="ExternalInput").ap()
    cosr_h = nc.dram_tensor("cosrow", [1, D], BF16, kind="ExternalInput").ap()
    sinr_h = nc.dram_tensor("sinrow", [1, D], BF16, kind="ExternalInput").ap()
    ident_h = nc.dram_tensor("ident", [40, 8], F32, kind="ExternalInput").ap()
    identb_h = nc.dram_tensor("identb", [128, 128], BF16, kind="ExternalInput").ap()
    onesb_h = nc.dram_tensor("onesb", [1, 128], BF16, kind="ExternalInput").ap()
    onescol_h = nc.dram_tensor("onescol", [128, 1], BF16, kind="ExternalInput").ap()
    mask_h = nc.dram_tensor("masktri", [128, 128], BF16, kind="ExternalInput").ap()
    sel16_h = nc.dram_tensor("sel16", [16, NH * 64], BF16, kind="ExternalInput").ap()

    out_h = nc.dram_tensor("out", [T, D], F32, kind="ExternalOutput").ap()

    KI = "ExternalOutput" if dbg else "Internal"
    wrs_h = nc.dram_tensor("w_rope_s", [16, 8, 128, 128], BF16, kind=KI).ap()
    wp_s = nc.dram_tensor("wp_s", [8, 128, D], BF16, kind=KI).ap()
    wbs_h = nc.dram_tensor("wbs", [72, 128, 1024], BF16, kind=KI).ap()
    xnT_s = nc.dram_tensor("xnT_s", [NT, 128, T], BF16, kind=KI).ap()
    s_s = nc.dram_tensor("s_s", [4 * NT, 128, D], BF16, kind=KI).ap()

    ctx = contextlib.ExitStack()
    with ctx:
        def sb(name, shape, dt):
            return ctx.enter_context(nc.sbuf_tensor(name, shape, dt))

        ident = sb("s_ident", [40, 8], F32)
        onesb = sb("s_onesb", [1, 128], BF16)
        onescol = sb("s_onescol", [128, 1], BF16)
        mask = sb("s_mask", [128, 128], BF16)
        bT = sb("bT", [128, 24], F32)
        bTb = sb("bTb", [128, 24], BF16)
        bpp = sb("bpp", [1, D], BF16)
        qmaxT = sb("qmaxT", [128, 16], BF16)
        gtmp = sb("gtmp", [128, 1], BF16)
        cbt = sb("cbt", [128, 8], F32)
        lnsc = sb("lnsc", [128, 65], F32)
        identb = sb("s_identb", [128, 128], BF16)
        c8 = sb("c8", [8, 1], F32)
        crowb = sb("crowb", [1, 8], BF16)
        prodall = sb("prodall", [128, 8], BF16)
        sel16 = sb("s_sel16", [16, NH * 64], BF16)

        d_ms = Ctr(ctx, nc, "d_ms")
        d_w4 = Ctr(ctx, nc, "d_w4")
        d_x = [Ctr(ctx, nc, "d_x0"), Ctr(ctx, nc, "d_x1")]
        d_we = [Ctr(ctx, nc, "d_we0"), Ctr(ctx, nc, "d_we1")]
        d_wp = [Ctr(ctx, nc, f"d_wp{i}") for i in range(4)]
        d_wcl = [Ctr(ctx, nc, f"d_wcl{i}") for i in range(2)]
        d_wcs = [Ctr(ctx, nc, f"d_wcs{i}") for i in range(2)]
        d_swp = [Ctr(ctx, nc, "d_swp0"), Ctr(ctx, nc, "d_swp1")]
        d_wrs = Ctr(ctx, nc, "d_wrs")
        d_sx = Ctr(ctx, nc, "d_sx")
        d_ss = Ctr(ctx, nc, "d_ss")
        d_sl = Ctr(ctx, nc, "d_sl")
        d_so = [Ctr(ctx, nc, "d_so0"), Ctr(ctx, nc, "d_so1")]
        d_cw = Ctr(ctx, nc, "d_cw")
        d_cx = Ctr(ctx, nc, "d_cx")
        d_cs = [Ctr(ctx, nc, "d_cs0"), Ctr(ctx, nc, "d_cs1")]
        t_p = Ctr(ctx, nc, "t_p")
        v_p = Ctr(ctx, nc, "v_p")
        c_p = Ctr(ctx, nc, "c_p")
        g_p = Ctr(ctx, nc, "g_p")

        blk = ctx.enter_context(nc.Block())
        W = {}

        ab = contextlib.ExitStack()
        with ab:
            def sba(name, shape, dt):
                return ab.enter_context(nc.sbuf_tensor(name, shape, dt))

            kT = sba("kT", [128, 8 * T], BF16)
            vP = sba("vP", [128, 32 * NH * VST], BF16)
            wv = sba("wv", [128, 8 * 1024], BF16)
            we = [sba(f"we{i}", [128, 1024], BF16) for i in range(2)]
            wpb = [sba(f"wpb{i}", [128, 1024], BF16) for i in range(2)]
            pP = [sba(f"pP{i}", [128, 512], BF16) for i in range(4)]
            xb = [sba(f"xb{i}", [128, D], F32) for i in range(2)]
            Lr = sba("Lr", [16, 512], F32)

            pA = ab.enter_context(nc.psum_tensor("pA", [128, 1024], F32))
            pB = ab.enter_context(nc.psum_tensor("pB", [128, 1024], F32))
            pC = ab.enter_context(nc.psum_tensor("pC", [128, 1024], F32))
            pD = ab.enter_context(nc.psum_tensor("pD", [128, 1024], F32))

            pro = contextlib.ExitStack()
            with pro:
                csb = pro.enter_context(nc.sbuf_tensor("csb", [128, 1024], BF16))

                @blk.sync
                def _(sy):
                    for dst, src in ((ident, ident_h), (identb, identb_h),
                                     (onesb, onesb_h),
                                     (onescol, onescol_h), (mask, mask_h),
                                     (sel16, sel16_h)):
                        d_ms.inc(sy.dma_start(out=dst[:], in_=src), 16)
                    d_ms.inc(sy.dma_start(out=csb[0:1, 0:512], in_=cosr_h[0:1, 0:512]), 16)
                    d_ms.inc(sy.dma_start(out=csb[0:1, 512:1024], in_=sinr_h[0:1, 0:512]), 16)
                    d_ms.inc(sy.dma_start(out=xb[0][0:1, :], in_=bp_h), 16)
                    d_ms.inc(sy.dma_start(out=kT[0:1, 0:2048].bitcast(F32), in_=ba_h[0:1, 0:1024]), 16)
                    d_ms.inc(sy.dma_start(out=kT[0:1, 2048:4096].bitcast(F32), in_=ba_h[0:1, 1024:2048]), 16)
                    d_ms.inc(sy.dma_start(out=xb[1][0:1, :], in_=ba_h[0:1, 2048:3072]), 16)
                    W["const_all"] = d_ms.n

                v0_cs = v_p.n

                @blk.tensor
                def _(te):
                    te.wait_ge(d_ms.sem, W["const_all"])
                    for j in range(2):
                        t_p.inc(te.matmul(pA[:, j * 512:(j + 1) * 512],
                                          onesb[0:1, :], csb[0:1, j * 512:(j + 1) * 512],
                                          start=True, stop=True))
                        W[f"csmm{j}"] = t_p.n

                @blk.vector
                def _(ve):
                    for j in range(2):
                        ve.wait_ge(t_p.sem, W[f"csmm{j}"])
                        v_p.inc(ve.tensor_copy(csb[:, j * 512:(j + 1) * 512],
                                               pA[:, j * 512:(j + 1) * 512]))
                    W["csb"] = v_p.n
                    ve.wait_ge(d_ms.sem, W["const_all"])
                    for third in (0, 1):
                        row = kT[0:1, third * 2048:(third + 1) * 2048].bitcast(F32)
                        bev = row[0:1, 0:1024:2]
                        bod = row[0:1, 1:1024:2]
                        cev = csb[0:1, 0:512]
                        snv = csb[0:1, 512:1024]
                        pa_ = pD[0:1, 0:512]
                        pb_ = pD[0:1, 512:1024]
                        sb_ = Lr[0:1, :]
                        ve.tensor_mul(pb_, bev, snv)
                        ve.tensor_mul(sb_, bod, snv)
                        v_p.inc(ve.tensor_mul(pa_, bev, cev))
                        ve.wait_ge(v_p.sem, v_p.n)
                        v_p.inc(ve.tensor_tensor(bev, pa_, sb_, OP.subtract))
                        ve.wait_ge(v_p.sem, v_p.n)
                        v_p.inc(ve.tensor_mul(sb_, bod, cev))
                        ve.wait_ge(v_p.sem, v_p.n)
                        ve.tensor_tensor(bod, pb_, sb_, OP.add)
                    v_p.inc(ve.engine_nop())
                    W["bfold"] = v_p.n

                @blk.tensor
                def _(te):
                    te.wait_ge(v_p.sem, W["bfold"])
                    for c in range(24):
                        src = (kT[0:1, 0:2048].bitcast(F32),
                               kT[0:1, 2048:4096].bitcast(F32), xb[1][0:1, :])[c // 8]
                        ins = te.transpose(pC[:, c:c + 1],
                                           src[0:1, (c % 8) * 128:(c % 8) * 128 + 128],
                                           ident[0:1, 0:1])
                    t_p.inc(ins)
                    W["btr"] = t_p.n

                @blk.vector
                def _(ve):
                    ve.wait_ge(t_p.sem, W["btr"])
                    v_p.inc(ve.tensor_copy(bT[:], pC[:, 0:24]))
                    ve.wait_ge(v_p.sem, v_p.n)
                    ve.tensor_copy(bTb[:], bT[:])
                    ve.memset(lnsc[:, 64:65], 1e-5)
                    ve.tensor_copy(pP[0][0:1, :], xb[0][0:1, 0:512])
                    v_p.inc(ve.tensor_copy(pP[1][0:1, :], xb[0][0:1, 512:1024]))
                    W["bT_bp"] = v_p.n

                wst4 = xb[1]
                if True:
                    v_wp0 = v_p.n

                    @blk.sync
                    def _(sy):
                        sy.wait_ge(t_p.sem, W["btr"])
                        for c in range(8):
                            if c >= 1:
                                sy.wait_ge(v_p.sem, v_wp0 + c)
                            d_w4.inc(sy.dma_start(out=wst4[:],
                                                  in_=wp_h[c * 128:(c + 1) * 128, :]), 16)
                            W[f"wpld{c}"] = d_w4.n

                    st_wp0 = [d_swp[0].n, d_swp[1].n]

                    @blk.vector
                    def _(ve):
                        for c in range(8):
                            ve.wait_ge(d_w4.sem, W[f"wpld{c}"])
                            if c >= 2:
                                ve.wait_ge(d_swp[c % 2].sem, st_wp0[c % 2] + 16 * (c // 2))
                            v_p.inc(ve.tensor_copy(we[c % 2][:], wst4[:]))
                            W[f"wpcv{c}"] = v_p.n

                    @blk.gpsimd
                    def _(gp):
                        for c in range(8):
                            gp.wait_ge(v_p.sem, W[f"wpcv{c}"])
                            d_swp[c % 2].inc(gp.dma_start(out=wp_s[c], in_=we[c % 2][:]), 16)
                            W[f"wpst{c}"] = d_swp[c % 2].n
                        W["wp_done0"] = d_swp[0].n
                        W["wp_done1"] = d_swp[1].n

                    t_bpp0 = t_p.n

                    @blk.sync
                    def _(sy):
                        sy.wait_ge(d_swp[0].sem, W["wp_done0"])
                        sy.wait_ge(d_swp[1].sem, W["wp_done1"])
                        for c in range(8):
                            if c >= 2:
                                sy.wait_ge(t_p.sem, t_bpp0 + c - 1)
                            d_wp[c % 2].inc(sy.dma_start(out=wpb[c % 2][:], in_=wp_s[c]), 16)
                            W[f"bppld{c}"] = d_wp[c % 2].n

                    @blk.tensor
                    def _(te):
                        te.wait_ge(v_p.sem, W["bT_bp"])
                        for c in range(8):
                            te.wait_ge(d_wp[c % 2].sem, W[f"bppld{c}"])
                            te.matmul(pD[0:1, 0:512], bTb[:, 16 + c:17 + c], wpb[c % 2][:, 0:512],
                                      start=(c == 0), stop=False)
                            t_p.inc(te.matmul(pD[0:1, 512:1024], bTb[:, 16 + c:17 + c],
                                              wpb[c % 2][:, 512:1024], start=(c == 0), stop=False))
                            W[f"bppmm{c}"] = t_p.n
                        te.matmul(pD[0:1, 0:512], onesb[0:1, 0:1], pP[0][0:1, :], start=False, stop=True)
                        t_p.inc(te.matmul(pD[0:1, 512:1024], onesb[0:1, 0:1], pP[1][0:1, :],
                                          start=False, stop=True))
                        W["bppdone"] = t_p.n

                    @blk.vector
                    def _(ve):
                        ve.wait_ge(t_p.sem, W["bppdone"])
                        ve.tensor_copy(bpp[:], pD[0:1, :])
                        ve.memset(qmaxT[:], 0.0)
                        v_p.inc(ve.memset(vP[:, 64::VST], 1.0))
                        W["initdone"] = v_p.n

                    v_wf0 = v_p.n

                    @blk.sync
                    def _(sy):
                        for dk in range(8):
                            for th in range(3):
                                if dk == 0 and th == 0:
                                    sy.wait_ge(t_p.sem, W["bppdone"])
                                else:
                                    sy.wait_ge(v_p.sem, v_wf0 + 3 * dk + th)
                                d_w4.inc(sy.dma_start(
                                    out=wst4[:],
                                    in_=wat_h[dk * 128:(dk + 1) * 128, th * 1024:(th + 1) * 1024]), 16)
                                W[f"watld_{dk}_{th}"] = d_w4.n

                    st_wf0 = d_wrs.n

                    @blk.vector
                    def _(ve):
                        for dk in range(8):
                            for th in range(3):
                                ve.wait_ge(d_w4.sem, W[f"watld_{dk}_{th}"])
                                if th < 2:
                                    if dk >= 1:
                                        ve.wait_ge(d_wrs.sem, st_wf0 + 32 * dk)
                                    dst = wpb[th]
                                    cev = csb[:, 0:512]
                                    snv = csb[:, 512:1024]
                                    ve.tensor_mul(xb[0][:, 0:512], wst4[:, 0::2], cev)
                                    ve.tensor_mul(pD[:, 0:512], wst4[:, 1::2], snv)
                                    ve.tensor_tensor(dst[:, 0::2], xb[0][:, 0:512],
                                                     pD[:, 0:512], OP.subtract)
                                    ve.tensor_mul(xb[0][:, 0:512], wst4[:, 1::2], cev)
                                    ve.tensor_mul(pD[:, 0:512], wst4[:, 0::2], snv)
                                    v_p.inc(ve.tensor_tensor(dst[:, 1::2], xb[0][:, 0:512],
                                                             pD[:, 0:512], OP.add))
                                else:
                                    v_p.inc(ve.tensor_copy(wv[:, dk * 1024:(dk + 1) * 1024], wst4[:]))
                                W[f"wf_{dk}_{th}"] = v_p.n

                    @blk.gpsimd
                    def _(gp):
                        for dk in range(8):
                            gp.wait_ge(v_p.sem, W[f"wf_{dk}_1"])
                            d_wrs.inc(gp.dma_start(
                                out=wrs_h[0:8, dk].rearrange("e p c -> p e c"),
                                in_=wpb[0][:].rearrange("p (e c) -> p e c", e=8)), 16)
                            d_wrs.inc(gp.dma_start(
                                out=wrs_h[8:16, dk].rearrange("e p c -> p e c"),
                                in_=wpb[1][:].rearrange("p (e c) -> p e c", e=8)), 16)
                            W[f"wqkst{dk}"] = d_wrs.n
                        W["wrs_done"] = d_wrs.n
            qy = sba("qy", [128, 8 * 512], BF16)
            yT = sba("yT", [128, 8 * 512], BF16)
            sev0 = sba("sev0", [128, D], BF16)
            xnb = [sba("xnb0", [128, D], BF16)] * 2
            xnT = sba("xnT", [128, 8 * 512], BF16)
            Rb = sba("Rb", [16, 512], BF16)
            wcb = [sba(f"wcb{i}", [128, 512], I32) for i in range(2)]
            wcvb = [sba(f"wcvb{i}", [128, 512], BF16) for i in range(2)]

            lastS = {("A", 0): 0, ("A", 1): 0, ("B", 0): 0, ("B", 1): 0}
            W["pj_done_-1"] = 0
            W["pjev_done_-1"] = W["initdone"]
            W["yev_done_-1"] = W["bT_bp"]
            last_wpb = [W["bppmm6"], W["bppmm7"]]

            def emit_ln_load(t, i):
                ti = 4 * t + i

                @blk.sync
                def _(sy, t=t, i=i, ti=ti):
                    if t == 0 and i <= 1:
                        sy.wait_ge(d_wrs.sem, W["wrs_done"])
                        sy.wait_ge(v_p.sem, W["bT_bp"])
                    if ti >= 2:
                        sy.wait_ge(v_p.sem, W[f"xnapply_{ti - 2}"])
                    d_x[i % 2].inc(sy.dma_start(out=xb[i % 2][:], in_=x_h[ti * 128:(ti + 1) * 128, :]), 16)
                    W[f"xld_{ti}"] = d_x[i % 2].n

            def emit_ln_stats(t, i):
                ti = 4 * t + i
                L = 16 * i

                @blk.vector
                def _(ve, t=t, i=i, ti=ti, L=L):
                    ve.wait_ge(d_x[i % 2].sem, W[f"xld_{ti}"])
                    ve.bn_stats(lnsc[:, L + 0:L + 6], xb[i % 2][:, 0:512])
                    v_p.inc(ve.bn_stats(lnsc[:, L + 6:L + 12], xb[i % 2][:, 512:1024]))
                    ve.wait_ge(v_p.sem, v_p.n)
                    v_p.inc(ve.bn_aggr(lnsc[:, L + 12:L + 14], lnsc[:, L + 0:L + 12]))
                    W[f"var_{ti}"] = v_p.n

            def emit_ln_apply(t, i, c_rstd_done):
                ti = 4 * t + i
                L = 16 * i

                @blk.vector
                def _(ve, t=t, i=i, ti=ti, L=L, c_rstd_done=c_rstd_done):
                    if ti >= 1:
                        ve.wait_ge(t_p.sem, W[f"trmm_{ti - 1}"])
                    ve.wait_ge(c_p.sem, c_rstd_done(i))
                    v_p.inc(ve.scalar_tensor_tensor(lnsc[:, L + 0:L + 1], lnsc[:, L + 12:L + 13], -1.0,
                                                    lnsc[:, L + 15:L + 16], OP.mult, OP.mult))
                    ve.wait_ge(v_p.sem, v_p.n)
                    v_p.inc(ve.tensor_scalar(xnb[i % 2][:], xb[i % 2][:],
                                             lnsc[:, L + 15:L + 16], lnsc[:, L + 0:L + 1],
                                             OP.mult, OP.add))
                    W[f"xnapply_{ti}"] = v_p.n

            def emit_ln_calc(t, i, c_rstd_done):
                emit_ln_stats(t, i)
                emit_ln_apply(t, i, c_rstd_done)

            def emit_ln_tile(t, i, c_rstd_done):
                emit_ln_load(t, i)
                emit_ln_calc(t, i, c_rstd_done)

            def emit_trans_tile(t, i):
                ti = 4 * t + i

                @blk.tensor
                def _(te, t=t, i=i, ti=ti):
                    te.wait_ge(v_p.sem, W[f"xnapply_{ti}"])
                    if i == 0:
                        te.wait_ge(v_p.sem, W[f"cbready_{t - 1}"])
                        te.wait_ge(v_p.sem, W.get(f"pjev_{t - 2}_2", 0))
                    else:
                        te.wait_ge(v_p.sem, W[f"trev_{ti - 1}"])
                    for dk in range(8):
                        ins = te.transpose(pD[:, dk * 64:dk * 64 + 64].bitcast(BF16),
                                           xnb[i % 2][:, dk * 128:(dk + 1) * 128], identb[:])
                    t_p.inc(ins)
                    W[f"trmm_{ti}"] = t_p.n

                @blk.vector
                def _(ve, t=t, i=i, ti=ti):
                    ve.wait_ge(t_p.sem, W[f"trmm_{ti}"])
                    if i == 0:
                        ve.wait_ge(d_sx.sem, W.get(f"xnst_{t - 1}", 0))
                        ve.wait_ge(t_p.sem, W.get(f"vmm_{4 * (t - 1) + 3}", 0))
                    for dk in range(8):
                        ins = ve.tensor_copy(xnT[:, dk * 512 + i * 128:dk * 512 + i * 128 + 128],
                                             pD[:, dk * 64:dk * 64 + 64].bitcast(BF16))
                    v_p.inc(ins)
                    W[f"trev_{ti}"] = v_p.n

            def emit_qk_chunk(t, e):
                if e >= 10:
                    prev_mm = f"qkmm_{t}_{e - 2}"
                    prev_ev = f"qkev_{t}_{e - 2}"
                elif e >= 8:
                    prev_mm = f"qkmm_{t - 1}_{6 + (e % 2)}"
                    prev_ev = None
                elif e >= 2:
                    prev_mm = f"qkmm_{t}_{e - 2}"
                    prev_ev = f"qkev_{t}_{e - 2}"
                else:
                    prev_mm = f"qkmm_{t}_{14 + (e % 2)}"
                    prev_ev = None

                @blk.sync
                def _(sy, t=t, e=e, prev_mm=prev_mm):
                    if t == 0 and e in (8, 9):
                        sy.wait_ge(d_swp[0].sem, W["wp_done0"])
                        sy.wait_ge(d_swp[1].sem, W["wp_done1"])
                    prev = W.get(prev_mm, 0)
                    if prev:
                        sy.wait_ge(t_p.sem, prev)
                    d_we[e % 2].inc(sy.dma_start(
                        out=we[e % 2][:].rearrange("p (d c) -> p d c", d=8),
                        in_=wrs_h[e].rearrange("d p c -> p d c")), 16)
                    W[f"weld_{t}_{e}"] = d_we[e % 2].n

                @blk.tensor
                def _(te, t=t, e=e, prev_ev=prev_ev):
                    ps = pA if e < 8 else pD
                    te.wait_ge(d_we[e % 2].sem, W[f"weld_{t}_{e}"])
                    te.wait_ge(v_p.sem, W[f"trev_{4 * t + 3}"])
                    if e < 2:
                        te.wait_ge(c_p.sem, W.get(f"exps_done_{t - 1}", 0))
                    if prev_ev is not None and prev_ev in W:
                        te.wait_ge(v_p.sem, W[prev_ev])
                    half = (e % 2) * 512
                    for dk in range(8):
                        ins = te.matmul(ps[:, half:half + 512],
                                        we[e % 2][:, dk * 128:(dk + 1) * 128],
                                        xnT[:, dk * 512:(dk + 1) * 512],
                                        start=(dk == 0), stop=(dk == 7))
                    t_p.inc(ins)
                    W[f"qkmm_{t}_{e}"] = t_p.n

                @blk.vector
                def _(ve, t=t, e=e):
                    ps = pA if e < 8 else pD
                    ve.wait_ge(t_p.sem, W[f"qkmm_{t}_{e}"])
                    if e < 8:
                        dst = qy[:, e * 512:(e + 1) * 512]
                    else:
                        dst = kT[:, (e - 8) * T + t * 512:(e - 8) * T + t * 512 + 512]
                    half = (e % 2) * 512
                    v_p.inc(ve.tensor_scalar(dst, ps[:, half:half + 512],
                                             bT[:, e:e + 1], None, OP.add))
                    W[f"qkev_{t}_{e}"] = v_p.n
                    ve.wait_ge(v_p.sem, v_p.n)
                    v_p.inc(ve.reduce_max(gtmp[:], dst, axis=AX.X, apply_absolute_value=True))
                    ve.wait_ge(v_p.sem, v_p.n)
                    g_p.inc(ve.tensor_max(qmaxT[:, e:e + 1], qmaxT[:, e:e + 1], gtmp[:]))
                    W[f"gmax_{t}_{e}"] = g_p.n

            def emit_qk_sync(t, e):
                prev_mm = f"qkmm_{t}_{e - 2}" if e >= 10 else f"qkmm_{t - 1}_{6 + (e % 2)}"

                @blk.sync
                def _(sy, t=t, e=e, prev_mm=prev_mm):
                    if t == 0 and e in (8, 9):
                        sy.wait_ge(d_swp[0].sem, W["wp_done0"])
                        sy.wait_ge(d_swp[1].sem, W["wp_done1"])
                    prev = W.get(prev_mm, 0)
                    if prev:
                        sy.wait_ge(t_p.sem, prev)
                    d_we[e % 2].inc(sy.dma_start(
                        out=we[e % 2][:].rearrange("p (d c) -> p d c", d=8),
                        in_=wrs_h[e].rearrange("d p c -> p d c")), 16)
                    W[f"weld_{t}_{e}"] = d_we[e % 2].n

            def qk_fillers(t, e):
                prev_ev = f"qkev_{t}_{e - 2}" if e >= 10 else None

                def mk(dk, t=t, e=e, prev_ev=prev_ev):
                    def emit(te):
                        if dk == 0:
                            te.wait_ge(d_we[e % 2].sem, W[f"weld_{t}_{e}"])
                            te.wait_ge(v_p.sem, W[f"trev_{4 * t + 3}"])
                            if prev_ev is not None and prev_ev in W:
                                te.wait_ge(v_p.sem, W[prev_ev])
                        half = (e % 2) * 512
                        ins = te.matmul(pD[:, half:half + 512],
                                        we[e % 2][:, dk * 128:(dk + 1) * 128],
                                        xnT[:, dk * 512:(dk + 1) * 512],
                                        start=(dk == 0), stop=(dk == 7))
                        if dk == 7:
                            t_p.inc(ins)
                            W[f"qkmm_{t}_{e}"] = t_p.n
                    return emit
                return [mk(dk) for dk in range(8)]

            def trans_fillers(t, i):
                ti = 4 * t + i

                def mk(dk, t=t, i=i, ti=ti):
                    def emit(te):
                        if dk == 0:
                            te.wait_ge(v_p.sem, W[f"xnapply_{ti}"])
                            if i == 0:
                                te.wait_ge(v_p.sem, W[f"cbready_{t - 1}"])
                                te.wait_ge(v_p.sem, W.get(f"pjev_{t - 2}_2", 0))
                            else:
                                te.wait_ge(v_p.sem, W[f"trev_{ti - 1}"])
                        ins = te.transpose(pD[:, dk * 64:dk * 64 + 64].bitcast(BF16),
                                           xnb[i % 2][:, dk * 128:(dk + 1) * 128], identb[:])
                        if dk == 7:
                            t_p.inc(ins)
                            W[f"trmm_{ti}"] = t_p.n
                    return emit
                return [mk(dk) for dk in range(8)]

            def emit_trans_ev(t, i):
                ti = 4 * t + i

                @blk.vector
                def _(ve, t=t, i=i, ti=ti):
                    ve.wait_ge(t_p.sem, W[f"trmm_{ti}"])
                    if i == 0:
                        ve.wait_ge(d_sx.sem, W.get(f"xnst_{t - 1}", 0))
                        ve.wait_ge(t_p.sem, W.get(f"vmm_{4 * (t - 1) + 3}", 0))
                    for dk in range(8):
                        ins = ve.tensor_copy(xnT[:, dk * 512 + i * 128:dk * 512 + i * 128 + 128],
                                             pD[:, dk * 64:dk * 64 + 64].bitcast(BF16))
                    v_p.inc(ins)
                    W[f"trev_{ti}"] = v_p.n

            def emit_qk_ev(t, e):
                @blk.vector
                def _(ve, t=t, e=e):
                    ve.wait_ge(t_p.sem, W[f"qkmm_{t}_{e}"])
                    dst = kT[:, (e - 8) * T + t * 512:(e - 8) * T + t * 512 + 512]
                    half = (e % 2) * 512
                    v_p.inc(ve.tensor_scalar(dst, pD[:, half:half + 512],
                                             bT[:, e:e + 1], None, OP.add))
                    W[f"qkev_{t}_{e}"] = v_p.n
                    ve.wait_ge(v_p.sem, v_p.n)
                    v_p.inc(ve.reduce_max(gtmp[:], dst, axis=AX.X, apply_absolute_value=True))
                    ve.wait_ge(v_p.sem, v_p.n)
                    g_p.inc(ve.tensor_max(qmaxT[:, e:e + 1], qmaxT[:, e:e + 1], gtmp[:]))
                    W[f"gmax_{t}_{e}"] = g_p.n

            def emit_v_part_i(t, i):
                ti = 4 * t + i

                @blk.tensor
                def _(te, t=t, i=i, ti=ti):
                    if i == 0:
                        te.wait_ge(c_p.sem, W.get(f"exps_done_{t - 1}", 0))
                    else:
                        te.wait_ge(v_p.sem, W.get(f"vev_{ti - 1}", 0))
                    for dk in range(8):
                        lhs = xnT[:, dk * 512 + i * 128:dk * 512 + i * 128 + 128]
                        te.matmul(pB[:, 0:512], lhs, wv[:, dk * 1024:dk * 1024 + 512],
                                  start=(dk == 0), stop=(dk == 7))
                        ins = te.matmul(pB[:, 512:1024], lhs, wv[:, dk * 1024 + 512:dk * 1024 + 1024],
                                        start=(dk == 0), stop=(dk == 7))
                    t_p.inc(ins)
                    W[f"vmm_{ti}"] = t_p.n

                @blk.vector
                def _(ve, ti=ti):
                    ve.wait_ge(t_p.sem, W[f"vmm_{ti}"])
                    dst = vP[:, ti * NH * VST:(ti + 1) * NH * VST]
                    dst3 = dst.rearrange("p (h s) -> p h s", s=VST)[:, :, 0:64]
                    src3 = pB[:].rearrange("p (h s) -> p h s", s=64)
                    v_p.inc(ve.tensor_copy(dst3, src3))
                    W[f"vev_{ti}"] = v_p.n

            def emit_cb(t):
                @blk.vector
                def _(ve, t=t):
                    ve.wait_ge(g_p.sem, W[f"gmax_{t}_7"])
                    ve.wait_ge(t_p.sem, W.get(f"cbm3_{t - 1}", 0))
                    v_p.inc(ve.tensor_mul(prodall[:], qmaxT[:, 0:8], qmaxT[:, 8:16]))
                    W[f"cbp_{t}"] = v_p.n

                @blk.tensor
                def _(te, t=t):
                    te.wait_ge(v_p.sem, W[f"cbp_{t}"])
                    t_p.inc(te.matmul(pD[0:8, 0:1], prodall[:], onescol[:], start=True, stop=True))
                    W[f"cbm1_{t}"] = t_p.n

                @blk.vector
                def _(ve, t=t):
                    ve.wait_ge(t_p.sem, W[f"cbm1_{t}"])
                    v_p.inc(ve.tensor_copy(c8[:], pD[0:8, 0:1]))
                    W[f"cbc8_{t}"] = v_p.n

                @blk.tensor
                def _(te, t=t):
                    te.wait_ge(v_p.sem, W[f"cbc8_{t}"])
                    t_p.inc(te.transpose(pD[0:1, 8:16], c8[:], ident[0:8, 0:8]))
                    W[f"cbm2_{t}"] = t_p.n

                @blk.vector
                def _(ve, t=t):
                    ve.wait_ge(t_p.sem, W[f"cbm2_{t}"])
                    v_p.inc(ve.tensor_scalar_mul(crowb[:], pD[0:1, 8:16], -0.0625))
                    W[f"cbrow_{t}"] = v_p.n

                @blk.tensor
                def _(te, t=t):
                    te.wait_ge(v_p.sem, W[f"cbrow_{t}"])
                    t_p.inc(te.matmul(pD[:, 16:24], onesb[0:1, :], crowb[:], start=True, stop=True))
                    W[f"cbm3_{t}"] = t_p.n

                @blk.vector
                def _(ve, t=t):
                    ve.wait_ge(t_p.sem, W[f"cbm3_{t}"])
                    v_p.inc(ve.tensor_copy(cbt[:], pD[:, 16:24]))
                    W[f"cbready_{t}"] = v_p.n

            def emit_recip(tb):
                @blk.vector
                def _(ve, tb=tb):
                    ve.wait_ge(t_p.sem, W.get(f"rbmm_{tb - 1}_7", 0))
                    ve.wait_ge(d_sl.sem, W[f"lst_{tb}_7"])
                    ve.reciprocal(Lr[:], Lr[:])
                    v_p.inc(ve.tensor_copy(Rb[:], Lr[:]))
                    W[f"rb_{tb}"] = v_p.n

            def emit_ynorm(tb, yc):
                @blk.tensor
                def _(te, tb=tb, yc=yc):
                    reg = (pD if yc % 4 < 2 else pC)
                    off = (yc % 2) * 512
                    te.wait_ge(v_p.sem, W[f"rb_{tb}"])
                    if yc >= 4:
                        te.wait_ge(v_p.sem, W[f"ynorm_{tb}_{yc - 4}"])
                    t_p.inc(te.matmul(reg[:, off:off + 512],
                                      sel16[:, yc * 128:(yc + 1) * 128], Rb[:],
                                      start=True, stop=True))
                    W[f"rbmm_{tb}_{yc}"] = t_p.n

                @blk.vector
                def _(ve, tb=tb, yc=yc):
                    reg = (pD if yc % 4 < 2 else pC)
                    off = (yc % 2) * 512
                    ve.wait_ge(t_p.sem, W[f"rbmm_{tb}_{yc}"])
                    v_p.inc(ve.tensor_mul(yT[:, yc * 512:(yc + 1) * 512],
                                          yT[:, yc * 512:(yc + 1) * 512],
                                          reg[:, off:off + 512]))
                    W[f"ynorm_{tb}_{yc}"] = v_p.n

            def emit_proj(tb):
                @blk.sync
                def _(sy, tb=tb):
                    t_base = t_p.n
                    if tb == 0:
                        sy.wait_ge(d_wrs.sem, W["wrs_done"])
                    for cc in range(8):
                        if cc < 2:
                            sy.wait_ge(t_p.sem, last_wpb[cc])
                        else:
                            sy.wait_ge(t_p.sem, t_base + 4 * (cc - 1))
                        d_wp[cc % 2].inc(sy.dma_start(out=wpb[cc % 2][:], in_=wp_s[cc]), 16)
                        W[f"wpld_{tb}_{cc}"] = d_wp[cc % 2].n

                @blk.tensor
                def _(te, tb=tb):
                    if f"cbready_{tb + 1}" in W:
                        te.wait_ge(v_p.sem, W[f"cbready_{tb + 1}"])
                    else:
                        te.wait_ge(c_p.sem, W[f"exps_done_{tb}"])
                        te.wait_ge(v_p.sem, W[f"ynorm_{tb}_7"])
                    pjs = (pA, pB, pD, pC)
                    for cc in range(8):
                        te.wait_ge(d_wp[cc % 2].sem, W[f"wpld_{tb}_{cc}"])
                        for i in range(4):
                            pj = pjs[i]
                            te.matmul(pj[:, 0:512],
                                      yT[:, cc * 512 + i * 128:cc * 512 + i * 128 + 128],
                                      wpb[cc % 2][:, 0:512], start=(cc == 0), stop=False)
                            t_p.inc(te.matmul(pj[:, 512:1024],
                                              yT[:, cc * 512 + i * 128:cc * 512 + i * 128 + 128],
                                              wpb[cc % 2][:, 512:1024], start=(cc == 0), stop=False))
                        last_wpb[cc % 2] = t_p.n
                    for i in range(4):
                        pj = pjs[i]
                        te.matmul(pj[:, 0:512], onesb[:], bpp[0:1, 0:512], start=False, stop=True)
                        t_p.inc(te.matmul(pj[:, 512:1024], onesb[:], bpp[0:1, 512:1024],
                                          start=False, stop=True))
                    W[f"pjmm_{tb}"] = t_p.n

                for i, pj in ((0, pA), (1, pB), (2, pD), (3, pC)):
                    @blk.vector
                    def _(ve, tb=tb, i=i, pj=pj):
                        ve.wait_ge(t_p.sem, W[f"pjmm_{tb}"])
                        ve.wait_ge(d_ss.sem, W.get(f"sst_{tb}_{i - 1}", 0))
                        v_p.inc(ve.tensor_copy(sev0[:], pj[:]))
                        W[f"pjev_{tb}_{i}"] = v_p.n

                    @blk.gpsimd
                    def _(gp, tb=tb, i=i):
                        gp.wait_ge(v_p.sem, W[f"pjev_{tb}_{i}"])
                        d_ss.inc(gp.dma_start(out=s_s[4 * tb + i], in_=sev0[:]), 16)
                        W[f"sst_{tb}_{i}"] = d_ss.n

                W[f"pjev_done_{tb}"] = v_p.n

            wconv_chunks = []
            for kind, n_tiles, nh in (("g", 8, 3), ("u", 8, 3), ("d", 24, 1)):
                for j in range(n_tiles):
                    for th in range(nh):
                        wconv_chunks.append((kind, j, th))
            NWH = 2 * len(wconv_chunks)

            def emit_wconv(sl):
                if sl < NWH or 0 <= sl - 2 < NWH:
                    @blk.gpsimd
                    def _(gp, sl=sl):
                        if 0 <= sl - 2 < NWH:
                            k = sl - 2
                            gp.wait_ge(v_p.sem, W[f"wcv_{k}"])
                            d_wcs[k % 2].inc(gp.dma_start(
                                out=wbs_h[k // 2, :, (k % 2) * 512:(k % 2) * 512 + 512],
                                in_=wcvb[k % 2][:]), 16)
                            W[f"wcs_{k}"] = d_wcs[k % 2].n
                        if sl < NWH:
                            kind, j, th = wconv_chunks[sl // 2]
                            src = {"g": wgT_h, "u": wuT_h, "d": wdT_h}[kind]
                            c0 = th * 1024 + (sl % 2) * 512
                            if sl >= 2:
                                gp.wait_ge(v_p.sem, W[f"wcv_{sl - 2}"])
                            d_wcl[sl % 2].inc(gp.dma_start(
                                out=wcb[sl % 2][:],
                                in_=src[j * 128:(j + 1) * 128, c0:c0 + 512]), 16)
                            W[f"wcl_{sl}"] = d_wcl[sl % 2].n

                if 0 <= sl - 1 < NWH:
                    @blk.vector
                    def _(ve, sl=sl - 1):
                        ve.wait_ge(d_wcl[sl % 2].sem, W[f"wcl_{sl}"])
                        if sl >= 2:
                            ve.wait_ge(d_wcs[sl % 2].sem, W[f"wcs_{sl - 2}"])
                        v_p.inc(ve.tensor_copy(wcvb[sl % 2][:], wcb[sl % 2][:]))
                        W[f"wcv_{sl}"] = v_p.n

            pre_c0 = c_p.n

            def c_rstd0(i, pre_c0=pre_c0):
                return pre_c0 + 2 * (i + 1)

            W["cbready_-1"] = W["initdone"]
            emit_ln_tile(0, 0, c_rstd0)
            emit_trans_tile(0, 0)
            emit_ln_tile(0, 1, c_rstd0)
            emit_trans_tile(0, 1)
            emit_ln_tile(0, 2, c_rstd0)
            emit_trans_tile(0, 2)
            emit_ln_tile(0, 3, c_rstd0)
            emit_trans_tile(0, 3)

            @blk.scalar
            def _(sc, pre_c0=pre_c0):
                for i in range(4):
                    L = 16 * i
                    sc.wait_ge(v_p.sem, W[f"var_{i}"])
                    c_p.inc(sc.activation(lnsc[:, L + 14:L + 15], lnsc[:, L + 13:L + 14],
                                          AF.Ln, bias=lnsc[:, 64:65]))
                    sc.wait_ge(c_p.sem, c_p.n)
                    c_p.inc(sc.activation(lnsc[:, L + 15:L + 16], lnsc[:, L + 14:L + 15],
                                          AF.Exp, scale=-0.5))
                    assert c_p.n == pre_c0 + 2 * (i + 1)

            @blk.gpsimd
            def _(gp):
                gp.wait_ge(v_p.sem, W["trev_3"])
                d_sx.inc(gp.dma_start(out=xnT_s[0], in_=xnT[:]), 16)
                W["xnst_0"] = d_sx.n

            emit_ln_load(1, 0)
            emit_ln_load(1, 1)
            for e in range(8, 16):
                emit_qk_chunk(0, e)

            for t in range(NT):
                nkc = 4 * t + 4
                has_next = t + 1 < NT
                R = 4 if has_next else 0
                c_base_t = c_p.n

                K0 = min(6, 4 * t + 2, nkc)

                def c_exp_done(c, kc, c_base_t=c_base_t, nkc=nkc, R=R, K0=K0):
                    return (c_base_t + 2 * min(c, R) + c * 2 * nkc + 2 * (kc + 1)
                            + (2 if (c < R and kc >= K0) else 0))

                def c_rstd_next(i, c_base_t=c_base_t, nkc=nkc, R=R, K0=K0):
                    return c_base_t + 2 * min(i, R) + i * 2 * nkc + 2 * K0 + 2

                emit_wconv(18 * t)
                tb = t - 1
                if t >= 1:
                    emit_recip(tb)
                emit_qk_chunk(t, 0)
                emit_qk_chunk(t, 1)
                if t >= 1:
                    emit_ynorm(tb, 0)
                    emit_ynorm(tb, 1)
                emit_qk_chunk(t, 2)
                emit_qk_chunk(t, 3)
                if t >= 1:
                    emit_ynorm(tb, 2)
                    emit_ynorm(tb, 3)
                emit_qk_chunk(t, 4)
                emit_qk_chunk(t, 5)
                if t >= 1:
                    emit_ynorm(tb, 4)
                    emit_ynorm(tb, 5)
                emit_qk_chunk(t, 6)
                emit_qk_chunk(t, 7)
                if t >= 1:
                    emit_ynorm(tb, 6)
                    emit_ynorm(tb, 7)
                emit_v_part_i(t, 0)
                emit_v_part_i(t, 1)
                emit_v_part_i(t, 2)
                emit_v_part_i(t, 3)
                emit_cb(t)
                if has_next:
                    emit_ln_stats(t + 1, 0)
                if t >= 1:
                    emit_proj(tb)
                if has_next:
                    emit_ln_apply(t + 1, 0, c_rstd_next)
                emit_wconv(18 * t + 1)

                for c in range(8):
                    emit_wconv(18 * t + 2 + 2 * c)
                    if has_next and 1 <= c <= 2:
                        emit_ln_load(t + 1, c + 1)
                    if t + 2 < NT and 6 <= c:
                        emit_ln_load(t + 2, c - 6)
                    if has_next and 1 <= c < 4:
                        emit_ln_calc(t + 1, c, c_rstd_next)
                    fillers = []
                    if has_next:
                        if c < 4:
                            fillers = trans_fillers(t + 1, c)
                        else:
                            for e in (8 + 2 * (c - 4), 9 + 2 * (c - 4)):
                                emit_qk_sync(t + 1, e)
                                fillers += qk_fillers(t + 1, e)
                    v_base = v_p.n

                    def v_mask_done(kc, v_base=v_base, t=t):
                        return v_base + (kc - 4 * t) + 1

                    def mk_mms(te, t=t, c=c, nkc=nkc, c_exp_done=c_exp_done,
                               v_mask_done=v_mask_done):
                        def mm1(kc):
                            b = kc % 2
                            q0 = 128 * (kc - 4 * t) if kc >= 4 * t else 0
                            if kc >= 2:
                                te.wait_ge(c_p.sem, c_exp_done(c, kc - 2))
                            else:
                                te.wait_ge(c_p.sem, lastS[("A", b)])
                                te.wait_ge(c_p.sem, lastS[("B", b)])
                                if c == 0:
                                    te.wait_ge(v_p.sem, W[f"cbready_{t}"])
                                    te.wait_ge(v_p.sem, W.get(f"pjev_{t - 1}_1", 0))
                            te.matmul(pA[:, b * 512 + q0:b * 512 + 512],
                                      kT[0:64, c * T + kc * 128:c * T + kc * 128 + 128],
                                      qy[0:64, c * 512 + q0:(c + 1) * 512],
                                      start=True, stop=True)
                            t_p.inc(te.matmul(pB[:, b * 512 + q0:b * 512 + 512],
                                              kT[64:128, c * T + kc * 128:c * T + kc * 128 + 128],
                                              qy[64:128, c * 512 + q0:(c + 1) * 512],
                                              start=True, stop=True))
                            W[f"mm1_{t}_{c}_{kc}"] = t_p.n

                        def mm2(kc):
                            b = kc % 2
                            diag = kc >= 4 * t
                            q0 = 128 * (kc - 4 * t) if diag else 0
                            te.wait_ge(c_p.sem, c_exp_done(c, kc))
                            if diag:
                                te.wait_ge(v_p.sem, v_mask_done(kc))
                            if kc == 0 and c == 0:
                                te.wait_ge(v_p.sem, W[f"vev_{4 * t + 3}"])
                                te.wait_ge(v_p.sem, W[f"pjev_done_{t - 1}"])
                            if kc == 0:
                                te.wait_ge(v_p.sem, W.get(f"yev_{t}_{c - 1}", W[f"yev_done_{t - 1}"]))
                            vbase = kc * NH * VST
                            last = (kc == nkc - 1)
                            te.matmul(pC[0:65, q0:512],
                                      vP[:, vbase + (2 * c) * VST:vbase + (2 * c) * VST + 65],
                                      pP[0 + b][:, q0:512], start=(kc == 0), stop=last)
                            t_p.inc(te.matmul(pC[0:65, 512 + q0:1024],
                                              vP[:, vbase + (2 * c + 1) * VST:vbase + (2 * c + 1) * VST + 65],
                                              pP[2 + b][:, q0:512], start=(kc == 0), stop=last))
                            W[f"mm2_{t}_{c}_{kc}"] = t_p.n
                        return mm1, mm2

                    fst = 5 if c < 4 else 3

                    @blk.tensor
                    def _(te, mk_mms=mk_mms, nkc=nkc, fillers=fillers, fst=fst):
                        mm1, mm2 = mk_mms(te)
                        fi = 0
                        for kc in range(nkc):
                            mm1(kc)
                            if 1 <= kc <= nkc - 2:
                                mm2(kc - 1)
                            while fi < len(fillers) and fi <= 2 * (kc - fst):
                                fillers[fi](te)
                                fi += 1
                        while fi < len(fillers):
                            fillers[fi](te)
                            fi += 1

                    assert v_p.n == v_base, (t, c, v_p.n, v_base)

                    @blk.tensor
                    def _(te, mk_mms=mk_mms, t=t, c=c, nkc=nkc):
                        mm1, mm2 = mk_mms(te)
                        mm2(nkc - 2)
                        mm2(nkc - 1)
                        W[f"mm2last_{t}_{c}"] = t_p.n

                    @blk.vector
                    def _(ve, t=t, c=c, nkc=nkc, c_exp_done=c_exp_done):
                        for kc in range(4 * t, nkc):
                            b = kc % 2
                            q0 = 128 * (kc - 4 * t)
                            ve.wait_ge(c_p.sem, c_exp_done(c, kc))
                            ve.tensor_mul(pP[0 + b][:, q0:q0 + 128], pP[0 + b][:, q0:q0 + 128], mask[:])
                            v_p.inc(ve.tensor_mul(pP[2 + b][:, q0:q0 + 128],
                                                  pP[2 + b][:, q0:q0 + 128], mask[:]))
                        lrowA = sev0[0:1, :].bitcast(F32)
                        lrowB = sev0[64:65, :].bitcast(F32)
                        ve.wait_ge(t_p.sem, W[f"mm2last_{t}_{c}"])
                        if c == 0:
                            ve.wait_ge(t_p.sem, W.get(f"pjmm_{t - 1}", 0))
                        ve.wait_ge(d_sl.sem, W.get(f"lst_{t}_{c - 1}", W.get(f"lst_{t - 1}_7", 0)))
                        if c == 0:
                            ve.wait_ge(d_ss.sem, W.get(f"sst_{t - 1}_3", 0))
                        ve.tensor_copy(yT[0:64, c * 512:(c + 1) * 512], pC[0:64, 0:512])
                        ve.tensor_copy(yT[64:128, c * 512:(c + 1) * 512], pC[0:64, 512:1024])
                        ve.tensor_copy(lrowA[0:1, 0:512], pC[64:65, 0:512])
                        v_p.inc(ve.tensor_copy(lrowB[0:1, 0:512], pC[64:65, 512:1024]))
                        W[f"yev_{t}_{c}"] = v_p.n

                    @blk.gpsimd
                    def _(gp, t=t, c=c):
                        lrowA = sev0[0:1, :].bitcast(F32)
                        lrowB = sev0[64:65, :].bitcast(F32)
                        gp.wait_ge(v_p.sem, W[f"yev_{t}_{c}"])
                        d_sl.inc(gp.dma_start(out=Lr[2 * c:2 * c + 1, :], in_=lrowA[0:1, 0:512]), 16)
                        d_sl.inc(gp.dma_start(out=Lr[2 * c + 1:2 * c + 2, :], in_=lrowB[0:1, 0:512]), 16)
                        W[f"lst_{t}_{c}"] = d_sl.n

                    if has_next:
                        if c < 4:
                            emit_trans_ev(t + 1, c)
                            if c == 3:
                                @blk.gpsimd
                                def _(gp, t=t):
                                    gp.wait_ge(v_p.sem, W[f"trev_{4 * (t + 1) + 3}"])
                                    d_sx.inc(gp.dma_start(out=xnT_s[t + 1], in_=xnT[:]), 16)
                                    W[f"xnst_{t + 1}"] = d_sx.n
                        else:
                            emit_qk_ev(t + 1, 8 + 2 * (c - 4))
                            emit_qk_ev(t + 1, 9 + 2 * (c - 4))

                    emit_wconv(18 * t + 3 + 2 * c)
                    lastS[("A", 0)] = lastS[("B", 0)] = c_exp_done(c, nkc - 2)
                    lastS[("A", 1)] = lastS[("B", 1)] = c_exp_done(c, nkc - 1)
                W[f"yev_done_{t}"] = v_p.n

                @blk.scalar
                def _(sc, t=t, nkc=nkc, R=R, c_base_t=c_base_t, K0=K0,
                      c_exp_done=c_exp_done, c_rstd_next=c_rstd_next):
                    for c in range(8):
                        for kc in range(nkc):
                            b = kc % 2
                            q0 = 128 * (kc - 4 * t) if kc >= 4 * t else 0
                            sc.wait_ge(t_p.sem, W[f"mm1_{t}_{c}_{kc}"])
                            if kc >= 2:
                                sc.wait_ge(t_p.sem, W[f"mm2_{t}_{c}_{kc - 2}"])
                            c_p.inc(sc.activation(pP[0 + b][:, q0:512], pA[:, b * 512 + q0:b * 512 + 512],
                                                  AF.Exp, bias=cbt[:, c:c + 1], scale=0.125))
                            c_p.inc(sc.activation(pP[2 + b][:, q0:512], pB[:, b * 512 + q0:b * 512 + 512],
                                                  AF.Exp, bias=cbt[:, c:c + 1], scale=0.125))
                            if c < R and kc == K0 - 1:
                                i = c
                                L = 16 * i
                                sc.wait_ge(v_p.sem, W[f"var_{4 * (t + 1) + i}"])
                                c_p.inc(sc.activation(lnsc[:, L + 14:L + 15], lnsc[:, L + 13:L + 14],
                                                      AF.Ln, bias=lnsc[:, 64:65]))
                                sc.wait_ge(c_p.sem, c_p.n)
                                c_p.inc(sc.activation(lnsc[:, L + 15:L + 16], lnsc[:, L + 14:L + 15],
                                                      AF.Exp, scale=-0.5))
                                assert c_p.n == c_rstd_next(i)
                            assert c_p.n == c_exp_done(c, kc) + (
                                2 if (c < R and kc == K0 - 1) else 0)

                W[f"exps_done_{t}"] = c_p.n

            emit_wconv(NWH)
            emit_wconv(NWH + 1)
            emit_recip(NT - 1)
            for yc in range(8):
                emit_ynorm(NT - 1, yc)
            emit_proj(NT - 1)

            AB_T, AB_V, AB_C = t_p.n, v_p.n, c_p.n
            AB_SX, AB_SS, AB_SL = d_sx.n, d_ss.n, d_sl.n
            AB_WCS = [d_wcs[0].n, d_wcs[1].n]

        cs = contextlib.ExitStack()
        with cs:
            def sbc(name, shape, dt):
                return cs.enter_context(nc.sbuf_tensor(name, shape, dt))

            wg = [sbc(f"wg{d}", [128, HID], BF16) for d in range(8)]
            wu = [sbc(f"wu{d}", [128, HID], BF16) for d in range(8)]
            wd = [sbc(f"wd{h}", [128, D], BF16) for h in range(24)]
            xnC = sbc("xnC", [128, T], BF16)
            hT = [sbc(f"hT{h}", [128, 512], BF16) for h in range(24)]
            xc = [sbc(f"xc{i}", [128, D], F32) for i in range(2)]
            scb_ = [sbc(f"scb{i}", [128, D], BF16) for i in range(2)]
            gs = [sbc("gs0", [128, D], F32)] * 2
            sgb = [sbc("sgb0", [128, 512], F32)] * 2

            pG = [cs.enter_context(nc.psum_tensor(f"pG{i}", [128, 512], F32)) for i in range(2)]
            pU = [cs.enter_context(nc.psum_tensor(f"pU{i}", [128, 512], F32)) for i in range(2)]
            pDn = [cs.enter_context(nc.psum_tensor(f"pDn{i}", [128, 1024], F32)) for i in range(2)]

            def cbar(eng):
                eng.wait_ge(t_p.sem, AB_T)
                eng.wait_ge(v_p.sem, AB_V)
                eng.wait_ge(c_p.sem, AB_C)
                eng.wait_ge(d_sx.sem, AB_SX)
                eng.wait_ge(d_ss.sem, AB_SS)
                eng.wait_ge(d_sl.sem, AB_SL)

            @blk.sync
            def _(sy):
                cbar(sy)
                sy.wait_ge(d_wcs[0].sem, AB_WCS[0])
                sy.wait_ge(d_wcs[1].sem, AB_WCS[1])
                for dk in range(8):
                    d_cw.inc(sy.dma_start(
                        out=wg[dk][:].rearrange("p (c x) -> p c x", c=3),
                        in_=wbs_h[3 * dk:3 * dk + 3].rearrange("c p x -> p c x")), 16)
                W["wg_ld"] = d_cw.n
                for dk in range(8):
                    d_cw.inc(sy.dma_start(
                        out=wu[dk][:].rearrange("p (c x) -> p c x", c=3),
                        in_=wbs_h[24 + 3 * dk:24 + 3 * dk + 3].rearrange("c p x -> p c x")), 16)
                W["wu_ld"] = d_cw.n
                for h in range(24):
                    d_cw.inc(sy.dma_start(out=wd[h][:], in_=wbs_h[48 + h]), 16)
                W["wd_ld"] = d_cw.n

            dn_last = {}
            for tc in range(NT):
                c_tc0 = c_p.n
                v_tc0 = v_p.n

                def c_sil(hc, c_tc0=c_tc0):
                    return c_tc0 + hc + 1

                def c_gsil(i, c_tc0=c_tc0):
                    return c_tc0 + 24 + i + 1

                def v_hm(hc, v_tc0=v_tc0):
                    return v_tc0 + hc + 1

                def v_fin(i, v_tc0=v_tc0):
                    return v_tc0 + 24 + i + 1

                dn_prev = dict(dn_last)
                so_tc0 = [d_so[0].n, d_so[1].n]

                @blk.sync
                def _(sy, tc=tc):
                    if tc == 0:
                        cbar(sy)
                    else:
                        sy.wait_ge(t_p.sem, W[f"gumm_{tc - 1}_23"])
                    d_cx.inc(sy.dma_start(out=xnC[:], in_=xnT_s[tc]), 16)
                    W[f"xnld_{tc}"] = d_cx.n

                for i in range(4):
                    ti = 4 * tc + i

                    @blk.sync
                    def _(sy, tc=tc, i=i, ti=ti, v_fin=v_fin):
                        if ti >= 2:
                            if i < 2:
                                sy.wait_ge(v_p.sem, W[f"fin_{ti - 2}"])
                            else:
                                sy.wait_ge(v_p.sem, v_fin(i - 2))
                        d_cs[i % 2].inc(sy.dma_start(out=xc[i % 2][:], in_=x_h[ti * 128:(ti + 1) * 128, :]), 16)
                        d_cs[i % 2].inc(sy.dma_start(out=scb_[i % 2][:], in_=s_s[ti]), 16)
                        W[f"xsld_{ti}"] = d_cs[i % 2].n

                for hc in range(24):
                    @blk.tensor
                    def _(te, tc=tc, hc=hc, c_sil=c_sil, v_hm=v_hm):
                        if tc == 0 and hc == 0:
                            cbar(te)
                            te.wait_ge(d_cw.sem, W["wg_ld"])
                        te.wait_ge(d_cx.sem, W[f"xnld_{tc}"])
                        if hc >= 2:
                            te.wait_ge(c_p.sem, c_sil(hc - 2))
                            te.wait_ge(v_p.sem, v_hm(hc - 2))
                        elif tc >= 1:
                            te.wait_ge(c_p.sem, W[f"sil_{tc - 1}_{22 + (hc % 2)}"])
                            te.wait_ge(v_p.sem, W[f"hm_{tc - 1}_{22 + (hc % 2)}"])
                        for dk in range(8):
                            te.matmul(pG[hc % 2][:], wg[dk][:, hc * 128:(hc + 1) * 128],
                                      xnC[:, dk * 512:(dk + 1) * 512],
                                      start=(dk == 0), stop=(dk == 7))
                        if tc == 0 and hc == 0:
                            te.wait_ge(d_cw.sem, W["wu_ld"])
                        for dk in range(8):
                            ins = te.matmul(pU[hc % 2][:], wu[dk][:, hc * 128:(hc + 1) * 128],
                                            xnC[:, dk * 512:(dk + 1) * 512],
                                            start=(dk == 0), stop=(dk == 7))
                        t_p.inc(ins)
                        W[f"gumm_{tc}_{hc}"] = t_p.n

                for i in range(4):
                    ti = 4 * tc + i

                    @blk.tensor
                    def _(te, tc=tc, i=i, ti=ti, v_hm=v_hm, c_gsil=c_gsil):
                        if tc == 0 and i == 0:
                            te.wait_ge(d_cw.sem, W["wd_ld"])
                        if i >= 2:
                            te.wait_ge(c_p.sem, c_gsil(i - 2))
                        elif tc >= 1:
                            te.wait_ge(c_p.sem, W[f"gsil_{4 * tc + i - 2}"])
                        for hc in range(24):
                            te.wait_ge(v_p.sem, v_hm(hc))
                            te.matmul(pDn[i % 2][:, 0:512], hT[hc][:, i * 128:(i + 1) * 128],
                                      wd[hc][:, 0:512], start=(hc == 0), stop=(hc == 23))
                            t_p.inc(te.matmul(pDn[i % 2][:, 512:1024], hT[hc][:, i * 128:(i + 1) * 128],
                                              wd[hc][:, 512:1024], start=(hc == 0), stop=(hc == 23)))
                            if i == 3:
                                dn_last[hc] = t_p.n
                        W[f"dnmm_{ti}"] = t_p.n

                @blk.scalar
                def _(sc, tc=tc, c_tc0=c_tc0, v_hm=v_hm, v_tc0=v_tc0, so_tc0=so_tc0):
                    if tc == 0:
                        cbar(sc)
                    for hc in range(24):
                        sc.wait_ge(t_p.sem, W[f"gumm_{tc}_{hc}"])
                        if hc >= 1:
                            sc.wait_ge(v_p.sem, v_hm(hc - 1))
                        elif tc >= 1:
                            sc.wait_ge(v_p.sem, W[f"hm_{tc - 1}_23"])
                        c_p.inc(sc.activation(sgb[hc % 2][:], pG[hc % 2][:], AF.Silu, scale=INV127))
                        W[f"sil_{tc}_{hc}"] = c_p.n
                    for i in range(4):
                        ti = 4 * tc + i
                        sc.wait_ge(t_p.sem, W[f"dnmm_{ti}"])
                        if i >= 1:
                            sc.wait_ge(v_p.sem, v_tc0 + 24 + (i - 1) + 1)
                            sc.wait_ge(d_so[(i - 1) % 2].sem, so_tc0[(i - 1) % 2] + 16 * ((i - 1) // 2 + 1))
                        elif ti >= 1:
                            sc.wait_ge(v_p.sem, W[f"fin_{ti - 1}"])
                            sc.wait_ge(d_so[(i - 1) % 2].sem, W.get(f"outst_{ti - 1}", 0))
                        c_p.inc(sc.activation(gs[i % 2][:], pDn[i % 2][:], AF.Silu,
                                              scale=INV127 * INV127))
                        W[f"gsil_{ti}"] = c_p.n

                @blk.vector
                def _(ve, tc=tc, c_sil=c_sil, c_gsil=c_gsil, dn_prev=dn_prev, so_tc0=so_tc0):
                    if tc == 0:
                        cbar(ve)
                    for hc in range(24):
                        ve.wait_ge(c_p.sem, c_sil(hc))
                        ve.wait_ge(t_p.sem, dn_prev.get(hc, 0))
                        v_p.inc(ve.tensor_mul(hT[hc][:], sgb[hc % 2][:], pU[hc % 2][:]))
                        W[f"hm_{tc}_{hc}"] = v_p.n
                    for i in range(4):
                        ti = 4 * tc + i
                        ve.wait_ge(c_p.sem, c_gsil(i))
                        ve.wait_ge(d_cs[i % 2].sem, W[f"xsld_{ti}"])
                        if i >= 2:
                            ve.wait_ge(d_so[i % 2].sem, so_tc0[i % 2] + 16)
                        else:
                            ve.wait_ge(d_so[i % 2].sem, W.get(f"outst_{ti - 2}", 0))
                        ve.tensor_mul(gs[i % 2][:], gs[i % 2][:], scb_[i % 2][:])
                        v_p.inc(ve.tensor_add(gs[i % 2][:], gs[i % 2][:], xc[i % 2][:]))
                        W[f"fin_{ti}"] = v_p.n

                for i in range(4):
                    ti = 4 * tc + i

                    @blk.gpsimd
                    def _(gp, ti=ti, i=i):
                        if ti == 0:
                            cbar(gp)
                        gp.wait_ge(v_p.sem, W[f"fin_{ti}"])
                        d_so[i % 2].inc(gp.dma_start(out=out_h[ti * 128:(ti + 1) * 128, :], in_=gs[i % 2][:]), 16)
                        W[f"outst_{ti}"] = d_so[i % 2].n

            @blk.gpsimd
            def _(gp):
                gp.wait_ge(d_so[0].sem, d_so[0].n)
                gp.wait_ge(d_so[1].sem, d_so[1].n)

    return nc


def _host_consts():
    hd = 64
    theta = 1.0 / (10000.0 ** (np.arange(0, hd, 2, dtype=np.float64) / hd))
    h_idx = np.arange(NH, dtype=np.float64)
    ang = h_idx[:, None] * theta[None, :]
    cos_t = np.cos(ang)
    sin_t = np.sin(ang)
    row = cos_t.reshape(-1)
    cosrow = np.concatenate([row, row]).reshape(1, D).astype(ml_dtypes.bfloat16)
    row = sin_t.reshape(-1)
    sinrow = np.concatenate([row, row]).reshape(1, D).astype(ml_dtypes.bfloat16)
    ident = np.zeros((40, 8), dtype=np.float32)
    ident[0:8, 0:8] = np.eye(8)
    ident[32:40, 0:8] = np.eye(8)
    identb = np.eye(128, dtype=ml_dtypes.bfloat16)
    onesb = np.ones((1, 128), dtype=ml_dtypes.bfloat16)
    onescol = np.ones((128, 1), dtype=ml_dtypes.bfloat16)
    kk = np.arange(128)[:, None]
    qq = np.arange(128)[None, :]
    masktri = (qq >= kk).astype(ml_dtypes.bfloat16)
    sel16 = np.zeros((16, NH * 64), dtype=ml_dtypes.bfloat16)
    for c in range(8):
        for m in range(128):
            sel16[2 * c + (m >= 64), c * 128 + m] = 1
    return cosrow, sinrow, ident, identb, onesb, onescol, masktri, sel16


def kernel(L0_attn_x, L0_attn_W_attn, L0_attn_b_attn, L0_attn_W_proj,
           L0_attn_b_proj, L0_ffn_w_gate, L0_ffn_w_up, L0_ffn_w_down):
    x = np.ascontiguousarray(np.asarray(L0_attn_x, dtype=np.float32))
    wat = np.ascontiguousarray(np.asarray(L0_attn_W_attn, dtype=np.float32))
    ba = np.ascontiguousarray(np.asarray(L0_attn_b_attn, dtype=np.float32)).reshape(1, 3 * D)
    wp = np.ascontiguousarray(np.asarray(L0_attn_W_proj, dtype=np.float32))
    bp = np.ascontiguousarray(np.asarray(L0_attn_b_proj, dtype=np.float32)).reshape(1, D)
    wgT = np.ascontiguousarray(np.asarray(L0_ffn_w_gate, dtype=np.int32).T)
    wuT = np.ascontiguousarray(np.asarray(L0_ffn_w_up, dtype=np.int32).T)
    wdT = np.ascontiguousarray(np.asarray(L0_ffn_w_down, dtype=np.int32).T)
    cosrow, sinrow, ident, identb, onesb, onescol, masktri, sel16 = _host_consts()

    nc = build_program()
    shared = dict(w_attn=wat, b_attn=ba, w_proj=wp, b_proj=bp,
                  wg_t=wgT, wu_t=wuT, wd_t=wdT,
                  cosrow=cosrow, sinrow=sinrow, ident=ident, identb=identb,
                  onesb=onesb, onescol=onescol, masktri=masktri, sel16=sel16)
    in_maps = [dict(x=np.ascontiguousarray(x[b]), **shared) for b in range(8)]
    res = run_bass_kernel_spmd(nc, in_maps, core_ids=list(range(8)))
    out = np.stack([res.results[b]["out"] for b in range(8)], axis=0)
    return out.astype(np.float32)


# version 24
# speedup vs baseline: 1.0835x; 1.0144x over previous
import contextlib
import math
import numpy as np
import ml_dtypes
import concourse.bass as bass
import concourse.mybir as mybir
from concourse.bass_utils import run_bass_kernel_spmd

F32 = mybir.dt.float32
BF16 = mybir.dt.bfloat16
I32 = mybir.dt.int32
AF = mybir.ActivationFunctionType
AX = mybir.AxisListType
OP = mybir.AluOpType

T = 4096
D = 1024
NH = 16
NT = 8
HID = 3072
VST = 65
INV127 = 1.0 / 127.0


class Ctr:
    def __init__(self, ctx, nc, name):
        self.sem = ctx.enter_context(nc.semaphore(name))
        self.n = 0

    def inc(self, instr, k=1):
        self.n += k
        instr.then_inc(self.sem, k)
        return instr


def build_program(dbg=False):
    nc = bass.Bass("TRN2", target_bir_lowering=False, debug=False)

    x_h = nc.dram_tensor("x", [T, D], F32, kind="ExternalInput").ap()
    wat_h = nc.dram_tensor("w_attn", [D, 3 * D], F32, kind="ExternalInput").ap()
    ba_h = nc.dram_tensor("b_attn", [1, 3 * D], F32, kind="ExternalInput").ap()
    wp_h = nc.dram_tensor("w_proj", [D, D], F32, kind="ExternalInput").ap()
    bp_h = nc.dram_tensor("b_proj", [1, D], F32, kind="ExternalInput").ap()
    wgT_h = nc.dram_tensor("wg_t", [D, HID], I32, kind="ExternalInput").ap()
    wuT_h = nc.dram_tensor("wu_t", [D, HID], I32, kind="ExternalInput").ap()
    wdT_h = nc.dram_tensor("wd_t", [HID, D], I32, kind="ExternalInput").ap()
    cosr_h = nc.dram_tensor("cosrow", [1, D], BF16, kind="ExternalInput").ap()
    sinr_h = nc.dram_tensor("sinrow", [1, D], BF16, kind="ExternalInput").ap()
    ident_h = nc.dram_tensor("ident", [40, 8], F32, kind="ExternalInput").ap()
    identb_h = nc.dram_tensor("identb", [128, 128], BF16, kind="ExternalInput").ap()
    onesb_h = nc.dram_tensor("onesb", [1, 128], BF16, kind="ExternalInput").ap()
    onescol_h = nc.dram_tensor("onescol", [128, 1], BF16, kind="ExternalInput").ap()
    mask_h = nc.dram_tensor("masktri", [128, 128], BF16, kind="ExternalInput").ap()
    sel16_h = nc.dram_tensor("sel16", [16, NH * 64], BF16, kind="ExternalInput").ap()

    out_h = nc.dram_tensor("out", [T, D], F32, kind="ExternalOutput").ap()

    KI = "ExternalOutput" if dbg else "Internal"
    wrs_h = nc.dram_tensor("w_rope_s", [16, 8, 128, 128], BF16, kind=KI).ap()
    wp_s = nc.dram_tensor("wp_s", [8, 128, D], BF16, kind=KI).ap()
    wbs_h = nc.dram_tensor("wbs", [72, 128, 1024], BF16, kind=KI).ap()
    xnT_s = nc.dram_tensor("xnT_s", [NT, 128, T], BF16, kind=KI).ap()
    s_s = nc.dram_tensor("s_s", [4 * NT, 128, D], BF16, kind=KI).ap()

    ctx = contextlib.ExitStack()
    with ctx:
        def sb(name, shape, dt):
            return ctx.enter_context(nc.sbuf_tensor(name, shape, dt))

        ident = sb("s_ident", [40, 8], F32)
        onesb = sb("s_onesb", [1, 128], BF16)
        onescol = sb("s_onescol", [128, 1], BF16)
        mask = sb("s_mask", [128, 128], BF16)
        bT = sb("bT", [128, 24], F32)
        bTb = sb("bTb", [128, 24], BF16)
        bpp = sb("bpp", [1, D], BF16)
        qmaxT = sb("qmaxT", [128, 16], BF16)
        gtmp = sb("gtmp", [128, 1], BF16)
        cbt = sb("cbt", [128, 8], F32)
        lnsc = sb("lnsc", [128, 65], F32)
        identb = sb("s_identb", [128, 128], BF16)
        c8 = sb("c8", [8, 1], F32)
        crowb = sb("crowb", [1, 8], BF16)
        prodall = sb("prodall", [128, 8], BF16)
        sel16 = sb("s_sel16", [16, NH * 64], BF16)
        ones128 = sb("s_ones128", [128, 128], BF16)

        d_ms = Ctr(ctx, nc, "d_ms")
        d_w4 = Ctr(ctx, nc, "d_w4")
        d_x = [Ctr(ctx, nc, "d_x0"), Ctr(ctx, nc, "d_x1")]
        d_we = [Ctr(ctx, nc, "d_we0"), Ctr(ctx, nc, "d_we1")]
        d_wp = [Ctr(ctx, nc, f"d_wp{i}") for i in range(4)]
        d_wcl = [Ctr(ctx, nc, f"d_wcl{i}") for i in range(2)]
        d_wcs = [Ctr(ctx, nc, f"d_wcs{i}") for i in range(2)]
        d_swp = [Ctr(ctx, nc, "d_swp0"), Ctr(ctx, nc, "d_swp1")]
        d_wrs = Ctr(ctx, nc, "d_wrs")
        d_sx = Ctr(ctx, nc, "d_sx")
        d_ss = Ctr(ctx, nc, "d_ss")
        d_sl = Ctr(ctx, nc, "d_sl")
        d_so = [Ctr(ctx, nc, "d_so0"), Ctr(ctx, nc, "d_so1")]
        d_cw = Ctr(ctx, nc, "d_cw")
        d_cx = Ctr(ctx, nc, "d_cx")
        d_cs = [Ctr(ctx, nc, "d_cs0"), Ctr(ctx, nc, "d_cs1")]
        t_p = Ctr(ctx, nc, "t_p")
        v_p = Ctr(ctx, nc, "v_p")
        c_p = Ctr(ctx, nc, "c_p")
        g_p = Ctr(ctx, nc, "g_p")

        blk = ctx.enter_context(nc.Block())
        W = {}

        ab = contextlib.ExitStack()
        with ab:
            def sba(name, shape, dt):
                return ab.enter_context(nc.sbuf_tensor(name, shape, dt))

            kT = sba("kT", [128, 8 * T], BF16)
            vP = sba("vP", [128, 32 * NH * VST], BF16)
            wv = sba("wv", [128, 8 * 1024], BF16)
            we = [sba(f"we{i}", [128, 1024], BF16) for i in range(2)]
            wpb = [sba(f"wpb{i}", [128, 1024], BF16) for i in range(2)]
            pP = [sba(f"pP{i}", [128, 512], BF16) for i in range(4)]
            xb = [sba(f"xb{i}", [128, D], F32) for i in range(2)]
            Lr = sba("Lr", [16, 512], F32)

            pA = ab.enter_context(nc.psum_tensor("pA", [128, 1024], F32))
            pB = ab.enter_context(nc.psum_tensor("pB", [128, 1024], F32))
            pC = ab.enter_context(nc.psum_tensor("pC", [128, 1024], F32))
            pD = ab.enter_context(nc.psum_tensor("pD", [128, 1024], F32))

            pro = contextlib.ExitStack()
            with pro:
                csb = pro.enter_context(nc.sbuf_tensor("csb", [128, 1024], BF16))

                @blk.sync
                def _(sy):
                    for dst, src in ((ident, ident_h), (identb, identb_h),
                                     (onesb, onesb_h),
                                     (onescol, onescol_h), (mask, mask_h),
                                     (sel16, sel16_h)):
                        d_ms.inc(sy.dma_start(out=dst[:], in_=src), 16)
                    d_ms.inc(sy.dma_start(out=csb[0:1, 0:512], in_=cosr_h[0:1, 0:512]), 16)
                    d_ms.inc(sy.dma_start(out=csb[0:1, 512:1024], in_=sinr_h[0:1, 0:512]), 16)
                    d_ms.inc(sy.dma_start(out=xb[0][0:1, :], in_=bp_h), 16)
                    d_ms.inc(sy.dma_start(out=kT[0:1, 0:2048].bitcast(F32), in_=ba_h[0:1, 0:1024]), 16)
                    d_ms.inc(sy.dma_start(out=kT[0:1, 2048:4096].bitcast(F32), in_=ba_h[0:1, 1024:2048]), 16)
                    d_ms.inc(sy.dma_start(out=xb[1][0:1, :], in_=ba_h[0:1, 2048:3072]), 16)
                    W["const_all"] = d_ms.n

                v0_cs = v_p.n

                @blk.tensor
                def _(te):
                    te.wait_ge(d_ms.sem, W["const_all"])
                    for j in range(2):
                        t_p.inc(te.matmul(pA[:, j * 512:(j + 1) * 512],
                                          onesb[0:1, :], csb[0:1, j * 512:(j + 1) * 512],
                                          start=True, stop=True))
                        W[f"csmm{j}"] = t_p.n

                @blk.vector
                def _(ve):
                    for j in range(2):
                        ve.wait_ge(t_p.sem, W[f"csmm{j}"])
                        v_p.inc(ve.tensor_copy(csb[:, j * 512:(j + 1) * 512],
                                               pA[:, j * 512:(j + 1) * 512]))
                    W["csb"] = v_p.n

                @blk.tensor
                def _(te):
                    te.wait_ge(v_p.sem, W["csb"])
                    t_p.inc(te.matmul(pA[:, 0:128], onesb[0:1, :], onesb[0:1, :],
                                      start=True, stop=True))
                    W["onesmm"] = t_p.n

                @blk.vector
                def _(ve):
                    ve.wait_ge(t_p.sem, W["onesmm"])
                    v_p.inc(ve.tensor_copy(ones128[:], pA[:, 0:128]))
                    W["ones128"] = v_p.n
                    ve.wait_ge(d_ms.sem, W["const_all"])
                    for third in (0, 1):
                        row = kT[0:1, third * 2048:(third + 1) * 2048].bitcast(F32)
                        bev = row[0:1, 0:1024:2]
                        bod = row[0:1, 1:1024:2]
                        cev = csb[0:1, 0:512]
                        snv = csb[0:1, 512:1024]
                        pa_ = pD[0:1, 0:512]
                        pb_ = pD[0:1, 512:1024]
                        sb_ = Lr[0:1, :]
                        ve.tensor_mul(pb_, bev, snv)
                        ve.tensor_mul(sb_, bod, snv)
                        v_p.inc(ve.tensor_mul(pa_, bev, cev))
                        ve.wait_ge(v_p.sem, v_p.n)
                        v_p.inc(ve.tensor_tensor(bev, pa_, sb_, OP.subtract))
                        ve.wait_ge(v_p.sem, v_p.n)
                        v_p.inc(ve.tensor_mul(sb_, bod, cev))
                        ve.wait_ge(v_p.sem, v_p.n)
                        ve.tensor_tensor(bod, pb_, sb_, OP.add)
                    v_p.inc(ve.engine_nop())
                    W["bfold"] = v_p.n

                @blk.tensor
                def _(te):
                    te.wait_ge(v_p.sem, W["bfold"])
                    for c in range(24):
                        src = (kT[0:1, 0:2048].bitcast(F32),
                               kT[0:1, 2048:4096].bitcast(F32), xb[1][0:1, :])[c // 8]
                        ins = te.transpose(pC[:, c:c + 1],
                                           src[0:1, (c % 8) * 128:(c % 8) * 128 + 128],
                                           ident[0:1, 0:1])
                    t_p.inc(ins)
                    W["btr"] = t_p.n

                @blk.vector
                def _(ve):
                    ve.wait_ge(t_p.sem, W["btr"])
                    v_p.inc(ve.tensor_copy(bT[:], pC[:, 0:24]))
                    ve.wait_ge(v_p.sem, v_p.n)
                    ve.tensor_copy(bTb[:], bT[:])
                    ve.memset(lnsc[:, 64:65], 1e-5)
                    ve.tensor_copy(pP[0][0:1, :], xb[0][0:1, 0:512])
                    v_p.inc(ve.tensor_copy(pP[1][0:1, :], xb[0][0:1, 512:1024]))
                    W["bT_bp"] = v_p.n

                wst4 = xb[1]
                if True:
                    v_wp0 = v_p.n

                    @blk.sync
                    def _(sy):
                        sy.wait_ge(t_p.sem, W["btr"])
                        for g in range(16):
                            c, h = g // 2, g % 2
                            if g >= 2:
                                sy.wait_ge(v_p.sem, v_wp0 + g - 1)
                            d_x[h].inc(sy.dma_start(
                                out=wst4[:, h * 512:(h + 1) * 512],
                                in_=wp_h[c * 128:(c + 1) * 128, h * 512:(h + 1) * 512]), 16)
                            W[f"wpld{g}"] = d_x[h].n

                    st_wp0 = [d_swp[0].n, d_swp[1].n]

                    @blk.vector
                    def _(ve):
                        for g in range(16):
                            c, h = g // 2, g % 2
                            ve.wait_ge(d_x[h].sem, W[f"wpld{g}"])
                            if h == 0 and c >= 2:
                                ve.wait_ge(d_swp[c % 2].sem, st_wp0[c % 2] + 16 * (c // 2))
                            v_p.inc(ve.tensor_copy(we[c % 2][:, h * 512:(h + 1) * 512],
                                                   wst4[:, h * 512:(h + 1) * 512]))
                            if h == 1:
                                W[f"wpcv{c}"] = v_p.n

                    @blk.gpsimd
                    def _(gp):
                        for c in range(8):
                            gp.wait_ge(v_p.sem, W[f"wpcv{c}"])
                            d_swp[c % 2].inc(gp.dma_start(out=wp_s[c], in_=we[c % 2][:]), 16)
                            W[f"wpst{c}"] = d_swp[c % 2].n
                        W["wp_done0"] = d_swp[0].n
                        W["wp_done1"] = d_swp[1].n

                    t_bpp0 = t_p.n

                    @blk.sync
                    def _(sy):
                        sy.wait_ge(d_swp[0].sem, W["wp_done0"])
                        sy.wait_ge(d_swp[1].sem, W["wp_done1"])
                        for c in range(8):
                            if c >= 2:
                                sy.wait_ge(t_p.sem, t_bpp0 + c - 1)
                            d_wp[c % 2].inc(sy.dma_start(out=wpb[c % 2][:], in_=wp_s[c]), 16)
                            W[f"bppld{c}"] = d_wp[c % 2].n

                    @blk.tensor
                    def _(te):
                        te.wait_ge(v_p.sem, W["bT_bp"])
                        for c in range(8):
                            te.wait_ge(d_wp[c % 2].sem, W[f"bppld{c}"])
                            te.matmul(pD[0:1, 0:512], bTb[:, 16 + c:17 + c], wpb[c % 2][:, 0:512],
                                      start=(c == 0), stop=False)
                            t_p.inc(te.matmul(pD[0:1, 512:1024], bTb[:, 16 + c:17 + c],
                                              wpb[c % 2][:, 512:1024], start=(c == 0), stop=False))
                            W[f"bppmm{c}"] = t_p.n
                        te.matmul(pD[0:1, 0:512], onesb[0:1, 0:1], pP[0][0:1, :], start=False, stop=True)
                        t_p.inc(te.matmul(pD[0:1, 512:1024], onesb[0:1, 0:1], pP[1][0:1, :],
                                          start=False, stop=True))
                        W["bppdone"] = t_p.n

                    @blk.vector
                    def _(ve):
                        ve.wait_ge(t_p.sem, W["bppdone"])
                        ve.tensor_copy(bpp[:], pD[0:1, :])
                        ve.memset(qmaxT[:], 0.0)
                        v_p.inc(ve.memset(vP[:, 64::VST], 1.0))
                        W["initdone"] = v_p.n

                    v_wf0 = v_p.n

                    @blk.sync
                    def _(sy):
                        for g in range(48):
                            k, h = g // 2, g % 2
                            dk, th = k // 3, k % 3
                            if g < 2:
                                sy.wait_ge(t_p.sem, W["bppdone"])
                                sy.wait_ge(v_p.sem, W["wpcv7"])
                            else:
                                sy.wait_ge(v_p.sem, v_wf0 + g - 1)
                            d_x[h].inc(sy.dma_start(
                                out=wst4[:, h * 512:(h + 1) * 512],
                                in_=wat_h[dk * 128:(dk + 1) * 128,
                                          th * 1024 + h * 512:th * 1024 + (h + 1) * 512]), 16)
                            W[f"watld_{g}"] = d_x[h].n

                    st_wf0 = d_wrs.n

                    @blk.vector
                    def _(ve):
                        for g in range(48):
                            k, h = g // 2, g % 2
                            dk, th = k // 3, k % 3
                            ve.wait_ge(d_x[h].sem, W[f"watld_{g}"])
                            src = wst4[:, h * 512:(h + 1) * 512]
                            if th < 2:
                                if h == 0 and dk >= 1:
                                    ve.wait_ge(d_wrs.sem, st_wf0 + 32 * dk)
                                dst = wpb[th]
                                cev = csb[:, h * 256:(h + 1) * 256]
                                snv = csb[:, 512 + h * 256:512 + (h + 1) * 256]
                                d0 = h * 512
                                ve.tensor_mul(xb[0][:, 0:256], src[:, 0::2], cev)
                                ve.tensor_mul(pD[:, 0:256], src[:, 1::2], snv)
                                ve.tensor_tensor(dst[:, d0:d0 + 512:2], xb[0][:, 0:256],
                                                 pD[:, 0:256], OP.subtract)
                                ve.tensor_mul(xb[0][:, 0:256], src[:, 1::2], cev)
                                ve.tensor_mul(pD[:, 0:256], src[:, 0::2], snv)
                                v_p.inc(ve.tensor_tensor(dst[:, d0 + 1:d0 + 512:2], xb[0][:, 0:256],
                                                         pD[:, 0:256], OP.add))
                            else:
                                v_p.inc(ve.tensor_copy(
                                    wv[:, dk * 1024 + h * 512:dk * 1024 + (h + 1) * 512], src))
                            if h == 1:
                                W[f"wf_{dk}_{th}"] = v_p.n

                    @blk.gpsimd
                    def _(gp):
                        for dk in range(8):
                            gp.wait_ge(v_p.sem, W[f"wf_{dk}_1"])
                            d_wrs.inc(gp.dma_start(
                                out=wrs_h[0:8, dk].rearrange("e p c -> p e c"),
                                in_=wpb[0][:].rearrange("p (e c) -> p e c", e=8)), 16)
                            d_wrs.inc(gp.dma_start(
                                out=wrs_h[8:16, dk].rearrange("e p c -> p e c"),
                                in_=wpb[1][:].rearrange("p (e c) -> p e c", e=8)), 16)
                            W[f"wqkst{dk}"] = d_wrs.n
                        W["wrs_done"] = d_wrs.n
            qy = sba("qy", [128, 8 * 512], BF16)
            yT = sba("yT", [128, 8 * 512], BF16)
            sev0 = sba("sev0", [128, D], BF16)
            xnb = [sba("xnb0", [128, D], BF16)] * 2
            xnT = sba("xnT", [128, 8 * 512], BF16)
            Rb = sba("Rb", [16, 512], BF16)
            wcb = [sba(f"wcb{i}", [128, 512], I32) for i in range(2)]
            wcvb = [sba(f"wcvb{i}", [128, 512], BF16) for i in range(2)]

            lastS = {("A", 0): 0, ("A", 1): 0, ("B", 0): 0, ("B", 1): 0}
            W["pj_done_-1"] = 0
            W["pjev_done_-1"] = W["initdone"]
            W["yev_done_-1"] = W["bT_bp"]
            last_wpb = [W["bppmm6"], W["bppmm7"]]

            def emit_ln_load(t, i):
                ti = 4 * t + i

                @blk.sync
                def _(sy, t=t, i=i, ti=ti):
                    if t == 0 and i <= 1:
                        sy.wait_ge(d_wrs.sem, W["wrs_done"])
                        sy.wait_ge(v_p.sem, W["bT_bp"])
                    if ti >= 2:
                        sy.wait_ge(v_p.sem, W[f"xnapply_{ti - 2}"])
                    d_x[i % 2].inc(sy.dma_start(out=xb[i % 2][:], in_=x_h[ti * 128:(ti + 1) * 128, :]), 16)
                    W[f"xld_{ti}"] = d_x[i % 2].n

            def emit_ln_stats(t, i):
                ti = 4 * t + i
                L = 16 * i

                @blk.vector
                def _(ve, t=t, i=i, ti=ti, L=L):
                    ve.wait_ge(d_x[i % 2].sem, W[f"xld_{ti}"])
                    ve.bn_stats(lnsc[:, L + 0:L + 6], xb[i % 2][:, 0:512])
                    v_p.inc(ve.bn_stats(lnsc[:, L + 6:L + 12], xb[i % 2][:, 512:1024]))
                    ve.wait_ge(v_p.sem, v_p.n)
                    v_p.inc(ve.bn_aggr(lnsc[:, L + 12:L + 14], lnsc[:, L + 0:L + 12]))
                    W[f"var_{ti}"] = v_p.n

            def emit_ln_apply(t, i, c_rstd_done):
                ti = 4 * t + i
                L = 16 * i

                @blk.vector
                def _(ve, t=t, i=i, ti=ti, L=L, c_rstd_done=c_rstd_done):
                    if ti >= 1:
                        ve.wait_ge(t_p.sem, W[f"trmm_{ti - 1}"])
                    ve.wait_ge(c_p.sem, c_rstd_done(i))
                    v_p.inc(ve.scalar_tensor_tensor(lnsc[:, L + 0:L + 1], lnsc[:, L + 12:L + 13], -1.0,
                                                    lnsc[:, L + 15:L + 16], OP.mult, OP.mult))
                    ve.wait_ge(v_p.sem, v_p.n)
                    v_p.inc(ve.tensor_scalar(xnb[i % 2][:], xb[i % 2][:],
                                             lnsc[:, L + 15:L + 16], lnsc[:, L + 0:L + 1],
                                             OP.mult, OP.add))
                    W[f"xnapply_{ti}"] = v_p.n

            def emit_ln_calc(t, i, c_rstd_done):
                emit_ln_stats(t, i)
                emit_ln_apply(t, i, c_rstd_done)

            def emit_ln_tile(t, i, c_rstd_done):
                emit_ln_load(t, i)
                emit_ln_calc(t, i, c_rstd_done)

            def emit_trans_tile(t, i):
                ti = 4 * t + i

                @blk.tensor
                def _(te, t=t, i=i, ti=ti):
                    te.wait_ge(v_p.sem, W[f"xnapply_{ti}"])
                    if i == 0:
                        te.wait_ge(v_p.sem, W[f"cbready_{t - 1}"])
                        te.wait_ge(v_p.sem, W.get(f"pjev_{t - 2}_2", 0))
                    else:
                        te.wait_ge(v_p.sem, W[f"trev_{ti - 1}"])
                    for dk in range(8):
                        ins = te.transpose(pD[:, dk * 64:dk * 64 + 64].bitcast(BF16),
                                           xnb[i % 2][:, dk * 128:(dk + 1) * 128], identb[:])
                    t_p.inc(ins)
                    W[f"trmm_{ti}"] = t_p.n

                @blk.vector
                def _(ve, t=t, i=i, ti=ti):
                    ve.wait_ge(t_p.sem, W[f"trmm_{ti}"])
                    if i == 0:
                        ve.wait_ge(d_sx.sem, W.get(f"xnst_{t - 1}", 0))
                        ve.wait_ge(t_p.sem, W.get(f"vmm_{4 * (t - 1) + 3}", 0))
                    for dk in range(8):
                        ins = ve.tensor_copy(xnT[:, dk * 512 + i * 128:dk * 512 + i * 128 + 128],
                                             pD[:, dk * 64:dk * 64 + 64].bitcast(BF16))
                    v_p.inc(ins)
                    W[f"trev_{ti}"] = v_p.n

            def emit_qk_chunk(t, e):
                if e >= 10:
                    prev_mm = f"qkmm_{t}_{e - 2}"
                    prev_ev = f"qkev_{t}_{e - 2}"
                elif e >= 8:
                    prev_mm = f"qkmm_{t - 1}_{6 + (e % 2)}"
                    prev_ev = None
                elif e >= 2:
                    prev_mm = f"qkmm_{t}_{e - 2}"
                    prev_ev = f"qkev_{t}_{e - 2}"
                else:
                    prev_mm = f"qkmm_{t}_{14 + (e % 2)}"
                    prev_ev = None

                @blk.sync
                def _(sy, t=t, e=e, prev_mm=prev_mm):
                    if t == 0 and e in (8, 9):
                        sy.wait_ge(d_swp[0].sem, W["wp_done0"])
                        sy.wait_ge(d_swp[1].sem, W["wp_done1"])
                    prev = W.get(prev_mm, 0)
                    if prev:
                        sy.wait_ge(t_p.sem, prev)
                    d_we[e % 2].inc(sy.dma_start(
                        out=we[e % 2][:].rearrange("p (d c) -> p d c", d=8),
                        in_=wrs_h[e].rearrange("d p c -> p d c")), 16)
                    W[f"weld_{t}_{e}"] = d_we[e % 2].n

                @blk.tensor
                def _(te, t=t, e=e, prev_ev=prev_ev):
                    ps = pA if e < 8 else pD
                    te.wait_ge(d_we[e % 2].sem, W[f"weld_{t}_{e}"])
                    te.wait_ge(v_p.sem, W[f"trev_{4 * t + 3}"])
                    if e < 2:
                        te.wait_ge(c_p.sem, W.get(f"exps_done_{t - 1}", 0))
                    if prev_ev is not None and prev_ev in W:
                        te.wait_ge(v_p.sem, W[prev_ev])
                    half = (e % 2) * 512
                    for dk in range(8):
                        ins = te.matmul(ps[:, half:half + 512],
                                        we[e % 2][:, dk * 128:(dk + 1) * 128],
                                        xnT[:, dk * 512:(dk + 1) * 512],
                                        start=(dk == 0), stop=(dk == 7))
                    t_p.inc(ins)
                    W[f"qkmm_{t}_{e}"] = t_p.n

                @blk.vector
                def _(ve, t=t, e=e):
                    ps = pA if e < 8 else pD
                    ve.wait_ge(t_p.sem, W[f"qkmm_{t}_{e}"])
                    if e < 8:
                        dst = qy[:, e * 512:(e + 1) * 512]
                    else:
                        dst = kT[:, (e - 8) * T + t * 512:(e - 8) * T + t * 512 + 512]
                    half = (e % 2) * 512
                    v_p.inc(ve.tensor_scalar(dst, ps[:, half:half + 512],
                                             bT[:, e:e + 1], None, OP.add))
                    W[f"qkev_{t}_{e}"] = v_p.n
                    ve.wait_ge(v_p.sem, v_p.n)
                    v_p.inc(ve.reduce_max(gtmp[:], dst, axis=AX.X, apply_absolute_value=True))
                    ve.wait_ge(v_p.sem, v_p.n)
                    g_p.inc(ve.tensor_max(qmaxT[:, e:e + 1], qmaxT[:, e:e + 1], gtmp[:]))
                    W[f"gmax_{t}_{e}"] = g_p.n

            def emit_qk_sync(t, e):
                prev_mm = f"qkmm_{t}_{e - 2}" if e >= 10 else f"qkmm_{t - 1}_{6 + (e % 2)}"

                @blk.sync
                def _(sy, t=t, e=e, prev_mm=prev_mm):
                    if t == 0 and e in (8, 9):
                        sy.wait_ge(d_swp[0].sem, W["wp_done0"])
                        sy.wait_ge(d_swp[1].sem, W["wp_done1"])
                    prev = W.get(prev_mm, 0)
                    if prev:
                        sy.wait_ge(t_p.sem, prev)
                    d_we[e % 2].inc(sy.dma_start(
                        out=we[e % 2][:].rearrange("p (d c) -> p d c", d=8),
                        in_=wrs_h[e].rearrange("d p c -> p d c")), 16)
                    W[f"weld_{t}_{e}"] = d_we[e % 2].n

            def qk_fillers(t, e):
                prev_ev = f"qkev_{t}_{e - 2}" if e >= 10 else None

                def mk(dk, t=t, e=e, prev_ev=prev_ev):
                    def emit(te):
                        if dk == 0:
                            te.wait_ge(d_we[e % 2].sem, W[f"weld_{t}_{e}"])
                            te.wait_ge(v_p.sem, W[f"trev_{4 * t + 3}"])
                            if prev_ev is not None and prev_ev in W:
                                te.wait_ge(v_p.sem, W[prev_ev])
                        half = (e % 2) * 512
                        ins = te.matmul(pD[:, half:half + 512],
                                        we[e % 2][:, dk * 128:(dk + 1) * 128],
                                        xnT[:, dk * 512:(dk + 1) * 512],
                                        start=(dk == 0), stop=(dk == 7))
                        if dk == 7:
                            t_p.inc(ins)
                            W[f"qkmm_{t}_{e}"] = t_p.n
                    return emit
                return [mk(dk) for dk in range(8)]

            def trans_fillers(t, i):
                ti = 4 * t + i

                def mk(dk, t=t, i=i, ti=ti):
                    def emit(te):
                        if dk == 0:
                            te.wait_ge(v_p.sem, W[f"xnapply_{ti}"])
                            if i == 0:
                                te.wait_ge(v_p.sem, W[f"cbready_{t - 1}"])
                                te.wait_ge(v_p.sem, W.get(f"pjev_{t - 2}_2", 0))
                            else:
                                te.wait_ge(v_p.sem, W[f"trev_{ti - 1}"])
                        ins = te.transpose(pD[:, dk * 64:dk * 64 + 64].bitcast(BF16),
                                           xnb[i % 2][:, dk * 128:(dk + 1) * 128], identb[:])
                        if dk == 7:
                            t_p.inc(ins)
                            W[f"trmm_{ti}"] = t_p.n
                    return emit
                return [mk(dk) for dk in range(8)]

            def emit_trans_ev(t, i):
                ti = 4 * t + i

                @blk.vector
                def _(ve, t=t, i=i, ti=ti):
                    ve.wait_ge(t_p.sem, W[f"trmm_{ti}"])
                    if i == 0:
                        ve.wait_ge(d_sx.sem, W.get(f"xnst_{t - 1}", 0))
                        ve.wait_ge(t_p.sem, W.get(f"vmm_{4 * (t - 1) + 3}", 0))
                    for dk in range(8):
                        ins = ve.tensor_copy(xnT[:, dk * 512 + i * 128:dk * 512 + i * 128 + 128],
                                             pD[:, dk * 64:dk * 64 + 64].bitcast(BF16))
                    v_p.inc(ins)
                    W[f"trev_{ti}"] = v_p.n

            def emit_qk_ev(t, e):
                @blk.vector
                def _(ve, t=t, e=e):
                    ve.wait_ge(t_p.sem, W[f"qkmm_{t}_{e}"])
                    dst = kT[:, (e - 8) * T + t * 512:(e - 8) * T + t * 512 + 512]
                    half = (e % 2) * 512
                    v_p.inc(ve.tensor_scalar(dst, pD[:, half:half + 512],
                                             bT[:, e:e + 1], None, OP.add))
                    W[f"qkev_{t}_{e}"] = v_p.n
                    ve.wait_ge(v_p.sem, v_p.n)
                    v_p.inc(ve.reduce_max(gtmp[:], dst, axis=AX.X, apply_absolute_value=True))
                    ve.wait_ge(v_p.sem, v_p.n)
                    g_p.inc(ve.tensor_max(qmaxT[:, e:e + 1], qmaxT[:, e:e + 1], gtmp[:]))
                    W[f"gmax_{t}_{e}"] = g_p.n

            def emit_v_part_i(t, i):
                ti = 4 * t + i

                @blk.tensor
                def _(te, t=t, i=i, ti=ti):
                    if i == 0:
                        te.wait_ge(c_p.sem, W.get(f"exps_done_{t - 1}", 0))
                    else:
                        te.wait_ge(v_p.sem, W.get(f"vev_{ti - 1}", 0))
                    for dk in range(8):
                        lhs = xnT[:, dk * 512 + i * 128:dk * 512 + i * 128 + 128]
                        te.matmul(pB[:, 0:512], lhs, wv[:, dk * 1024:dk * 1024 + 512],
                                  start=(dk == 0), stop=(dk == 7))
                        ins = te.matmul(pB[:, 512:1024], lhs, wv[:, dk * 1024 + 512:dk * 1024 + 1024],
                                        start=(dk == 0), stop=(dk == 7))
                    t_p.inc(ins)
                    W[f"vmm_{ti}"] = t_p.n

                @blk.vector
                def _(ve, ti=ti):
                    ve.wait_ge(t_p.sem, W[f"vmm_{ti}"])
                    dst = vP[:, ti * NH * VST:(ti + 1) * NH * VST]
                    dst3 = dst.rearrange("p (h s) -> p h s", s=VST)[:, :, 0:64]
                    src3 = pB[:].rearrange("p (h s) -> p h s", s=64)
                    v_p.inc(ve.tensor_copy(dst3, src3))
                    W[f"vev_{ti}"] = v_p.n

            def emit_cb(t):
                @blk.vector
                def _(ve, t=t):
                    ve.wait_ge(g_p.sem, W[f"gmax_{t}_7"])
                    ve.wait_ge(t_p.sem, W.get(f"cbm_{t - 1}", 0))
                    v_p.inc(ve.tensor_mul(prodall[:], qmaxT[:, 0:8], qmaxT[:, 8:16]))
                    W[f"cbp_{t}"] = v_p.n

                @blk.tensor
                def _(te, t=t):
                    te.wait_ge(v_p.sem, W[f"cbp_{t}"])
                    t_p.inc(te.matmul(pD[:, 16:24], ones128[:], prodall[:],
                                      start=True, stop=True))
                    W[f"cbm_{t}"] = t_p.n

                @blk.vector
                def _(ve, t=t):
                    ve.wait_ge(t_p.sem, W[f"cbm_{t}"])
                    v_p.inc(ve.tensor_scalar_mul(cbt[:], pD[:, 16:24], -0.0625))
                    W[f"cbready_{t}"] = v_p.n

            def emit_recip(tb):
                @blk.vector
                def _(ve, tb=tb):
                    ve.wait_ge(t_p.sem, W.get(f"rbmm_{tb - 1}_7", 0))
                    ve.wait_ge(d_sl.sem, W[f"lst_{tb}_7"])
                    ve.reciprocal(Lr[:], Lr[:])
                    v_p.inc(ve.tensor_copy(Rb[:], Lr[:]))
                    W[f"rb_{tb}"] = v_p.n

            def emit_ynorm(tb, yc):
                @blk.tensor
                def _(te, tb=tb, yc=yc):
                    reg = (pD if yc % 4 < 2 else pC)
                    off = (yc % 2) * 512
                    te.wait_ge(v_p.sem, W[f"rb_{tb}"])
                    if yc >= 4:
                        te.wait_ge(v_p.sem, W[f"ynorm_{tb}_{yc - 4}"])
                    t_p.inc(te.matmul(reg[:, off:off + 512],
                                      sel16[:, yc * 128:(yc + 1) * 128], Rb[:],
                                      start=True, stop=True))
                    W[f"rbmm_{tb}_{yc}"] = t_p.n

                @blk.vector
                def _(ve, tb=tb, yc=yc):
                    reg = (pD if yc % 4 < 2 else pC)
                    off = (yc % 2) * 512
                    ve.wait_ge(t_p.sem, W[f"rbmm_{tb}_{yc}"])
                    v_p.inc(ve.tensor_mul(yT[:, yc * 512:(yc + 1) * 512],
                                          yT[:, yc * 512:(yc + 1) * 512],
                                          reg[:, off:off + 512]))
                    W[f"ynorm_{tb}_{yc}"] = v_p.n

            def emit_proj(tb):
                @blk.sync
                def _(sy, tb=tb):
                    t_base = t_p.n
                    if tb == 0:
                        sy.wait_ge(d_wrs.sem, W["wrs_done"])
                    for cc in range(8):
                        if cc < 2:
                            sy.wait_ge(t_p.sem, last_wpb[cc])
                        else:
                            sy.wait_ge(t_p.sem, t_base + 4 * (cc - 1))
                        d_wp[cc % 2].inc(sy.dma_start(out=wpb[cc % 2][:], in_=wp_s[cc]), 16)
                        W[f"wpld_{tb}_{cc}"] = d_wp[cc % 2].n

                @blk.tensor
                def _(te, tb=tb):
                    if f"cbready_{tb + 1}" in W:
                        te.wait_ge(v_p.sem, W[f"cbready_{tb + 1}"])
                    else:
                        te.wait_ge(c_p.sem, W[f"exps_done_{tb}"])
                        te.wait_ge(v_p.sem, W[f"ynorm_{tb}_7"])
                    pjs = (pA, pB, pD, pC)
                    for cc in range(8):
                        te.wait_ge(d_wp[cc % 2].sem, W[f"wpld_{tb}_{cc}"])
                        for i in range(4):
                            pj = pjs[i]
                            te.matmul(pj[:, 0:512],
                                      yT[:, cc * 512 + i * 128:cc * 512 + i * 128 + 128],
                                      wpb[cc % 2][:, 0:512], start=(cc == 0), stop=False)
                            t_p.inc(te.matmul(pj[:, 512:1024],
                                              yT[:, cc * 512 + i * 128:cc * 512 + i * 128 + 128],
                                              wpb[cc % 2][:, 512:1024], start=(cc == 0), stop=False))
                        last_wpb[cc % 2] = t_p.n
                    for i in range(4):
                        pj = pjs[i]
                        te.matmul(pj[:, 0:512], onesb[:], bpp[0:1, 0:512], start=False, stop=True)
                        t_p.inc(te.matmul(pj[:, 512:1024], onesb[:], bpp[0:1, 512:1024],
                                          start=False, stop=True))
                    W[f"pjmm_{tb}"] = t_p.n

                for i, pj in ((0, pA), (1, pB), (2, pD), (3, pC)):
                    @blk.vector
                    def _(ve, tb=tb, i=i, pj=pj):
                        ve.wait_ge(t_p.sem, W[f"pjmm_{tb}"])
                        ve.wait_ge(d_ss.sem, W.get(f"sst_{tb}_{i - 1}", 0))
                        v_p.inc(ve.tensor_copy(sev0[:], pj[:]))
                        W[f"pjev_{tb}_{i}"] = v_p.n

                    @blk.gpsimd
                    def _(gp, tb=tb, i=i):
                        gp.wait_ge(v_p.sem, W[f"pjev_{tb}_{i}"])
                        d_ss.inc(gp.dma_start(out=s_s[4 * tb + i], in_=sev0[:]), 16)
                        W[f"sst_{tb}_{i}"] = d_ss.n

                W[f"pjev_done_{tb}"] = v_p.n

            wconv_chunks = []
            for kind, n_tiles, nh in (("g", 8, 3), ("u", 8, 3), ("d", 24, 1)):
                for j in range(n_tiles):
                    for th in range(nh):
                        wconv_chunks.append((kind, j, th))
            NWH = 2 * len(wconv_chunks)

            def emit_wconv(sl):
                if sl < NWH or 0 <= sl - 2 < NWH:
                    @blk.gpsimd
                    def _(gp, sl=sl):
                        if 0 <= sl - 2 < NWH:
                            k = sl - 2
                            gp.wait_ge(v_p.sem, W[f"wcv_{k}"])
                            d_wcs[k % 2].inc(gp.dma_start(
                                out=wbs_h[k // 2, :, (k % 2) * 512:(k % 2) * 512 + 512],
                                in_=wcvb[k % 2][:]), 16)
                            W[f"wcs_{k}"] = d_wcs[k % 2].n
                        if sl < NWH:
                            kind, j, th = wconv_chunks[sl // 2]
                            src = {"g": wgT_h, "u": wuT_h, "d": wdT_h}[kind]
                            c0 = th * 1024 + (sl % 2) * 512
                            if sl < 2:
                                gp.wait_ge(d_wrs.sem, W["wrs_done"])
                            if sl >= 2:
                                gp.wait_ge(v_p.sem, W[f"wcv_{sl - 2}"])
                            d_wcl[sl % 2].inc(gp.dma_start(
                                out=wcb[sl % 2][:],
                                in_=src[j * 128:(j + 1) * 128, c0:c0 + 512]), 16)
                            W[f"wcl_{sl}"] = d_wcl[sl % 2].n

                if 0 <= sl - 1 < NWH:
                    @blk.vector
                    def _(ve, sl=sl - 1):
                        ve.wait_ge(d_wcl[sl % 2].sem, W[f"wcl_{sl}"])
                        if sl >= 2:
                            ve.wait_ge(d_wcs[sl % 2].sem, W[f"wcs_{sl - 2}"])
                        v_p.inc(ve.tensor_copy(wcvb[sl % 2][:], wcb[sl % 2][:]))
                        W[f"wcv_{sl}"] = v_p.n

            pre_c0 = c_p.n

            def c_rstd0(i, pre_c0=pre_c0):
                return pre_c0 + 2 * (i + 1)

            W["cbready_-1"] = W["initdone"]
            emit_ln_tile(0, 0, c_rstd0)
            emit_trans_tile(0, 0)
            emit_ln_tile(0, 1, c_rstd0)
            emit_trans_tile(0, 1)
            emit_ln_tile(0, 2, c_rstd0)
            emit_trans_tile(0, 2)
            emit_ln_tile(0, 3, c_rstd0)
            emit_trans_tile(0, 3)

            @blk.scalar
            def _(sc, pre_c0=pre_c0):
                for i in range(4):
                    L = 16 * i
                    sc.wait_ge(v_p.sem, W[f"var_{i}"])
                    c_p.inc(sc.activation(lnsc[:, L + 14:L + 15], lnsc[:, L + 13:L + 14],
                                          AF.Ln, bias=lnsc[:, 64:65]))
                    sc.wait_ge(c_p.sem, c_p.n)
                    c_p.inc(sc.activation(lnsc[:, L + 15:L + 16], lnsc[:, L + 14:L + 15],
                                          AF.Exp, scale=-0.5))
                    assert c_p.n == pre_c0 + 2 * (i + 1)

            @blk.gpsimd
            def _(gp):
                gp.wait_ge(v_p.sem, W["trev_3"])
                d_sx.inc(gp.dma_start(out=xnT_s[0], in_=xnT[:]), 16)
                W["xnst_0"] = d_sx.n

            emit_ln_load(1, 0)
            emit_ln_load(1, 1)
            for e in range(8, 16):
                emit_qk_chunk(0, e)

            for t in range(NT):
                nkc = 4 * t + 4
                has_next = t + 1 < NT
                R = 4 if has_next else 0
                c_base_t = c_p.n

                K0 = min(6, 4 * t + 2, nkc)

                def c_exp_done(c, kc, c_base_t=c_base_t, nkc=nkc, R=R, K0=K0):
                    return (c_base_t + 2 * min(c, R) + c * 2 * nkc + 2 * (kc + 1)
                            + (2 if (c < R and kc >= K0) else 0))

                def c_rstd_next(i, c_base_t=c_base_t, nkc=nkc, R=R, K0=K0):
                    return c_base_t + 2 * min(i, R) + i * 2 * nkc + 2 * K0 + 2

                emit_wconv(18 * t)
                tb = t - 1
                if t >= 1:
                    emit_recip(tb)
                emit_qk_chunk(t, 0)
                emit_qk_chunk(t, 1)
                if t >= 1:
                    emit_ynorm(tb, 0)
                    emit_ynorm(tb, 1)
                emit_qk_chunk(t, 2)
                emit_qk_chunk(t, 3)
                if t >= 1:
                    emit_ynorm(tb, 2)
                    emit_ynorm(tb, 3)
                emit_qk_chunk(t, 4)
                emit_qk_chunk(t, 5)
                if t >= 1:
                    emit_ynorm(tb, 4)
                    emit_ynorm(tb, 5)
                emit_qk_chunk(t, 6)
                emit_qk_chunk(t, 7)
                if t >= 1:
                    emit_ynorm(tb, 6)
                    emit_ynorm(tb, 7)
                emit_v_part_i(t, 0)
                emit_v_part_i(t, 1)
                emit_v_part_i(t, 2)
                emit_v_part_i(t, 3)
                emit_cb(t)
                if has_next:
                    emit_ln_stats(t + 1, 0)
                if t >= 1:
                    emit_proj(tb)
                if has_next:
                    emit_ln_apply(t + 1, 0, c_rstd_next)
                emit_wconv(18 * t + 1)

                for c in range(8):
                    emit_wconv(18 * t + 2 + 2 * c)
                    if has_next and 1 <= c <= 2:
                        emit_ln_load(t + 1, c + 1)
                    if t + 2 < NT and 6 <= c:
                        emit_ln_load(t + 2, c - 6)
                    if has_next and 1 <= c < 4:
                        emit_ln_calc(t + 1, c, c_rstd_next)
                    fillers = []
                    if has_next:
                        if c < 4:
                            fillers = trans_fillers(t + 1, c)
                        else:
                            for e in (8 + 2 * (c - 4), 9 + 2 * (c - 4)):
                                emit_qk_sync(t + 1, e)
                                fillers += qk_fillers(t + 1, e)
                    v_base = v_p.n

                    def v_mask_done(kc, v_base=v_base, t=t):
                        return v_base + (kc - 4 * t) + 1

                    def mk_mms(te, t=t, c=c, nkc=nkc, c_exp_done=c_exp_done,
                               v_mask_done=v_mask_done):
                        def mm1(kc):
                            b = kc % 2
                            q0 = 128 * (kc - 4 * t) if kc >= 4 * t else 0
                            if kc >= 2:
                                te.wait_ge(c_p.sem, c_exp_done(c, kc - 2))
                            else:
                                te.wait_ge(c_p.sem, lastS[("A", b)])
                                te.wait_ge(c_p.sem, lastS[("B", b)])
                                if c == 0:
                                    te.wait_ge(v_p.sem, W[f"cbready_{t}"])
                                    te.wait_ge(v_p.sem, W.get(f"pjev_{t - 1}_1", 0))
                            te.matmul(pA[:, b * 512 + q0:b * 512 + 512],
                                      kT[0:64, c * T + kc * 128:c * T + kc * 128 + 128],
                                      qy[0:64, c * 512 + q0:(c + 1) * 512],
                                      start=True, stop=True)
                            t_p.inc(te.matmul(pB[:, b * 512 + q0:b * 512 + 512],
                                              kT[64:128, c * T + kc * 128:c * T + kc * 128 + 128],
                                              qy[64:128, c * 512 + q0:(c + 1) * 512],
                                              start=True, stop=True))
                            W[f"mm1_{t}_{c}_{kc}"] = t_p.n

                        def mm2(kc):
                            b = kc % 2
                            diag = kc >= 4 * t
                            q0 = 128 * (kc - 4 * t) if diag else 0
                            te.wait_ge(c_p.sem, c_exp_done(c, kc))
                            if diag:
                                te.wait_ge(v_p.sem, v_mask_done(kc))
                            if kc == 0 and c == 0:
                                te.wait_ge(v_p.sem, W[f"vev_{4 * t + 3}"])
                                te.wait_ge(v_p.sem, W[f"pjev_done_{t - 1}"])
                            if kc == 0:
                                te.wait_ge(v_p.sem, W.get(f"yev_{t}_{c - 1}", W[f"yev_done_{t - 1}"]))
                            vbase = kc * NH * VST
                            last = (kc == nkc - 1)
                            te.matmul(pC[0:65, q0:512],
                                      vP[:, vbase + (2 * c) * VST:vbase + (2 * c) * VST + 65],
                                      pP[0 + b][:, q0:512], start=(kc == 0), stop=last)
                            t_p.inc(te.matmul(pC[0:65, 512 + q0:1024],
                                              vP[:, vbase + (2 * c + 1) * VST:vbase + (2 * c + 1) * VST + 65],
                                              pP[2 + b][:, q0:512], start=(kc == 0), stop=last))
                            W[f"mm2_{t}_{c}_{kc}"] = t_p.n
                        return mm1, mm2

                    fst = 5 if c < 4 else 3

                    @blk.tensor
                    def _(te, mk_mms=mk_mms, nkc=nkc, fillers=fillers, fst=fst):
                        mm1, mm2 = mk_mms(te)
                        fi = 0
                        for kc in range(nkc):
                            mm1(kc)
                            if 1 <= kc <= nkc - 2:
                                mm2(kc - 1)
                            while fi < len(fillers) and fi <= 2 * (kc - fst):
                                fillers[fi](te)
                                fi += 1
                        while fi < len(fillers):
                            fillers[fi](te)
                            fi += 1

                    assert v_p.n == v_base, (t, c, v_p.n, v_base)

                    @blk.tensor
                    def _(te, mk_mms=mk_mms, t=t, c=c, nkc=nkc):
                        mm1, mm2 = mk_mms(te)
                        mm2(nkc - 2)
                        mm2(nkc - 1)
                        W[f"mm2last_{t}_{c}"] = t_p.n

                    @blk.vector
                    def _(ve, t=t, c=c, nkc=nkc, c_exp_done=c_exp_done):
                        for kc in range(4 * t, nkc):
                            b = kc % 2
                            q0 = 128 * (kc - 4 * t)
                            ve.wait_ge(c_p.sem, c_exp_done(c, kc))
                            ve.tensor_mul(pP[0 + b][:, q0:q0 + 128], pP[0 + b][:, q0:q0 + 128], mask[:])
                            v_p.inc(ve.tensor_mul(pP[2 + b][:, q0:q0 + 128],
                                                  pP[2 + b][:, q0:q0 + 128], mask[:]))
                        lrowA = sev0[0:1, :].bitcast(F32)
                        lrowB = sev0[64:65, :].bitcast(F32)
                        ve.wait_ge(t_p.sem, W[f"mm2last_{t}_{c}"])
                        if c == 0:
                            ve.wait_ge(t_p.sem, W.get(f"pjmm_{t - 1}", 0))
                        ve.wait_ge(d_sl.sem, W.get(f"lst_{t}_{c - 1}", W.get(f"lst_{t - 1}_7", 0)))
                        if c == 0:
                            ve.wait_ge(d_ss.sem, W.get(f"sst_{t - 1}_3", 0))
                        ve.tensor_copy(yT[0:64, c * 512:(c + 1) * 512], pC[0:64, 0:512])
                        ve.tensor_copy(yT[64:128, c * 512:(c + 1) * 512], pC[0:64, 512:1024])
                        ve.tensor_copy(lrowA[0:1, 0:512], pC[64:65, 0:512])
                        v_p.inc(ve.tensor_copy(lrowB[0:1, 0:512], pC[64:65, 512:1024]))
                        W[f"yev_{t}_{c}"] = v_p.n

                    @blk.gpsimd
                    def _(gp, t=t, c=c):
                        lrowA = sev0[0:1, :].bitcast(F32)
                        lrowB = sev0[64:65, :].bitcast(F32)
                        gp.wait_ge(v_p.sem, W[f"yev_{t}_{c}"])
                        d_sl.inc(gp.dma_start(out=Lr[2 * c:2 * c + 1, :], in_=lrowA[0:1, 0:512]), 16)
                        d_sl.inc(gp.dma_start(out=Lr[2 * c + 1:2 * c + 2, :], in_=lrowB[0:1, 0:512]), 16)
                        W[f"lst_{t}_{c}"] = d_sl.n

                    if has_next:
                        if c < 4:
                            emit_trans_ev(t + 1, c)
                            if c == 3:
                                @blk.gpsimd
                                def _(gp, t=t):
                                    gp.wait_ge(v_p.sem, W[f"trev_{4 * (t + 1) + 3}"])
                                    d_sx.inc(gp.dma_start(out=xnT_s[t + 1], in_=xnT[:]), 16)
                                    W[f"xnst_{t + 1}"] = d_sx.n
                        else:
                            emit_qk_ev(t + 1, 8 + 2 * (c - 4))
                            emit_qk_ev(t + 1, 9 + 2 * (c - 4))

                    emit_wconv(18 * t + 3 + 2 * c)
                    lastS[("A", 0)] = lastS[("B", 0)] = c_exp_done(c, nkc - 2)
                    lastS[("A", 1)] = lastS[("B", 1)] = c_exp_done(c, nkc - 1)
                W[f"yev_done_{t}"] = v_p.n

                @blk.scalar
                def _(sc, t=t, nkc=nkc, R=R, c_base_t=c_base_t, K0=K0,
                      c_exp_done=c_exp_done, c_rstd_next=c_rstd_next):
                    for c in range(8):
                        for kc in range(nkc):
                            b = kc % 2
                            q0 = 128 * (kc - 4 * t) if kc >= 4 * t else 0
                            sc.wait_ge(t_p.sem, W[f"mm1_{t}_{c}_{kc}"])
                            if kc >= 2:
                                sc.wait_ge(t_p.sem, W[f"mm2_{t}_{c}_{kc - 2}"])
                            c_p.inc(sc.activation(pP[0 + b][:, q0:512], pA[:, b * 512 + q0:b * 512 + 512],
                                                  AF.Exp, bias=cbt[:, c:c + 1], scale=0.125))
                            c_p.inc(sc.activation(pP[2 + b][:, q0:512], pB[:, b * 512 + q0:b * 512 + 512],
                                                  AF.Exp, bias=cbt[:, c:c + 1], scale=0.125))
                            if c < R and kc == K0 - 1:
                                i = c
                                L = 16 * i
                                sc.wait_ge(v_p.sem, W[f"var_{4 * (t + 1) + i}"])
                                c_p.inc(sc.activation(lnsc[:, L + 14:L + 15], lnsc[:, L + 13:L + 14],
                                                      AF.Ln, bias=lnsc[:, 64:65]))
                                sc.wait_ge(c_p.sem, c_p.n)
                                c_p.inc(sc.activation(lnsc[:, L + 15:L + 16], lnsc[:, L + 14:L + 15],
                                                      AF.Exp, scale=-0.5))
                                assert c_p.n == c_rstd_next(i)
                            assert c_p.n == c_exp_done(c, kc) + (
                                2 if (c < R and kc == K0 - 1) else 0)

                W[f"exps_done_{t}"] = c_p.n

            emit_wconv(NWH)
            emit_wconv(NWH + 1)
            emit_recip(NT - 1)
            for yc in range(8):
                emit_ynorm(NT - 1, yc)
            emit_proj(NT - 1)

            AB_T, AB_V, AB_C = t_p.n, v_p.n, c_p.n
            AB_SX, AB_SS, AB_SL = d_sx.n, d_ss.n, d_sl.n
            AB_WCS = [d_wcs[0].n, d_wcs[1].n]

        cs = contextlib.ExitStack()
        with cs:
            def sbc(name, shape, dt):
                return cs.enter_context(nc.sbuf_tensor(name, shape, dt))

            wg = [sbc(f"wg{d}", [128, HID], BF16) for d in range(8)]
            wu = [sbc(f"wu{d}", [128, HID], BF16) for d in range(8)]
            wd = [sbc(f"wd{h}", [128, D], BF16) for h in range(24)]
            xnC = sbc("xnC", [128, T], BF16)
            hT = [sbc(f"hT{h}", [128, 512], BF16) for h in range(24)]
            xc = [sbc(f"xc{i}", [128, D], F32) for i in range(2)]
            scb_ = [sbc(f"scb{i}", [128, D], BF16) for i in range(2)]
            gs = [sbc("gs0", [128, D], F32)] * 2
            sgb = [sbc("sgb0", [128, 512], F32)] * 2

            pG = [cs.enter_context(nc.psum_tensor(f"pG{i}", [128, 512], F32)) for i in range(2)]
            pU = [cs.enter_context(nc.psum_tensor(f"pU{i}", [128, 512], F32)) for i in range(2)]
            pDn = [cs.enter_context(nc.psum_tensor(f"pDn{i}", [128, 1024], F32)) for i in range(2)]

            def cbar(eng):
                eng.wait_ge(t_p.sem, AB_T)
                eng.wait_ge(v_p.sem, AB_V)
                eng.wait_ge(c_p.sem, AB_C)
                eng.wait_ge(d_sx.sem, AB_SX)
                eng.wait_ge(d_ss.sem, AB_SS)
                eng.wait_ge(d_sl.sem, AB_SL)

            @blk.sync
            def _(sy):
                cbar(sy)
                sy.wait_ge(d_wcs[0].sem, AB_WCS[0])
                sy.wait_ge(d_wcs[1].sem, AB_WCS[1])
                for dk in range(8):
                    d_cw.inc(sy.dma_start(
                        out=wg[dk][:].rearrange("p (c x) -> p c x", c=3),
                        in_=wbs_h[3 * dk:3 * dk + 3].rearrange("c p x -> p c x")), 16)
                W["wg_ld"] = d_cw.n
                for dk in range(8):
                    d_cw.inc(sy.dma_start(
                        out=wu[dk][:].rearrange("p (c x) -> p c x", c=3),
                        in_=wbs_h[24 + 3 * dk:24 + 3 * dk + 3].rearrange("c p x -> p c x")), 16)
                W["wu_ld"] = d_cw.n
                for h in range(24):
                    d_cw.inc(sy.dma_start(out=wd[h][:], in_=wbs_h[48 + h]), 16)
                W["wd_ld"] = d_cw.n

            dn_last = {}
            for tc in range(NT):
                c_tc0 = c_p.n
                v_tc0 = v_p.n

                def c_sil(hc, c_tc0=c_tc0):
                    return c_tc0 + hc + 1

                def c_gsil(i, c_tc0=c_tc0):
                    return c_tc0 + 24 + i + 1

                def v_hm(hc, v_tc0=v_tc0):
                    return v_tc0 + hc + 1

                def v_fin(i, v_tc0=v_tc0):
                    return v_tc0 + 24 + i + 1

                dn_prev = dict(dn_last)
                so_tc0 = [d_so[0].n, d_so[1].n]

                @blk.sync
                def _(sy, tc=tc):
                    if tc == 0:
                        cbar(sy)
                    else:
                        sy.wait_ge(t_p.sem, W[f"gumm_{tc - 1}_23"])
                    d_cx.inc(sy.dma_start(out=xnC[:], in_=xnT_s[tc]), 16)
                    W[f"xnld_{tc}"] = d_cx.n

                for i in range(4):
                    ti = 4 * tc + i

                    @blk.sync
                    def _(sy, tc=tc, i=i, ti=ti, v_fin=v_fin):
                        if ti >= 2:
                            if i < 2:
                                sy.wait_ge(v_p.sem, W[f"fin_{ti - 2}"])
                            else:
                                sy.wait_ge(v_p.sem, v_fin(i - 2))
                        d_cs[i % 2].inc(sy.dma_start(out=xc[i % 2][:], in_=x_h[ti * 128:(ti + 1) * 128, :]), 16)
                        d_cs[i % 2].inc(sy.dma_start(out=scb_[i % 2][:], in_=s_s[ti]), 16)
                        W[f"xsld_{ti}"] = d_cs[i % 2].n

                for hc in range(24):
                    @blk.tensor
                    def _(te, tc=tc, hc=hc, c_sil=c_sil, v_hm=v_hm):
                        if tc == 0 and hc == 0:
                            cbar(te)
                            te.wait_ge(d_cw.sem, W["wg_ld"])
                        te.wait_ge(d_cx.sem, W[f"xnld_{tc}"])
                        if hc >= 2:
                            te.wait_ge(c_p.sem, c_sil(hc - 2))
                            te.wait_ge(v_p.sem, v_hm(hc - 2))
                        elif tc >= 1:
                            te.wait_ge(c_p.sem, W[f"sil_{tc - 1}_{22 + (hc % 2)}"])
                            te.wait_ge(v_p.sem, W[f"hm_{tc - 1}_{22 + (hc % 2)}"])
                        for dk in range(8):
                            te.matmul(pG[hc % 2][:], wg[dk][:, hc * 128:(hc + 1) * 128],
                                      xnC[:, dk * 512:(dk + 1) * 512],
                                      start=(dk == 0), stop=(dk == 7))
                        if tc == 0 and hc == 0:
                            te.wait_ge(d_cw.sem, W["wu_ld"])
                        for dk in range(8):
                            ins = te.matmul(pU[hc % 2][:], wu[dk][:, hc * 128:(hc + 1) * 128],
                                            xnC[:, dk * 512:(dk + 1) * 512],
                                            start=(dk == 0), stop=(dk == 7))
                        t_p.inc(ins)
                        W[f"gumm_{tc}_{hc}"] = t_p.n

                for i in range(4):
                    ti = 4 * tc + i

                    @blk.tensor
                    def _(te, tc=tc, i=i, ti=ti, v_hm=v_hm, c_gsil=c_gsil):
                        if tc == 0 and i == 0:
                            te.wait_ge(d_cw.sem, W["wd_ld"])
                        if i >= 2:
                            te.wait_ge(c_p.sem, c_gsil(i - 2))
                        elif tc >= 1:
                            te.wait_ge(c_p.sem, W[f"gsil_{4 * tc + i - 2}"])
                        for hc in range(24):
                            te.wait_ge(v_p.sem, v_hm(hc))
                            te.matmul(pDn[i % 2][:, 0:512], hT[hc][:, i * 128:(i + 1) * 128],
                                      wd[hc][:, 0:512], start=(hc == 0), stop=(hc == 23))
                            t_p.inc(te.matmul(pDn[i % 2][:, 512:1024], hT[hc][:, i * 128:(i + 1) * 128],
                                              wd[hc][:, 512:1024], start=(hc == 0), stop=(hc == 23)))
                            if i == 3:
                                dn_last[hc] = t_p.n
                        W[f"dnmm_{ti}"] = t_p.n

                @blk.scalar
                def _(sc, tc=tc, c_tc0=c_tc0, v_hm=v_hm, v_tc0=v_tc0, so_tc0=so_tc0):
                    if tc == 0:
                        cbar(sc)
                    for hc in range(24):
                        sc.wait_ge(t_p.sem, W[f"gumm_{tc}_{hc}"])
                        if hc >= 1:
                            sc.wait_ge(v_p.sem, v_hm(hc - 1))
                        elif tc >= 1:
                            sc.wait_ge(v_p.sem, W[f"hm_{tc - 1}_23"])
                        c_p.inc(sc.activation(sgb[hc % 2][:], pG[hc % 2][:], AF.Silu, scale=INV127))
                        W[f"sil_{tc}_{hc}"] = c_p.n
                    for i in range(4):
                        ti = 4 * tc + i
                        sc.wait_ge(t_p.sem, W[f"dnmm_{ti}"])
                        if i >= 1:
                            sc.wait_ge(v_p.sem, v_tc0 + 24 + (i - 1) + 1)
                            sc.wait_ge(d_so[(i - 1) % 2].sem, so_tc0[(i - 1) % 2] + 16 * ((i - 1) // 2 + 1))
                        elif ti >= 1:
                            sc.wait_ge(v_p.sem, W[f"fin_{ti - 1}"])
                            sc.wait_ge(d_so[(i - 1) % 2].sem, W.get(f"outst_{ti - 1}", 0))
                        c_p.inc(sc.activation(gs[i % 2][:], pDn[i % 2][:], AF.Silu,
                                              scale=INV127 * INV127))
                        W[f"gsil_{ti}"] = c_p.n

                @blk.vector
                def _(ve, tc=tc, c_sil=c_sil, c_gsil=c_gsil, dn_prev=dn_prev, so_tc0=so_tc0):
                    if tc == 0:
                        cbar(ve)
                    for hc in range(24):
                        ve.wait_ge(c_p.sem, c_sil(hc))
                        ve.wait_ge(t_p.sem, dn_prev.get(hc, 0))
                        v_p.inc(ve.tensor_mul(hT[hc][:], sgb[hc % 2][:], pU[hc % 2][:]))
                        W[f"hm_{tc}_{hc}"] = v_p.n
                    for i in range(4):
                        ti = 4 * tc + i
                        ve.wait_ge(c_p.sem, c_gsil(i))
                        ve.wait_ge(d_cs[i % 2].sem, W[f"xsld_{ti}"])
                        if i >= 2:
                            ve.wait_ge(d_so[i % 2].sem, so_tc0[i % 2] + 16)
                        else:
                            ve.wait_ge(d_so[i % 2].sem, W.get(f"outst_{ti - 2}", 0))
                        ve.tensor_mul(gs[i % 2][:], gs[i % 2][:], scb_[i % 2][:])
                        v_p.inc(ve.tensor_add(gs[i % 2][:], gs[i % 2][:], xc[i % 2][:]))
                        W[f"fin_{ti}"] = v_p.n

                for i in range(4):
                    ti = 4 * tc + i

                    @blk.gpsimd
                    def _(gp, ti=ti, i=i):
                        if ti == 0:
                            cbar(gp)
                        gp.wait_ge(v_p.sem, W[f"fin_{ti}"])
                        d_so[i % 2].inc(gp.dma_start(out=out_h[ti * 128:(ti + 1) * 128, :], in_=gs[i % 2][:]), 16)
                        W[f"outst_{ti}"] = d_so[i % 2].n

            @blk.gpsimd
            def _(gp):
                gp.wait_ge(d_so[0].sem, d_so[0].n)
                gp.wait_ge(d_so[1].sem, d_so[1].n)

    return nc


def _host_consts():
    hd = 64
    theta = 1.0 / (10000.0 ** (np.arange(0, hd, 2, dtype=np.float64) / hd))
    h_idx = np.arange(NH, dtype=np.float64)
    ang = h_idx[:, None] * theta[None, :]
    cos_t = np.cos(ang)
    sin_t = np.sin(ang)
    row = cos_t.reshape(-1)
    cosrow = np.concatenate([row, row]).reshape(1, D).astype(ml_dtypes.bfloat16)
    row = sin_t.reshape(-1)
    sinrow = np.concatenate([row, row]).reshape(1, D).astype(ml_dtypes.bfloat16)
    ident = np.zeros((40, 8), dtype=np.float32)
    ident[0:8, 0:8] = np.eye(8)
    ident[32:40, 0:8] = np.eye(8)
    identb = np.eye(128, dtype=ml_dtypes.bfloat16)
    onesb = np.ones((1, 128), dtype=ml_dtypes.bfloat16)
    onescol = np.ones((128, 1), dtype=ml_dtypes.bfloat16)
    kk = np.arange(128)[:, None]
    qq = np.arange(128)[None, :]
    masktri = (qq >= kk).astype(ml_dtypes.bfloat16)
    sel16 = np.zeros((16, NH * 64), dtype=ml_dtypes.bfloat16)
    for c in range(8):
        for m in range(128):
            sel16[2 * c + (m >= 64), c * 128 + m] = 1
    return cosrow, sinrow, ident, identb, onesb, onescol, masktri, sel16


def kernel(L0_attn_x, L0_attn_W_attn, L0_attn_b_attn, L0_attn_W_proj,
           L0_attn_b_proj, L0_ffn_w_gate, L0_ffn_w_up, L0_ffn_w_down):
    x = np.ascontiguousarray(np.asarray(L0_attn_x, dtype=np.float32))
    wat = np.ascontiguousarray(np.asarray(L0_attn_W_attn, dtype=np.float32))
    ba = np.ascontiguousarray(np.asarray(L0_attn_b_attn, dtype=np.float32)).reshape(1, 3 * D)
    wp = np.ascontiguousarray(np.asarray(L0_attn_W_proj, dtype=np.float32))
    bp = np.ascontiguousarray(np.asarray(L0_attn_b_proj, dtype=np.float32)).reshape(1, D)
    wgT = np.ascontiguousarray(np.asarray(L0_ffn_w_gate, dtype=np.int32).T)
    wuT = np.ascontiguousarray(np.asarray(L0_ffn_w_up, dtype=np.int32).T)
    wdT = np.ascontiguousarray(np.asarray(L0_ffn_w_down, dtype=np.int32).T)
    cosrow, sinrow, ident, identb, onesb, onescol, masktri, sel16 = _host_consts()

    nc = build_program()
    shared = dict(w_attn=wat, b_attn=ba, w_proj=wp, b_proj=bp,
                  wg_t=wgT, wu_t=wuT, wd_t=wdT,
                  cosrow=cosrow, sinrow=sinrow, ident=ident, identb=identb,
                  onesb=onesb, onescol=onescol, masktri=masktri, sel16=sel16)
    in_maps = [dict(x=np.ascontiguousarray(x[b]), **shared) for b in range(8)]
    res = run_bass_kernel_spmd(nc, in_maps, core_ids=list(range(8)))
    out = np.stack([res.results[b]["out"] for b in range(8)], axis=0)
    return out.astype(np.float32)


# version 25
# speedup vs baseline: 1.1013x; 1.0164x over previous
import contextlib
import math
import numpy as np
import ml_dtypes
import concourse.bass as bass
import concourse.mybir as mybir
from concourse.bass_utils import run_bass_kernel_spmd

F32 = mybir.dt.float32
BF16 = mybir.dt.bfloat16
I32 = mybir.dt.int32
AF = mybir.ActivationFunctionType
AX = mybir.AxisListType
OP = mybir.AluOpType

T = 4096
D = 1024
NH = 16
NT = 8
HID = 3072
VST = 65
INV127 = 1.0 / 127.0


class Ctr:
    def __init__(self, ctx, nc, name):
        self.sem = ctx.enter_context(nc.semaphore(name))
        self.n = 0

    def inc(self, instr, k=1):
        self.n += k
        instr.then_inc(self.sem, k)
        return instr


def build_program(dbg=False):
    nc = bass.Bass("TRN2", target_bir_lowering=False, debug=False)

    x_h = nc.dram_tensor("x", [T, D], F32, kind="ExternalInput").ap()
    wat_h = nc.dram_tensor("w_attn", [D, 3 * D], F32, kind="ExternalInput").ap()
    ba_h = nc.dram_tensor("b_attn", [1, 3 * D], F32, kind="ExternalInput").ap()
    wp_h = nc.dram_tensor("w_proj", [D, D], F32, kind="ExternalInput").ap()
    bp_h = nc.dram_tensor("b_proj", [1, D], F32, kind="ExternalInput").ap()
    wgT_h = nc.dram_tensor("wg_t", [D, HID], I32, kind="ExternalInput").ap()
    wuT_h = nc.dram_tensor("wu_t", [D, HID], I32, kind="ExternalInput").ap()
    wdT_h = nc.dram_tensor("wd_t", [HID, D], I32, kind="ExternalInput").ap()
    cosr_h = nc.dram_tensor("cosrow", [1, D], BF16, kind="ExternalInput").ap()
    sinr_h = nc.dram_tensor("sinrow", [1, D], BF16, kind="ExternalInput").ap()
    ident_h = nc.dram_tensor("ident", [40, 8], F32, kind="ExternalInput").ap()
    identb_h = nc.dram_tensor("identb", [128, 128], BF16, kind="ExternalInput").ap()
    onesb_h = nc.dram_tensor("onesb", [1, 128], BF16, kind="ExternalInput").ap()
    onescol_h = nc.dram_tensor("onescol", [128, 1], BF16, kind="ExternalInput").ap()
    mask_h = nc.dram_tensor("masktri", [128, 128], BF16, kind="ExternalInput").ap()
    sel16_h = nc.dram_tensor("sel16", [16, NH * 64], BF16, kind="ExternalInput").ap()

    out_h = nc.dram_tensor("out", [T, D], F32, kind="ExternalOutput").ap()

    KI = "ExternalOutput" if dbg else "Internal"
    wrs_h = nc.dram_tensor("w_rope_s", [16, 8, 128, 128], BF16, kind=KI).ap()
    wp_s = nc.dram_tensor("wp_s", [8, 128, D], BF16, kind=KI).ap()
    wbs_h = nc.dram_tensor("wbs", [72, 128, 1024], BF16, kind=KI).ap()
    xnT_s = nc.dram_tensor("xnT_s", [NT, 128, T], BF16, kind=KI).ap()
    s_s = nc.dram_tensor("s_s", [4 * NT, 128, D], BF16, kind=KI).ap()

    ctx = contextlib.ExitStack()
    with ctx:
        def sb(name, shape, dt):
            return ctx.enter_context(nc.sbuf_tensor(name, shape, dt))

        ident = sb("s_ident", [40, 8], F32)
        onesb = sb("s_onesb", [1, 128], BF16)
        onescol = sb("s_onescol", [128, 1], BF16)
        mask = sb("s_mask", [128, 128], BF16)
        bT = sb("bT", [128, 24], F32)
        bTb = sb("bTb", [128, 24], BF16)
        bpp = sb("bpp", [1, D], BF16)
        qmaxT = sb("qmaxT", [128, 16], BF16)
        gtmp = sb("gtmp", [128, 1], BF16)
        cbt = sb("cbt", [128, 8], F32)
        lnsc = sb("lnsc", [128, 65], F32)
        identb = sb("s_identb", [128, 128], BF16)
        c8 = sb("c8", [8, 1], F32)
        crowb = sb("crowb", [1, 8], BF16)
        prodall = sb("prodall", [128, 8], BF16)
        sel16 = sb("s_sel16", [16, NH * 64], BF16)
        ones128 = sb("s_ones128", [128, 128], BF16)

        d_ms = Ctr(ctx, nc, "d_ms")
        d_w4 = Ctr(ctx, nc, "d_w4")
        d_x = [Ctr(ctx, nc, "d_x0"), Ctr(ctx, nc, "d_x1")]
        d_we = [Ctr(ctx, nc, "d_we0"), Ctr(ctx, nc, "d_we1")]
        d_wp = [Ctr(ctx, nc, f"d_wp{i}") for i in range(4)]
        d_wcl = [Ctr(ctx, nc, f"d_wcl{i}") for i in range(2)]
        d_wcs = [Ctr(ctx, nc, f"d_wcs{i}") for i in range(2)]
        d_swp = [Ctr(ctx, nc, "d_swp0"), Ctr(ctx, nc, "d_swp1")]
        d_wrs = Ctr(ctx, nc, "d_wrs")
        d_sx = Ctr(ctx, nc, "d_sx")
        d_ss = Ctr(ctx, nc, "d_ss")
        d_sl = Ctr(ctx, nc, "d_sl")
        d_so = [Ctr(ctx, nc, "d_so0"), Ctr(ctx, nc, "d_so1")]
        d_cw = Ctr(ctx, nc, "d_cw")
        d_cx = Ctr(ctx, nc, "d_cx")
        d_cs = [Ctr(ctx, nc, "d_cs0"), Ctr(ctx, nc, "d_cs1")]
        t_p = Ctr(ctx, nc, "t_p")
        v_p = Ctr(ctx, nc, "v_p")
        c_p = Ctr(ctx, nc, "c_p")
        g_p = Ctr(ctx, nc, "g_p")

        blk = ctx.enter_context(nc.Block())
        W = {}

        ab = contextlib.ExitStack()
        with ab:
            def sba(name, shape, dt):
                return ab.enter_context(nc.sbuf_tensor(name, shape, dt))

            kT = sba("kT", [128, 8 * T], BF16)
            vP = sba("vP", [128, 32 * NH * VST], BF16)
            wv = sba("wv", [128, 8 * 1024], BF16)
            we = [sba(f"we{i}", [128, 1024], BF16) for i in range(2)]
            wpb = [sba(f"wpb{i}", [128, 1024], BF16) for i in range(2)]
            pP = [sba(f"pP{i}", [128, 1024], BF16) for i in range(2)]
            xb = [sba(f"xb{i}", [128, D], F32) for i in range(2)]
            Lr = sba("Lr", [16, 512], F32)

            pAB = ab.enter_context(nc.psum_tensor("pAB", [128, 2048], F32))
            pC = ab.enter_context(nc.psum_tensor("pC", [128, 1024], F32))
            pD = ab.enter_context(nc.psum_tensor("pD", [128, 1024], F32))

            pro = contextlib.ExitStack()
            with pro:
                csb = pro.enter_context(nc.sbuf_tensor("csb", [128, 1024], BF16))

                @blk.sync
                def _(sy):
                    for dst, src in ((ident, ident_h), (identb, identb_h),
                                     (onesb, onesb_h),
                                     (onescol, onescol_h), (mask, mask_h),
                                     (sel16, sel16_h)):
                        d_ms.inc(sy.dma_start(out=dst[:], in_=src), 16)
                    d_ms.inc(sy.dma_start(out=csb[0:1, 0:512], in_=cosr_h[0:1, 0:512]), 16)
                    d_ms.inc(sy.dma_start(out=csb[0:1, 512:1024], in_=sinr_h[0:1, 0:512]), 16)
                    d_ms.inc(sy.dma_start(out=xb[0][0:1, :], in_=bp_h), 16)
                    d_ms.inc(sy.dma_start(out=kT[0:1, 0:2048].bitcast(F32), in_=ba_h[0:1, 0:1024]), 16)
                    d_ms.inc(sy.dma_start(out=kT[0:1, 2048:4096].bitcast(F32), in_=ba_h[0:1, 1024:2048]), 16)
                    d_ms.inc(sy.dma_start(out=xb[1][0:1, :], in_=ba_h[0:1, 2048:3072]), 16)
                    W["const_all"] = d_ms.n

                v0_cs = v_p.n

                @blk.tensor
                def _(te):
                    te.wait_ge(d_ms.sem, W["const_all"])
                    for j in range(2):
                        t_p.inc(te.matmul(pAB[:, j * 512:(j + 1) * 512],
                                          onesb[0:1, :], csb[0:1, j * 512:(j + 1) * 512],
                                          start=True, stop=True))
                        W[f"csmm{j}"] = t_p.n

                @blk.vector
                def _(ve):
                    for j in range(2):
                        ve.wait_ge(t_p.sem, W[f"csmm{j}"])
                        v_p.inc(ve.tensor_copy(csb[:, j * 512:(j + 1) * 512],
                                               pAB[:, j * 512:(j + 1) * 512]))
                    W["csb"] = v_p.n

                @blk.tensor
                def _(te):
                    te.wait_ge(v_p.sem, W["csb"])
                    t_p.inc(te.matmul(pAB[:, 0:128], onesb[0:1, :], onesb[0:1, :],
                                      start=True, stop=True))
                    W["onesmm"] = t_p.n

                @blk.vector
                def _(ve):
                    ve.wait_ge(t_p.sem, W["onesmm"])
                    v_p.inc(ve.tensor_copy(ones128[:], pAB[:, 0:128]))
                    W["ones128"] = v_p.n
                    ve.wait_ge(d_ms.sem, W["const_all"])
                    for third in (0, 1):
                        row = kT[0:1, third * 2048:(third + 1) * 2048].bitcast(F32)
                        bev = row[0:1, 0:1024:2]
                        bod = row[0:1, 1:1024:2]
                        cev = csb[0:1, 0:512]
                        snv = csb[0:1, 512:1024]
                        pa_ = pD[0:1, 0:512]
                        pb_ = pD[0:1, 512:1024]
                        sb_ = Lr[0:1, :]
                        ve.tensor_mul(pb_, bev, snv)
                        ve.tensor_mul(sb_, bod, snv)
                        v_p.inc(ve.tensor_mul(pa_, bev, cev))
                        ve.wait_ge(v_p.sem, v_p.n)
                        v_p.inc(ve.tensor_tensor(bev, pa_, sb_, OP.subtract))
                        ve.wait_ge(v_p.sem, v_p.n)
                        v_p.inc(ve.tensor_mul(sb_, bod, cev))
                        ve.wait_ge(v_p.sem, v_p.n)
                        ve.tensor_tensor(bod, pb_, sb_, OP.add)
                    v_p.inc(ve.engine_nop())
                    W["bfold"] = v_p.n

                @blk.tensor
                def _(te):
                    te.wait_ge(v_p.sem, W["bfold"])
                    for c in range(24):
                        src = (kT[0:1, 0:2048].bitcast(F32),
                               kT[0:1, 2048:4096].bitcast(F32), xb[1][0:1, :])[c // 8]
                        ins = te.transpose(pC[:, c:c + 1],
                                           src[0:1, (c % 8) * 128:(c % 8) * 128 + 128],
                                           ident[0:1, 0:1])
                    t_p.inc(ins)
                    W["btr"] = t_p.n

                @blk.vector
                def _(ve):
                    ve.wait_ge(t_p.sem, W["btr"])
                    v_p.inc(ve.tensor_copy(bT[:], pC[:, 0:24]))
                    ve.wait_ge(v_p.sem, v_p.n)
                    ve.tensor_copy(bTb[:], bT[:])
                    ve.memset(lnsc[:, 64:65], 1e-5)
                    v_p.inc(ve.tensor_copy(pP[0][0:1, :], xb[0][0:1, :]))
                    W["bT_bp"] = v_p.n

                wst4 = xb[1]
                if True:
                    v_wp0 = v_p.n

                    @blk.sync
                    def _(sy):
                        sy.wait_ge(t_p.sem, W["btr"])
                        for g in range(16):
                            c, h = g // 2, g % 2
                            if g >= 2:
                                sy.wait_ge(v_p.sem, v_wp0 + g - 1)
                            d_x[h].inc(sy.dma_start(
                                out=wst4[:, h * 512:(h + 1) * 512],
                                in_=wp_h[c * 128:(c + 1) * 128, h * 512:(h + 1) * 512]), 16)
                            W[f"wpld{g}"] = d_x[h].n

                    st_wp0 = [d_swp[0].n, d_swp[1].n]

                    @blk.vector
                    def _(ve):
                        for g in range(16):
                            c, h = g // 2, g % 2
                            ve.wait_ge(d_x[h].sem, W[f"wpld{g}"])
                            if h == 0 and c >= 2:
                                ve.wait_ge(d_swp[c % 2].sem, st_wp0[c % 2] + 16 * (c // 2))
                            v_p.inc(ve.tensor_copy(we[c % 2][:, h * 512:(h + 1) * 512],
                                                   wst4[:, h * 512:(h + 1) * 512]))
                            if h == 1:
                                W[f"wpcv{c}"] = v_p.n

                    @blk.gpsimd
                    def _(gp):
                        for c in range(8):
                            gp.wait_ge(v_p.sem, W[f"wpcv{c}"])
                            d_swp[c % 2].inc(gp.dma_start(out=wp_s[c], in_=we[c % 2][:]), 16)
                            W[f"wpst{c}"] = d_swp[c % 2].n
                        W["wp_done0"] = d_swp[0].n
                        W["wp_done1"] = d_swp[1].n

                    t_bpp0 = t_p.n

                    @blk.sync
                    def _(sy):
                        sy.wait_ge(d_swp[0].sem, W["wp_done0"])
                        sy.wait_ge(d_swp[1].sem, W["wp_done1"])
                        for c in range(8):
                            if c >= 2:
                                sy.wait_ge(t_p.sem, t_bpp0 + c - 1)
                            d_wp[c % 2].inc(sy.dma_start(out=wpb[c % 2][:], in_=wp_s[c]), 16)
                            W[f"bppld{c}"] = d_wp[c % 2].n

                    @blk.tensor
                    def _(te):
                        te.wait_ge(v_p.sem, W["bT_bp"])
                        for c in range(8):
                            te.wait_ge(d_wp[c % 2].sem, W[f"bppld{c}"])
                            te.matmul(pD[0:1, 0:512], bTb[:, 16 + c:17 + c], wpb[c % 2][:, 0:512],
                                      start=(c == 0), stop=False)
                            t_p.inc(te.matmul(pD[0:1, 512:1024], bTb[:, 16 + c:17 + c],
                                              wpb[c % 2][:, 512:1024], start=(c == 0), stop=False))
                            W[f"bppmm{c}"] = t_p.n
                        te.matmul(pD[0:1, 0:512], onesb[0:1, 0:1], pP[0][0:1, 0:512],
                                  start=False, stop=True)
                        t_p.inc(te.matmul(pD[0:1, 512:1024], onesb[0:1, 0:1], pP[0][0:1, 512:1024],
                                          start=False, stop=True))
                        W["bppdone"] = t_p.n

                    @blk.vector
                    def _(ve):
                        ve.wait_ge(t_p.sem, W["bppdone"])
                        ve.tensor_copy(bpp[:], pD[0:1, :])
                        ve.memset(qmaxT[:], 0.0)
                        v_p.inc(ve.memset(vP[:, 64::VST], 1.0))
                        W["initdone"] = v_p.n

                    v_wf0 = v_p.n

                    @blk.sync
                    def _(sy):
                        for g in range(48):
                            k, h = g // 2, g % 2
                            dk, th = k // 3, k % 3
                            if g < 2:
                                sy.wait_ge(t_p.sem, W["bppdone"])
                                sy.wait_ge(v_p.sem, W["wpcv7"])
                            else:
                                sy.wait_ge(v_p.sem, v_wf0 + g - 1)
                            d_x[h].inc(sy.dma_start(
                                out=wst4[:, h * 512:(h + 1) * 512],
                                in_=wat_h[dk * 128:(dk + 1) * 128,
                                          th * 1024 + h * 512:th * 1024 + (h + 1) * 512]), 16)
                            W[f"watld_{g}"] = d_x[h].n

                    st_wf0 = d_wrs.n

                    @blk.vector
                    def _(ve):
                        for g in range(48):
                            k, h = g // 2, g % 2
                            dk, th = k // 3, k % 3
                            ve.wait_ge(d_x[h].sem, W[f"watld_{g}"])
                            src = wst4[:, h * 512:(h + 1) * 512]
                            if th < 2:
                                if h == 0 and dk >= 1:
                                    ve.wait_ge(d_wrs.sem, st_wf0 + 32 * dk)
                                dst = wpb[th]
                                cev = csb[:, h * 256:(h + 1) * 256]
                                snv = csb[:, 512 + h * 256:512 + (h + 1) * 256]
                                d0 = h * 512
                                ve.tensor_mul(xb[0][:, 0:256], src[:, 0::2], cev)
                                ve.tensor_mul(pD[:, 0:256], src[:, 1::2], snv)
                                ve.tensor_tensor(dst[:, d0:d0 + 512:2], xb[0][:, 0:256],
                                                 pD[:, 0:256], OP.subtract)
                                ve.tensor_mul(xb[0][:, 0:256], src[:, 1::2], cev)
                                ve.tensor_mul(pD[:, 0:256], src[:, 0::2], snv)
                                v_p.inc(ve.tensor_tensor(dst[:, d0 + 1:d0 + 512:2], xb[0][:, 0:256],
                                                         pD[:, 0:256], OP.add))
                            else:
                                v_p.inc(ve.tensor_copy(
                                    wv[:, dk * 1024 + h * 512:dk * 1024 + (h + 1) * 512], src))
                            if h == 1:
                                W[f"wf_{dk}_{th}"] = v_p.n

                    @blk.gpsimd
                    def _(gp):
                        for dk in range(8):
                            gp.wait_ge(v_p.sem, W[f"wf_{dk}_1"])
                            d_wrs.inc(gp.dma_start(
                                out=wrs_h[0:8, dk].rearrange("e p c -> p e c"),
                                in_=wpb[0][:].rearrange("p (e c) -> p e c", e=8)), 16)
                            d_wrs.inc(gp.dma_start(
                                out=wrs_h[8:16, dk].rearrange("e p c -> p e c"),
                                in_=wpb[1][:].rearrange("p (e c) -> p e c", e=8)), 16)
                            W[f"wqkst{dk}"] = d_wrs.n
                        W["wrs_done"] = d_wrs.n
            qy = sba("qy", [128, 8 * 512], BF16)
            yT = sba("yT", [128, 8 * 512], BF16)
            sev0 = sba("sev0", [128, D], BF16)
            xnb = [sba("xnb0", [128, D], BF16)] * 2
            xnT = sba("xnT", [128, 8 * 512], BF16)
            Rb = sba("Rb", [16, 512], BF16)
            wcb = [sba(f"wcb{i}", [128, 512], I32) for i in range(2)]
            wcvb = [sba(f"wcvb{i}", [128, 512], BF16) for i in range(2)]

            lastS = {("A", 0): 0, ("A", 1): 0, ("B", 0): 0, ("B", 1): 0}
            W["pj_done_-1"] = 0
            W["pjev_done_-1"] = W["initdone"]
            W["yev_done_-1"] = W["bT_bp"]
            last_wpb = [W["bppmm6"], W["bppmm7"]]

            def emit_ln_load(t, i):
                ti = 4 * t + i

                @blk.sync
                def _(sy, t=t, i=i, ti=ti):
                    if t == 0 and i <= 1:
                        sy.wait_ge(d_wrs.sem, W["wrs_done"])
                        sy.wait_ge(v_p.sem, W["bT_bp"])
                    if ti >= 2:
                        sy.wait_ge(v_p.sem, W[f"xnapply_{ti - 2}"])
                    d_x[i % 2].inc(sy.dma_start(out=xb[i % 2][:], in_=x_h[ti * 128:(ti + 1) * 128, :]), 16)
                    W[f"xld_{ti}"] = d_x[i % 2].n

            def emit_ln_stats(t, i):
                ti = 4 * t + i
                L = 16 * i

                @blk.vector
                def _(ve, t=t, i=i, ti=ti, L=L):
                    ve.wait_ge(d_x[i % 2].sem, W[f"xld_{ti}"])
                    ve.bn_stats(lnsc[:, L + 0:L + 6], xb[i % 2][:, 0:512])
                    v_p.inc(ve.bn_stats(lnsc[:, L + 6:L + 12], xb[i % 2][:, 512:1024]))
                    ve.wait_ge(v_p.sem, v_p.n)
                    v_p.inc(ve.bn_aggr(lnsc[:, L + 12:L + 14], lnsc[:, L + 0:L + 12]))
                    W[f"var_{ti}"] = v_p.n

            def emit_ln_apply(t, i, c_rstd_done):
                ti = 4 * t + i
                L = 16 * i

                @blk.vector
                def _(ve, t=t, i=i, ti=ti, L=L, c_rstd_done=c_rstd_done):
                    if ti >= 1:
                        ve.wait_ge(t_p.sem, W[f"trmm_{ti - 1}"])
                    ve.wait_ge(c_p.sem, c_rstd_done(i))
                    v_p.inc(ve.scalar_tensor_tensor(lnsc[:, L + 0:L + 1], lnsc[:, L + 12:L + 13], -1.0,
                                                    lnsc[:, L + 15:L + 16], OP.mult, OP.mult))
                    ve.wait_ge(v_p.sem, v_p.n)
                    v_p.inc(ve.tensor_scalar(xnb[i % 2][:], xb[i % 2][:],
                                             lnsc[:, L + 15:L + 16], lnsc[:, L + 0:L + 1],
                                             OP.mult, OP.add))
                    W[f"xnapply_{ti}"] = v_p.n

            def emit_ln_calc(t, i, c_rstd_done):
                emit_ln_stats(t, i)
                emit_ln_apply(t, i, c_rstd_done)

            def emit_ln_tile(t, i, c_rstd_done):
                emit_ln_load(t, i)
                emit_ln_calc(t, i, c_rstd_done)

            def emit_trans_tile(t, i):
                ti = 4 * t + i

                @blk.tensor
                def _(te, t=t, i=i, ti=ti):
                    te.wait_ge(v_p.sem, W[f"xnapply_{ti}"])
                    if i == 0:
                        te.wait_ge(v_p.sem, W[f"cbready_{t - 1}"])
                        te.wait_ge(v_p.sem, W.get(f"pjev_{t - 2}_2", 0))
                    else:
                        te.wait_ge(v_p.sem, W[f"trev_{ti - 1}"])
                    for dk in range(8):
                        ins = te.transpose(pD[:, dk * 64:dk * 64 + 64].bitcast(BF16),
                                           xnb[i % 2][:, dk * 128:(dk + 1) * 128], identb[:])
                    t_p.inc(ins)
                    W[f"trmm_{ti}"] = t_p.n

                @blk.vector
                def _(ve, t=t, i=i, ti=ti):
                    ve.wait_ge(t_p.sem, W[f"trmm_{ti}"])
                    if i == 0:
                        ve.wait_ge(d_sx.sem, W.get(f"xnst_{t - 1}", 0))
                        ve.wait_ge(t_p.sem, W.get(f"vmm_{4 * (t - 1) + 3}", 0))
                    for dk in range(8):
                        ins = ve.tensor_copy(xnT[:, dk * 512 + i * 128:dk * 512 + i * 128 + 128],
                                             pD[:, dk * 64:dk * 64 + 64].bitcast(BF16))
                    v_p.inc(ins)
                    W[f"trev_{ti}"] = v_p.n

            def emit_qk_chunk(t, e):
                if e >= 10:
                    prev_mm = f"qkmm_{t}_{e - 2}"
                    prev_ev = f"qkev_{t}_{e - 2}"
                elif e >= 8:
                    prev_mm = f"qkmm_{t - 1}_{6 + (e % 2)}"
                    prev_ev = None
                elif e >= 2:
                    prev_mm = f"qkmm_{t}_{e - 2}"
                    prev_ev = f"qkev_{t}_{e - 2}"
                else:
                    prev_mm = f"qkmm_{t}_{14 + (e % 2)}"
                    prev_ev = None

                @blk.sync
                def _(sy, t=t, e=e, prev_mm=prev_mm):
                    if t == 0 and e in (8, 9):
                        sy.wait_ge(d_swp[0].sem, W["wp_done0"])
                        sy.wait_ge(d_swp[1].sem, W["wp_done1"])
                    prev = W.get(prev_mm, 0)
                    if prev:
                        sy.wait_ge(t_p.sem, prev)
                    d_we[e % 2].inc(sy.dma_start(
                        out=we[e % 2][:].rearrange("p (d c) -> p d c", d=8),
                        in_=wrs_h[e].rearrange("d p c -> p d c")), 16)
                    W[f"weld_{t}_{e}"] = d_we[e % 2].n

                @blk.tensor
                def _(te, t=t, e=e, prev_ev=prev_ev):
                    ps = pAB if e < 8 else pD
                    te.wait_ge(d_we[e % 2].sem, W[f"weld_{t}_{e}"])
                    te.wait_ge(v_p.sem, W[f"trev_{4 * t + 3}"])
                    if e < 2:
                        te.wait_ge(c_p.sem, W.get(f"exps_done_{t - 1}", 0))
                    if prev_ev is not None and prev_ev in W:
                        te.wait_ge(v_p.sem, W[prev_ev])
                    half = (e % 2) * 512
                    for dk in range(8):
                        ins = te.matmul(ps[:, half:half + 512],
                                        we[e % 2][:, dk * 128:(dk + 1) * 128],
                                        xnT[:, dk * 512:(dk + 1) * 512],
                                        start=(dk == 0), stop=(dk == 7))
                    t_p.inc(ins)
                    W[f"qkmm_{t}_{e}"] = t_p.n

                @blk.vector
                def _(ve, t=t, e=e):
                    ps = pAB if e < 8 else pD
                    ve.wait_ge(t_p.sem, W[f"qkmm_{t}_{e}"])
                    if e < 8:
                        dst = qy[:, e * 512:(e + 1) * 512]
                    else:
                        dst = kT[:, (e - 8) * T + t * 512:(e - 8) * T + t * 512 + 512]
                    half = (e % 2) * 512
                    v_p.inc(ve.tensor_scalar(dst, ps[:, half:half + 512],
                                             bT[:, e:e + 1], None, OP.add))
                    W[f"qkev_{t}_{e}"] = v_p.n
                    ve.wait_ge(v_p.sem, v_p.n)
                    v_p.inc(ve.reduce_max(gtmp[:], dst, axis=AX.X, apply_absolute_value=True))
                    ve.wait_ge(v_p.sem, v_p.n)
                    g_p.inc(ve.tensor_max(qmaxT[:, e:e + 1], qmaxT[:, e:e + 1], gtmp[:]))
                    W[f"gmax_{t}_{e}"] = g_p.n

            def emit_qk_sync(t, e):
                prev_mm = f"qkmm_{t}_{e - 2}" if e >= 10 else f"qkmm_{t - 1}_{6 + (e % 2)}"

                @blk.sync
                def _(sy, t=t, e=e, prev_mm=prev_mm):
                    if t == 0 and e in (8, 9):
                        sy.wait_ge(d_swp[0].sem, W["wp_done0"])
                        sy.wait_ge(d_swp[1].sem, W["wp_done1"])
                    prev = W.get(prev_mm, 0)
                    if prev:
                        sy.wait_ge(t_p.sem, prev)
                    d_we[e % 2].inc(sy.dma_start(
                        out=we[e % 2][:].rearrange("p (d c) -> p d c", d=8),
                        in_=wrs_h[e].rearrange("d p c -> p d c")), 16)
                    W[f"weld_{t}_{e}"] = d_we[e % 2].n

            def qk_fillers(t, e):
                prev_ev = f"qkev_{t}_{e - 2}" if e >= 10 else None

                def mk(dk, t=t, e=e, prev_ev=prev_ev):
                    def emit(te):
                        if dk == 0:
                            te.wait_ge(d_we[e % 2].sem, W[f"weld_{t}_{e}"])
                            te.wait_ge(v_p.sem, W[f"trev_{4 * t + 3}"])
                            if prev_ev is not None and prev_ev in W:
                                te.wait_ge(v_p.sem, W[prev_ev])
                        half = (e % 2) * 512
                        ins = te.matmul(pD[:, half:half + 512],
                                        we[e % 2][:, dk * 128:(dk + 1) * 128],
                                        xnT[:, dk * 512:(dk + 1) * 512],
                                        start=(dk == 0), stop=(dk == 7))
                        if dk == 7:
                            t_p.inc(ins)
                            W[f"qkmm_{t}_{e}"] = t_p.n
                    return emit
                return [mk(dk) for dk in range(8)]

            def trans_fillers(t, i):
                ti = 4 * t + i

                def mk(dk, t=t, i=i, ti=ti):
                    def emit(te):
                        if dk == 0:
                            te.wait_ge(v_p.sem, W[f"xnapply_{ti}"])
                            if i == 0:
                                te.wait_ge(v_p.sem, W[f"cbready_{t - 1}"])
                                te.wait_ge(v_p.sem, W.get(f"pjev_{t - 2}_2", 0))
                            else:
                                te.wait_ge(v_p.sem, W[f"trev_{ti - 1}"])
                        ins = te.transpose(pD[:, dk * 64:dk * 64 + 64].bitcast(BF16),
                                           xnb[i % 2][:, dk * 128:(dk + 1) * 128], identb[:])
                        if dk == 7:
                            t_p.inc(ins)
                            W[f"trmm_{ti}"] = t_p.n
                    return emit
                return [mk(dk) for dk in range(8)]

            def emit_trans_ev(t, i):
                ti = 4 * t + i

                @blk.vector
                def _(ve, t=t, i=i, ti=ti):
                    ve.wait_ge(t_p.sem, W[f"trmm_{ti}"])
                    if i == 0:
                        ve.wait_ge(d_sx.sem, W.get(f"xnst_{t - 1}", 0))
                        ve.wait_ge(t_p.sem, W.get(f"vmm_{4 * (t - 1) + 3}", 0))
                    for dk in range(8):
                        ins = ve.tensor_copy(xnT[:, dk * 512 + i * 128:dk * 512 + i * 128 + 128],
                                             pD[:, dk * 64:dk * 64 + 64].bitcast(BF16))
                    v_p.inc(ins)
                    W[f"trev_{ti}"] = v_p.n

            def emit_qk_ev(t, e):
                @blk.vector
                def _(ve, t=t, e=e):
                    ve.wait_ge(t_p.sem, W[f"qkmm_{t}_{e}"])
                    dst = kT[:, (e - 8) * T + t * 512:(e - 8) * T + t * 512 + 512]
                    half = (e % 2) * 512
                    v_p.inc(ve.tensor_scalar(dst, pD[:, half:half + 512],
                                             bT[:, e:e + 1], None, OP.add))
                    W[f"qkev_{t}_{e}"] = v_p.n
                    ve.wait_ge(v_p.sem, v_p.n)
                    v_p.inc(ve.reduce_max(gtmp[:], dst, axis=AX.X, apply_absolute_value=True))
                    ve.wait_ge(v_p.sem, v_p.n)
                    g_p.inc(ve.tensor_max(qmaxT[:, e:e + 1], qmaxT[:, e:e + 1], gtmp[:]))
                    W[f"gmax_{t}_{e}"] = g_p.n

            def emit_v_part_i(t, i):
                ti = 4 * t + i

                @blk.tensor
                def _(te, t=t, i=i, ti=ti):
                    if i == 0:
                        te.wait_ge(c_p.sem, W.get(f"exps_done_{t - 1}", 0))
                    else:
                        te.wait_ge(v_p.sem, W.get(f"vev_{ti - 1}", 0))
                    for dk in range(8):
                        lhs = xnT[:, dk * 512 + i * 128:dk * 512 + i * 128 + 128]
                        te.matmul(pAB[:, 1024:1536], lhs, wv[:, dk * 1024:dk * 1024 + 512],
                                  start=(dk == 0), stop=(dk == 7))
                        ins = te.matmul(pAB[:, 1536:2048], lhs, wv[:, dk * 1024 + 512:dk * 1024 + 1024],
                                        start=(dk == 0), stop=(dk == 7))
                    t_p.inc(ins)
                    W[f"vmm_{ti}"] = t_p.n

                @blk.vector
                def _(ve, ti=ti):
                    ve.wait_ge(t_p.sem, W[f"vmm_{ti}"])
                    dst = vP[:, ti * NH * VST:(ti + 1) * NH * VST]
                    dst3 = dst.rearrange("p (h s) -> p h s", s=VST)[:, :, 0:64]
                    src3 = pAB[:, 1024:2048].rearrange("p (h s) -> p h s", s=64)
                    v_p.inc(ve.tensor_copy(dst3, src3))
                    W[f"vev_{ti}"] = v_p.n

            def emit_cb(t):
                @blk.vector
                def _(ve, t=t):
                    ve.wait_ge(g_p.sem, W[f"gmax_{t}_7"])
                    ve.wait_ge(t_p.sem, W.get(f"cbm_{t - 1}", 0))
                    v_p.inc(ve.tensor_mul(prodall[:], qmaxT[:, 0:8], qmaxT[:, 8:16]))
                    W[f"cbp_{t}"] = v_p.n

                @blk.tensor
                def _(te, t=t):
                    te.wait_ge(v_p.sem, W[f"cbp_{t}"])
                    t_p.inc(te.matmul(pD[:, 16:24], ones128[:], prodall[:],
                                      start=True, stop=True))
                    W[f"cbm_{t}"] = t_p.n

                @blk.vector
                def _(ve, t=t):
                    ve.wait_ge(t_p.sem, W[f"cbm_{t}"])
                    v_p.inc(ve.tensor_scalar_mul(cbt[:], pD[:, 16:24], -0.0625))
                    W[f"cbready_{t}"] = v_p.n

            def emit_recip(tb):
                @blk.vector
                def _(ve, tb=tb):
                    ve.wait_ge(t_p.sem, W.get(f"rbmm_{tb - 1}_7", 0))
                    ve.wait_ge(d_sl.sem, W[f"lst_{tb}_7"])
                    ve.reciprocal(Lr[:], Lr[:])
                    v_p.inc(ve.tensor_copy(Rb[:], Lr[:]))
                    W[f"rb_{tb}"] = v_p.n

            def emit_ynorm(tb, yc):
                @blk.tensor
                def _(te, tb=tb, yc=yc):
                    reg = (pD if yc % 4 < 2 else pC)
                    off = (yc % 2) * 512
                    te.wait_ge(v_p.sem, W[f"rb_{tb}"])
                    if yc >= 4:
                        te.wait_ge(v_p.sem, W[f"ynorm_{tb}_{yc - 4}"])
                    t_p.inc(te.matmul(reg[:, off:off + 512],
                                      sel16[:, yc * 128:(yc + 1) * 128], Rb[:],
                                      start=True, stop=True))
                    W[f"rbmm_{tb}_{yc}"] = t_p.n

                @blk.vector
                def _(ve, tb=tb, yc=yc):
                    reg = (pD if yc % 4 < 2 else pC)
                    off = (yc % 2) * 512
                    ve.wait_ge(t_p.sem, W[f"rbmm_{tb}_{yc}"])
                    v_p.inc(ve.tensor_mul(yT[:, yc * 512:(yc + 1) * 512],
                                          yT[:, yc * 512:(yc + 1) * 512],
                                          reg[:, off:off + 512]))
                    W[f"ynorm_{tb}_{yc}"] = v_p.n

            def emit_proj(tb):
                @blk.sync
                def _(sy, tb=tb):
                    t_base = t_p.n
                    if tb == 0:
                        sy.wait_ge(d_wrs.sem, W["wrs_done"])
                    for cc in range(8):
                        if cc < 2:
                            sy.wait_ge(t_p.sem, last_wpb[cc])
                        else:
                            sy.wait_ge(t_p.sem, t_base + 4 * (cc - 1))
                        d_wp[cc % 2].inc(sy.dma_start(out=wpb[cc % 2][:], in_=wp_s[cc]), 16)
                        W[f"wpld_{tb}_{cc}"] = d_wp[cc % 2].n

                @blk.tensor
                def _(te, tb=tb):
                    if f"cbready_{tb + 1}" in W:
                        te.wait_ge(v_p.sem, W[f"cbready_{tb + 1}"])
                    else:
                        te.wait_ge(c_p.sem, W[f"exps_done_{tb}"])
                        te.wait_ge(v_p.sem, W[f"ynorm_{tb}_7"])
                    pjs = ((pAB, 0), (pAB, 1024), (pD, 0), (pC, 0))
                    for cc in range(8):
                        te.wait_ge(d_wp[cc % 2].sem, W[f"wpld_{tb}_{cc}"])
                        for i in range(4):
                            pj, c0 = pjs[i]
                            te.matmul(pj[:, c0:c0 + 512],
                                      yT[:, cc * 512 + i * 128:cc * 512 + i * 128 + 128],
                                      wpb[cc % 2][:, 0:512], start=(cc == 0), stop=False)
                            t_p.inc(te.matmul(pj[:, c0 + 512:c0 + 1024],
                                              yT[:, cc * 512 + i * 128:cc * 512 + i * 128 + 128],
                                              wpb[cc % 2][:, 512:1024], start=(cc == 0), stop=False))
                        last_wpb[cc % 2] = t_p.n
                    for i in range(4):
                        pj, c0 = pjs[i]
                        te.matmul(pj[:, c0:c0 + 512], onesb[:], bpp[0:1, 0:512], start=False, stop=True)
                        t_p.inc(te.matmul(pj[:, c0 + 512:c0 + 1024], onesb[:], bpp[0:1, 512:1024],
                                          start=False, stop=True))
                    W[f"pjmm_{tb}"] = t_p.n

                for i, pj, c0 in ((0, pAB, 0), (1, pAB, 1024), (2, pD, 0), (3, pC, 0)):
                    @blk.vector
                    def _(ve, tb=tb, i=i, pj=pj, c0=c0):
                        ve.wait_ge(t_p.sem, W[f"pjmm_{tb}"])
                        ve.wait_ge(d_ss.sem, W.get(f"sst_{tb}_{i - 1}", 0))
                        v_p.inc(ve.tensor_copy(sev0[:], pj[:, c0:c0 + 1024]))
                        W[f"pjev_{tb}_{i}"] = v_p.n

                    @blk.gpsimd
                    def _(gp, tb=tb, i=i):
                        gp.wait_ge(v_p.sem, W[f"pjev_{tb}_{i}"])
                        d_ss.inc(gp.dma_start(out=s_s[4 * tb + i], in_=sev0[:]), 16)
                        W[f"sst_{tb}_{i}"] = d_ss.n

                W[f"pjev_done_{tb}"] = v_p.n

            wconv_chunks = []
            for kind, n_tiles, nh in (("g", 8, 3), ("u", 8, 3), ("d", 24, 1)):
                for j in range(n_tiles):
                    for th in range(nh):
                        wconv_chunks.append((kind, j, th))
            NWH = 2 * len(wconv_chunks)

            def emit_wconv(sl):
                if sl < NWH or 0 <= sl - 2 < NWH:
                    @blk.gpsimd
                    def _(gp, sl=sl):
                        if 0 <= sl - 2 < NWH:
                            k = sl - 2
                            gp.wait_ge(v_p.sem, W[f"wcv_{k}"])
                            d_wcs[k % 2].inc(gp.dma_start(
                                out=wbs_h[k // 2, :, (k % 2) * 512:(k % 2) * 512 + 512],
                                in_=wcvb[k % 2][:]), 16)
                            W[f"wcs_{k}"] = d_wcs[k % 2].n
                        if sl < NWH:
                            kind, j, th = wconv_chunks[sl // 2]
                            src = {"g": wgT_h, "u": wuT_h, "d": wdT_h}[kind]
                            c0 = th * 1024 + (sl % 2) * 512
                            if sl < 2:
                                gp.wait_ge(d_wrs.sem, W["wrs_done"])
                            if sl >= 2:
                                gp.wait_ge(v_p.sem, W[f"wcv_{sl - 2}"])
                            d_wcl[sl % 2].inc(gp.dma_start(
                                out=wcb[sl % 2][:],
                                in_=src[j * 128:(j + 1) * 128, c0:c0 + 512]), 16)
                            W[f"wcl_{sl}"] = d_wcl[sl % 2].n

                if 0 <= sl - 1 < NWH:
                    @blk.vector
                    def _(ve, sl=sl - 1):
                        ve.wait_ge(d_wcl[sl % 2].sem, W[f"wcl_{sl}"])
                        if sl >= 2:
                            ve.wait_ge(d_wcs[sl % 2].sem, W[f"wcs_{sl - 2}"])
                        v_p.inc(ve.tensor_copy(wcvb[sl % 2][:], wcb[sl % 2][:]))
                        W[f"wcv_{sl}"] = v_p.n

            pre_c0 = c_p.n

            def c_rstd0(i, pre_c0=pre_c0):
                return pre_c0 + 2 * (i + 1)

            W["cbready_-1"] = W["initdone"]
            emit_ln_tile(0, 0, c_rstd0)
            emit_trans_tile(0, 0)
            emit_ln_tile(0, 1, c_rstd0)
            emit_trans_tile(0, 1)
            emit_ln_tile(0, 2, c_rstd0)
            emit_trans_tile(0, 2)
            emit_ln_tile(0, 3, c_rstd0)
            emit_trans_tile(0, 3)

            @blk.scalar
            def _(sc, pre_c0=pre_c0):
                for i in range(4):
                    L = 16 * i
                    sc.wait_ge(v_p.sem, W[f"var_{i}"])
                    c_p.inc(sc.activation(lnsc[:, L + 14:L + 15], lnsc[:, L + 13:L + 14],
                                          AF.Ln, bias=lnsc[:, 64:65]))
                    sc.wait_ge(c_p.sem, c_p.n)
                    c_p.inc(sc.activation(lnsc[:, L + 15:L + 16], lnsc[:, L + 14:L + 15],
                                          AF.Exp, scale=-0.5))
                    assert c_p.n == pre_c0 + 2 * (i + 1)

            @blk.gpsimd
            def _(gp):
                gp.wait_ge(v_p.sem, W["trev_3"])
                d_sx.inc(gp.dma_start(out=xnT_s[0], in_=xnT[:]), 16)
                W["xnst_0"] = d_sx.n

            emit_ln_load(1, 0)
            emit_ln_load(1, 1)
            for e in range(8, 16):
                emit_qk_chunk(0, e)

            for t in range(NT):
                nkc = 4 * t + 4
                has_next = t + 1 < NT
                R = 4 if has_next else 0
                c_base_t = c_p.n

                K0 = min(6, 4 * t + 2, nkc)

                def exp_incs(kc, t=t):
                    return (kc + 1) if kc < 4 * t else (4 * t + 2 * (kc - 4 * t + 1))

                PINC = 4 * t + 8

                def c_exp_done(c, kc, c_base_t=c_base_t, R=R, K0=K0, PINC=PINC,
                               exp_incs=exp_incs):
                    return (c_base_t + 2 * min(c, R) + c * PINC + exp_incs(kc)
                            + (2 if (c < R and kc >= K0) else 0))

                def c_rstd_next(i, c_base_t=c_base_t, R=R, K0=K0, PINC=PINC,
                                exp_incs=exp_incs):
                    return c_base_t + 2 * min(i, R) + i * PINC + exp_incs(K0 - 1) + 2

                emit_wconv(18 * t)
                tb = t - 1
                if t >= 1:
                    emit_recip(tb)
                emit_qk_chunk(t, 0)
                emit_qk_chunk(t, 1)
                if t >= 1:
                    emit_ynorm(tb, 0)
                    emit_ynorm(tb, 1)
                emit_qk_chunk(t, 2)
                emit_qk_chunk(t, 3)
                if t >= 1:
                    emit_ynorm(tb, 2)
                    emit_ynorm(tb, 3)
                emit_qk_chunk(t, 4)
                emit_qk_chunk(t, 5)
                if t >= 1:
                    emit_ynorm(tb, 4)
                    emit_ynorm(tb, 5)
                emit_qk_chunk(t, 6)
                emit_qk_chunk(t, 7)
                if t >= 1:
                    emit_ynorm(tb, 6)
                    emit_ynorm(tb, 7)
                emit_v_part_i(t, 0)
                emit_v_part_i(t, 1)
                emit_v_part_i(t, 2)
                emit_v_part_i(t, 3)
                emit_cb(t)
                if has_next:
                    emit_ln_stats(t + 1, 0)
                if t >= 1:
                    emit_proj(tb)
                if has_next:
                    emit_ln_apply(t + 1, 0, c_rstd_next)
                emit_wconv(18 * t + 1)

                for c in range(8):
                    emit_wconv(18 * t + 2 + 2 * c)
                    if has_next and 1 <= c <= 2:
                        emit_ln_load(t + 1, c + 1)
                    if t + 2 < NT and 6 <= c:
                        emit_ln_load(t + 2, c - 6)
                    if has_next and 1 <= c < 4:
                        emit_ln_calc(t + 1, c, c_rstd_next)
                    fillers = []
                    if has_next:
                        if c < 4:
                            fillers = trans_fillers(t + 1, c)
                        else:
                            for e in (8 + 2 * (c - 4), 9 + 2 * (c - 4)):
                                emit_qk_sync(t + 1, e)
                                fillers += qk_fillers(t + 1, e)
                    v_base = v_p.n

                    def v_mask_done(kc, v_base=v_base, t=t):
                        return v_base + (kc - 4 * t) + 1

                    def mk_mms(te, t=t, c=c, nkc=nkc, c_exp_done=c_exp_done,
                               v_mask_done=v_mask_done):
                        def mm1(kc):
                            b = kc % 2
                            q0 = 128 * (kc - 4 * t) if kc >= 4 * t else 0
                            if kc >= 2:
                                te.wait_ge(c_p.sem, c_exp_done(c, kc - 2))
                            else:
                                te.wait_ge(c_p.sem, lastS[("A", b)])
                                te.wait_ge(c_p.sem, lastS[("B", b)])
                                if c == 0:
                                    te.wait_ge(v_p.sem, W[f"cbready_{t}"])
                                    te.wait_ge(v_p.sem, W.get(f"pjev_{t - 1}_1", 0))
                            te.matmul(pAB[:, b * 1024 + q0:b * 1024 + 512],
                                      kT[0:64, c * T + kc * 128:c * T + kc * 128 + 128],
                                      qy[0:64, c * 512 + q0:(c + 1) * 512],
                                      start=True, stop=True)
                            t_p.inc(te.matmul(pAB[:, b * 1024 + 512 + q0:b * 1024 + 1024],
                                              kT[64:128, c * T + kc * 128:c * T + kc * 128 + 128],
                                              qy[64:128, c * 512 + q0:(c + 1) * 512],
                                              start=True, stop=True))
                            W[f"mm1_{t}_{c}_{kc}"] = t_p.n

                        def mm2(kc):
                            b = kc % 2
                            diag = kc >= 4 * t
                            q0 = 128 * (kc - 4 * t) if diag else 0
                            te.wait_ge(c_p.sem, c_exp_done(c, kc))
                            if diag:
                                te.wait_ge(v_p.sem, v_mask_done(kc))
                            if kc == 0 and c == 0:
                                te.wait_ge(v_p.sem, W[f"vev_{4 * t + 3}"])
                                te.wait_ge(v_p.sem, W[f"pjev_done_{t - 1}"])
                            if kc == 0:
                                te.wait_ge(v_p.sem, W.get(f"yev_{t}_{c - 1}", W[f"yev_done_{t - 1}"]))
                            vbase = kc * NH * VST
                            last = (kc == nkc - 1)
                            te.matmul(pC[0:65, q0:512],
                                      vP[:, vbase + (2 * c) * VST:vbase + (2 * c) * VST + 65],
                                      pP[b][:, q0:512], start=(kc == 0), stop=last)
                            t_p.inc(te.matmul(pC[0:65, 512 + q0:1024],
                                              vP[:, vbase + (2 * c + 1) * VST:vbase + (2 * c + 1) * VST + 65],
                                              pP[b][:, 512 + q0:1024], start=(kc == 0), stop=last))
                            W[f"mm2_{t}_{c}_{kc}"] = t_p.n
                        return mm1, mm2

                    fst = 5 if c < 4 else 3

                    @blk.tensor
                    def _(te, mk_mms=mk_mms, nkc=nkc, fillers=fillers, fst=fst):
                        mm1, mm2 = mk_mms(te)
                        fi = 0
                        for kc in range(nkc):
                            mm1(kc)
                            if 1 <= kc <= nkc - 2:
                                mm2(kc - 1)
                            while fi < len(fillers) and fi <= 2 * (kc - fst):
                                fillers[fi](te)
                                fi += 1
                        while fi < len(fillers):
                            fillers[fi](te)
                            fi += 1

                    assert v_p.n == v_base, (t, c, v_p.n, v_base)

                    @blk.tensor
                    def _(te, mk_mms=mk_mms, t=t, c=c, nkc=nkc):
                        mm1, mm2 = mk_mms(te)
                        mm2(nkc - 2)
                        mm2(nkc - 1)
                        W[f"mm2last_{t}_{c}"] = t_p.n

                    @blk.vector
                    def _(ve, t=t, c=c, nkc=nkc, c_exp_done=c_exp_done):
                        for kc in range(4 * t, nkc):
                            b = kc % 2
                            q0 = 128 * (kc - 4 * t)
                            ve.wait_ge(c_p.sem, c_exp_done(c, kc))
                            ve.tensor_mul(pP[b][:, q0:q0 + 128], pP[b][:, q0:q0 + 128], mask[:])
                            v_p.inc(ve.tensor_mul(pP[b][:, 512 + q0:512 + q0 + 128],
                                                  pP[b][:, 512 + q0:512 + q0 + 128], mask[:]))
                        lrowA = sev0[0:1, :].bitcast(F32)
                        lrowB = sev0[64:65, :].bitcast(F32)
                        ve.wait_ge(t_p.sem, W[f"mm2last_{t}_{c}"])
                        if c == 0:
                            ve.wait_ge(t_p.sem, W.get(f"pjmm_{t - 1}", 0))
                        ve.wait_ge(d_sl.sem, W.get(f"lst_{t}_{c - 1}", W.get(f"lst_{t - 1}_7", 0)))
                        if c == 0:
                            ve.wait_ge(d_ss.sem, W.get(f"sst_{t - 1}_3", 0))
                        ve.tensor_copy(yT[0:64, c * 512:(c + 1) * 512], pC[0:64, 0:512])
                        ve.tensor_copy(yT[64:128, c * 512:(c + 1) * 512], pC[0:64, 512:1024])
                        ve.tensor_copy(lrowA[0:1, 0:512], pC[64:65, 0:512])
                        v_p.inc(ve.tensor_copy(lrowB[0:1, 0:512], pC[64:65, 512:1024]))
                        W[f"yev_{t}_{c}"] = v_p.n

                    @blk.gpsimd
                    def _(gp, t=t, c=c):
                        lrowA = sev0[0:1, :].bitcast(F32)
                        lrowB = sev0[64:65, :].bitcast(F32)
                        gp.wait_ge(v_p.sem, W[f"yev_{t}_{c}"])
                        d_sl.inc(gp.dma_start(out=Lr[2 * c:2 * c + 1, :], in_=lrowA[0:1, 0:512]), 16)
                        d_sl.inc(gp.dma_start(out=Lr[2 * c + 1:2 * c + 2, :], in_=lrowB[0:1, 0:512]), 16)
                        W[f"lst_{t}_{c}"] = d_sl.n

                    if has_next:
                        if c < 4:
                            emit_trans_ev(t + 1, c)
                            if c == 3:
                                @blk.gpsimd
                                def _(gp, t=t):
                                    gp.wait_ge(v_p.sem, W[f"trev_{4 * (t + 1) + 3}"])
                                    d_sx.inc(gp.dma_start(out=xnT_s[t + 1], in_=xnT[:]), 16)
                                    W[f"xnst_{t + 1}"] = d_sx.n
                        else:
                            emit_qk_ev(t + 1, 8 + 2 * (c - 4))
                            emit_qk_ev(t + 1, 9 + 2 * (c - 4))

                    emit_wconv(18 * t + 3 + 2 * c)
                    lastS[("A", 0)] = lastS[("B", 0)] = c_exp_done(c, nkc - 2)
                    lastS[("A", 1)] = lastS[("B", 1)] = c_exp_done(c, nkc - 1)
                W[f"yev_done_{t}"] = v_p.n

                @blk.scalar
                def _(sc, t=t, nkc=nkc, R=R, c_base_t=c_base_t, K0=K0,
                      c_exp_done=c_exp_done, c_rstd_next=c_rstd_next):
                    for c in range(8):
                        for kc in range(nkc):
                            b = kc % 2
                            q0 = 128 * (kc - 4 * t) if kc >= 4 * t else 0
                            sc.wait_ge(t_p.sem, W[f"mm1_{t}_{c}_{kc}"])
                            if kc >= 2:
                                sc.wait_ge(t_p.sem, W[f"mm2_{t}_{c}_{kc - 2}"])
                            if kc < 4 * t:
                                c_p.inc(sc.activation(pP[b][:], pAB[:, b * 1024:(b + 1) * 1024],
                                                      AF.Exp, bias=cbt[:, c:c + 1], scale=0.125))
                            else:
                                c_p.inc(sc.activation(pP[b][:, q0:512],
                                                      pAB[:, b * 1024 + q0:b * 1024 + 512],
                                                      AF.Exp, bias=cbt[:, c:c + 1], scale=0.125))
                                c_p.inc(sc.activation(pP[b][:, 512 + q0:1024],
                                                      pAB[:, b * 1024 + 512 + q0:b * 1024 + 1024],
                                                      AF.Exp, bias=cbt[:, c:c + 1], scale=0.125))
                            if c < R and kc == K0 - 1:
                                i = c
                                L = 16 * i
                                sc.wait_ge(v_p.sem, W[f"var_{4 * (t + 1) + i}"])
                                c_p.inc(sc.activation(lnsc[:, L + 14:L + 15], lnsc[:, L + 13:L + 14],
                                                      AF.Ln, bias=lnsc[:, 64:65]))
                                sc.wait_ge(c_p.sem, c_p.n)
                                c_p.inc(sc.activation(lnsc[:, L + 15:L + 16], lnsc[:, L + 14:L + 15],
                                                      AF.Exp, scale=-0.5))
                                assert c_p.n == c_rstd_next(i)
                            assert c_p.n == c_exp_done(c, kc) + (
                                2 if (c < R and kc == K0 - 1) else 0)

                W[f"exps_done_{t}"] = c_p.n

            emit_wconv(NWH)
            emit_wconv(NWH + 1)
            emit_recip(NT - 1)
            for yc in range(8):
                emit_ynorm(NT - 1, yc)
            emit_proj(NT - 1)

            AB_T, AB_V, AB_C = t_p.n, v_p.n, c_p.n
            AB_SX, AB_SS, AB_SL = d_sx.n, d_ss.n, d_sl.n
            AB_WCS = [d_wcs[0].n, d_wcs[1].n]

        cs = contextlib.ExitStack()
        with cs:
            def sbc(name, shape, dt):
                return cs.enter_context(nc.sbuf_tensor(name, shape, dt))

            wg = [sbc(f"wg{d}", [128, HID], BF16) for d in range(8)]
            wu = [sbc(f"wu{d}", [128, HID], BF16) for d in range(8)]
            wd = [sbc(f"wd{h}", [128, D], BF16) for h in range(24)]
            xnC = sbc("xnC", [128, T], BF16)
            hT = [sbc(f"hT{h}", [128, 512], BF16) for h in range(24)]
            xc = [sbc(f"xc{i}", [128, D], F32) for i in range(2)]
            scb_ = [sbc(f"scb{i}", [128, D], BF16) for i in range(2)]
            gs = [sbc("gs0", [128, D], F32)] * 2
            sgb = [sbc("sgb0", [128, 512], F32)] * 2

            pG = [cs.enter_context(nc.psum_tensor(f"pG{i}", [128, 512], F32)) for i in range(2)]
            pU = [cs.enter_context(nc.psum_tensor(f"pU{i}", [128, 512], F32)) for i in range(2)]
            pDn = [cs.enter_context(nc.psum_tensor(f"pDn{i}", [128, 1024], F32)) for i in range(2)]

            def cbar(eng):
                eng.wait_ge(t_p.sem, AB_T)
                eng.wait_ge(v_p.sem, AB_V)
                eng.wait_ge(c_p.sem, AB_C)
                eng.wait_ge(d_sx.sem, AB_SX)
                eng.wait_ge(d_ss.sem, AB_SS)
                eng.wait_ge(d_sl.sem, AB_SL)

            @blk.sync
            def _(sy):
                cbar(sy)
                sy.wait_ge(d_wcs[0].sem, AB_WCS[0])
                sy.wait_ge(d_wcs[1].sem, AB_WCS[1])
                for dk in range(8):
                    d_cw.inc(sy.dma_start(
                        out=wg[dk][:].rearrange("p (c x) -> p c x", c=3),
                        in_=wbs_h[3 * dk:3 * dk + 3].rearrange("c p x -> p c x")), 16)
                W["wg_ld"] = d_cw.n
                for dk in range(8):
                    d_cw.inc(sy.dma_start(
                        out=wu[dk][:].rearrange("p (c x) -> p c x", c=3),
                        in_=wbs_h[24 + 3 * dk:24 + 3 * dk + 3].rearrange("c p x -> p c x")), 16)
                W["wu_ld"] = d_cw.n
                for h in range(24):
                    d_cw.inc(sy.dma_start(out=wd[h][:], in_=wbs_h[48 + h]), 16)
                W["wd_ld"] = d_cw.n

            dn_last = {}
            for tc in range(NT):
                c_tc0 = c_p.n
                v_tc0 = v_p.n

                def c_sil(hc, c_tc0=c_tc0):
                    return c_tc0 + hc + 1

                def c_gsil(i, c_tc0=c_tc0):
                    return c_tc0 + 24 + i + 1

                def v_hm(hc, v_tc0=v_tc0):
                    return v_tc0 + hc + 1

                def v_fin(i, v_tc0=v_tc0):
                    return v_tc0 + 24 + i + 1

                dn_prev = dict(dn_last)
                so_tc0 = [d_so[0].n, d_so[1].n]

                @blk.sync
                def _(sy, tc=tc):
                    if tc == 0:
                        cbar(sy)
                    else:
                        sy.wait_ge(t_p.sem, W[f"gumm_{tc - 1}_23"])
                    d_cx.inc(sy.dma_start(out=xnC[:], in_=xnT_s[tc]), 16)
                    W[f"xnld_{tc}"] = d_cx.n

                for i in range(4):
                    ti = 4 * tc + i

                    @blk.sync
                    def _(sy, tc=tc, i=i, ti=ti, v_fin=v_fin):
                        if ti >= 2:
                            if i < 2:
                                sy.wait_ge(v_p.sem, W[f"fin_{ti - 2}"])
                            else:
                                sy.wait_ge(v_p.sem, v_fin(i - 2))
                        d_cs[i % 2].inc(sy.dma_start(out=xc[i % 2][:], in_=x_h[ti * 128:(ti + 1) * 128, :]), 16)
                        d_cs[i % 2].inc(sy.dma_start(out=scb_[i % 2][:], in_=s_s[ti]), 16)
                        W[f"xsld_{ti}"] = d_cs[i % 2].n

                for hc in range(24):
                    @blk.tensor
                    def _(te, tc=tc, hc=hc, c_sil=c_sil, v_hm=v_hm):
                        if tc == 0 and hc == 0:
                            cbar(te)
                            te.wait_ge(d_cw.sem, W["wg_ld"])
                        te.wait_ge(d_cx.sem, W[f"xnld_{tc}"])
                        if hc >= 2:
                            te.wait_ge(c_p.sem, c_sil(hc - 2))
                            te.wait_ge(v_p.sem, v_hm(hc - 2))
                        elif tc >= 1:
                            te.wait_ge(c_p.sem, W[f"sil_{tc - 1}_{22 + (hc % 2)}"])
                            te.wait_ge(v_p.sem, W[f"hm_{tc - 1}_{22 + (hc % 2)}"])
                        for dk in range(8):
                            te.matmul(pG[hc % 2][:], wg[dk][:, hc * 128:(hc + 1) * 128],
                                      xnC[:, dk * 512:(dk + 1) * 512],
                                      start=(dk == 0), stop=(dk == 7))
                        if tc == 0 and hc == 0:
                            te.wait_ge(d_cw.sem, W["wu_ld"])
                        for dk in range(8):
                            ins = te.matmul(pU[hc % 2][:], wu[dk][:, hc * 128:(hc + 1) * 128],
                                            xnC[:, dk * 512:(dk + 1) * 512],
                                            start=(dk == 0), stop=(dk == 7))
                        t_p.inc(ins)
                        W[f"gumm_{tc}_{hc}"] = t_p.n

                for i in range(4):
                    ti = 4 * tc + i

                    @blk.tensor
                    def _(te, tc=tc, i=i, ti=ti, v_hm=v_hm, c_gsil=c_gsil):
                        if tc == 0 and i == 0:
                            te.wait_ge(d_cw.sem, W["wd_ld"])
                        if i >= 2:
                            te.wait_ge(c_p.sem, c_gsil(i - 2))
                        elif tc >= 1:
                            te.wait_ge(c_p.sem, W[f"gsil_{4 * tc + i - 2}"])
                        for hc in range(24):
                            te.wait_ge(v_p.sem, v_hm(hc))
                            te.matmul(pDn[i % 2][:, 0:512], hT[hc][:, i * 128:(i + 1) * 128],
                                      wd[hc][:, 0:512], start=(hc == 0), stop=(hc == 23))
                            t_p.inc(te.matmul(pDn[i % 2][:, 512:1024], hT[hc][:, i * 128:(i + 1) * 128],
                                              wd[hc][:, 512:1024], start=(hc == 0), stop=(hc == 23)))
                            if i == 3:
                                dn_last[hc] = t_p.n
                        W[f"dnmm_{ti}"] = t_p.n

                @blk.scalar
                def _(sc, tc=tc, c_tc0=c_tc0, v_hm=v_hm, v_tc0=v_tc0, so_tc0=so_tc0):
                    if tc == 0:
                        cbar(sc)
                    for hc in range(24):
                        sc.wait_ge(t_p.sem, W[f"gumm_{tc}_{hc}"])
                        if hc >= 1:
                            sc.wait_ge(v_p.sem, v_hm(hc - 1))
                        elif tc >= 1:
                            sc.wait_ge(v_p.sem, W[f"hm_{tc - 1}_23"])
                        c_p.inc(sc.activation(sgb[hc % 2][:], pG[hc % 2][:], AF.Silu, scale=INV127))
                        W[f"sil_{tc}_{hc}"] = c_p.n
                    for i in range(4):
                        ti = 4 * tc + i
                        sc.wait_ge(t_p.sem, W[f"dnmm_{ti}"])
                        if i >= 1:
                            sc.wait_ge(v_p.sem, v_tc0 + 24 + (i - 1) + 1)
                            sc.wait_ge(d_so[(i - 1) % 2].sem, so_tc0[(i - 1) % 2] + 16 * ((i - 1) // 2 + 1))
                        elif ti >= 1:
                            sc.wait_ge(v_p.sem, W[f"fin_{ti - 1}"])
                            sc.wait_ge(d_so[(i - 1) % 2].sem, W.get(f"outst_{ti - 1}", 0))
                        c_p.inc(sc.activation(gs[i % 2][:], pDn[i % 2][:], AF.Silu,
                                              scale=INV127 * INV127))
                        W[f"gsil_{ti}"] = c_p.n

                @blk.vector
                def _(ve, tc=tc, c_sil=c_sil, c_gsil=c_gsil, dn_prev=dn_prev, so_tc0=so_tc0):
                    if tc == 0:
                        cbar(ve)
                    for hc in range(24):
                        ve.wait_ge(c_p.sem, c_sil(hc))
                        ve.wait_ge(t_p.sem, dn_prev.get(hc, 0))
                        v_p.inc(ve.tensor_mul(hT[hc][:], sgb[hc % 2][:], pU[hc % 2][:]))
                        W[f"hm_{tc}_{hc}"] = v_p.n
                    for i in range(4):
                        ti = 4 * tc + i
                        ve.wait_ge(c_p.sem, c_gsil(i))
                        ve.wait_ge(d_cs[i % 2].sem, W[f"xsld_{ti}"])
                        if i >= 2:
                            ve.wait_ge(d_so[i % 2].sem, so_tc0[i % 2] + 16)
                        else:
                            ve.wait_ge(d_so[i % 2].sem, W.get(f"outst_{ti - 2}", 0))
                        ve.tensor_mul(gs[i % 2][:], gs[i % 2][:], scb_[i % 2][:])
                        v_p.inc(ve.tensor_add(gs[i % 2][:], gs[i % 2][:], xc[i % 2][:]))
                        W[f"fin_{ti}"] = v_p.n

                for i in range(4):
                    ti = 4 * tc + i

                    @blk.gpsimd
                    def _(gp, ti=ti, i=i):
                        if ti == 0:
                            cbar(gp)
                        gp.wait_ge(v_p.sem, W[f"fin_{ti}"])
                        d_so[i % 2].inc(gp.dma_start(out=out_h[ti * 128:(ti + 1) * 128, :], in_=gs[i % 2][:]), 16)
                        W[f"outst_{ti}"] = d_so[i % 2].n

            @blk.gpsimd
            def _(gp):
                gp.wait_ge(d_so[0].sem, d_so[0].n)
                gp.wait_ge(d_so[1].sem, d_so[1].n)

    return nc


def _host_consts():
    hd = 64
    theta = 1.0 / (10000.0 ** (np.arange(0, hd, 2, dtype=np.float64) / hd))
    h_idx = np.arange(NH, dtype=np.float64)
    ang = h_idx[:, None] * theta[None, :]
    cos_t = np.cos(ang)
    sin_t = np.sin(ang)
    row = cos_t.reshape(-1)
    cosrow = np.concatenate([row, row]).reshape(1, D).astype(ml_dtypes.bfloat16)
    row = sin_t.reshape(-1)
    sinrow = np.concatenate([row, row]).reshape(1, D).astype(ml_dtypes.bfloat16)
    ident = np.zeros((40, 8), dtype=np.float32)
    ident[0:8, 0:8] = np.eye(8)
    ident[32:40, 0:8] = np.eye(8)
    identb = np.eye(128, dtype=ml_dtypes.bfloat16)
    onesb = np.ones((1, 128), dtype=ml_dtypes.bfloat16)
    onescol = np.ones((128, 1), dtype=ml_dtypes.bfloat16)
    kk = np.arange(128)[:, None]
    qq = np.arange(128)[None, :]
    masktri = (qq >= kk).astype(ml_dtypes.bfloat16)
    sel16 = np.zeros((16, NH * 64), dtype=ml_dtypes.bfloat16)
    for c in range(8):
        for m in range(128):
            sel16[2 * c + (m >= 64), c * 128 + m] = 1
    return cosrow, sinrow, ident, identb, onesb, onescol, masktri, sel16


def kernel(L0_attn_x, L0_attn_W_attn, L0_attn_b_attn, L0_attn_W_proj,
           L0_attn_b_proj, L0_ffn_w_gate, L0_ffn_w_up, L0_ffn_w_down):
    x = np.ascontiguousarray(np.asarray(L0_attn_x, dtype=np.float32))
    wat = np.ascontiguousarray(np.asarray(L0_attn_W_attn, dtype=np.float32))
    ba = np.ascontiguousarray(np.asarray(L0_attn_b_attn, dtype=np.float32)).reshape(1, 3 * D)
    wp = np.ascontiguousarray(np.asarray(L0_attn_W_proj, dtype=np.float32))
    bp = np.ascontiguousarray(np.asarray(L0_attn_b_proj, dtype=np.float32)).reshape(1, D)
    wgT = np.ascontiguousarray(np.asarray(L0_ffn_w_gate, dtype=np.int32).T)
    wuT = np.ascontiguousarray(np.asarray(L0_ffn_w_up, dtype=np.int32).T)
    wdT = np.ascontiguousarray(np.asarray(L0_ffn_w_down, dtype=np.int32).T)
    cosrow, sinrow, ident, identb, onesb, onescol, masktri, sel16 = _host_consts()

    nc = build_program()
    shared = dict(w_attn=wat, b_attn=ba, w_proj=wp, b_proj=bp,
                  wg_t=wgT, wu_t=wuT, wd_t=wdT,
                  cosrow=cosrow, sinrow=sinrow, ident=ident, identb=identb,
                  onesb=onesb, onescol=onescol, masktri=masktri, sel16=sel16)
    in_maps = [dict(x=np.ascontiguousarray(x[b]), **shared) for b in range(8)]
    res = run_bass_kernel_spmd(nc, in_maps, core_ids=list(range(8)))
    out = np.stack([res.results[b]["out"] for b in range(8)], axis=0)
    return out.astype(np.float32)
